# version 03
# baseline (speedup 1.0000x reference)
import contextlib
import numpy as np
import ml_dtypes
import concourse.bass as bass
import concourse.mybir as mybir
from concourse.bass_utils import run_bass_kernel_spmd

F32 = mybir.dt.float32
BF16 = mybir.dt.bfloat16
AF = mybir.ActivationFunctionType
ALU = mybir.AluOpType
AX = mybir.AxisListType
BFD = ml_dtypes.bfloat16

D = 1024
NT_OUT = 26
NT_EXT = 27
HALO = 1280
T_SH = 2048
NLAT = 64
GROUPS = [(0, 4), (4, 4), (8, 4), (12, 4), (16, 4), (20, 4), (24, 2)]
L1F_KV_LO = 0
L1F_QA_LO = 4
L1F_KVB_LO = 4
L1F_QB_LO = 9
L1A_KV_LO = 9
L1A_Q_LO = 10

NEG = np.float32(-1e30)


class Em:
    ENGS = ("sync", "tensor", "vector", "scalar", "gpsimd")

    def __init__(self, nc):
        self.nc = nc
        self.streams = {e: [] for e in self.ENGS}
        self.sems = {}
        self.led = {}

    def sem(self, name):
        h = self.nc.alloc_semaphore(name)
        self.sems[name] = h
        self.led[name] = 0
        return h

    def step(self, eng, fn, waits=(), inc=None, inc_by=1):
        for s, v in waits:
            assert v <= self.led[s], f"wait {s}>={v} but only {self.led[s]} emitted"
        waits = tuple(waits)
        sems = self.sems
        if inc is not None:
            self.led[inc] += inc_by
        def closure(e, fn=fn, waits=waits, inc=inc, inc_by=inc_by):
            for s, v in waits:
                if v > 0:
                    e.wait_ge(sems[s], v)
            last = fn(e)
            if inc is not None:
                if last is None:
                    e.nop().then_inc(sems[inc], inc_by)
                else:
                    last.then_inc(sems[inc], inc_by)
        self.streams[eng].append(closure)

    ENG_SEM = {"sync": "ld", "tensor": "tc", "vector": "vc",
               "scalar": "sc", "gpsimd": "wd"}

    def barrier(self):
        snap = dict(self.led)
        for eng in self.ENGS:
            own = self.ENG_SEM[eng]
            waits = [(s, v) for s, v in snap.items() if s != own and v > 0]
            self.step(eng, lambda e: None, waits=waits, inc=own)

    def dma(self, eng, out, in_, waits=(), inc=None, slow=False):
        self.step(eng, lambda e: e.dma_start(out=out, in_=in_,
                                             allow_slow_non_contiguous=slow),
                  waits=waits, inc=inc, inc_by=16)

    def finish(self, block):
        for name in self.ENGS:
            stream = self.streams[name]
            def runner(eng, stream=stream):
                for c in stream:
                    c(eng)
            getattr(block, name)(runner)


def build_masks_for_core(s):
    t0 = s * T_SH
    a0 = t0 - HALO
    out = {}
    ml = np.zeros((NT_OUT, 128, 256), np.float32)
    zl = np.ones((NT_OUT, 128, 1), np.float32)
    mlat = np.zeros((NT_OUT, 128, NLAT), np.float32)
    zlat = np.ones((NT_OUT, 128, 1), np.float32)
    pad = np.ones((NT_EXT, 128, 1), np.float32)
    for T in range(NT_OUT):
        abs0 = a0 + 128 * T
        i = abs0 + np.arange(128)[:, None]
        j = abs0 - 128 + np.arange(256)[None, :]
        ok = (j >= i - 128) & (j <= i - 1) & (j >= 0)
        ml[T] = np.where(ok, 0.0, NEG)
        zl[T, :, 0] = ok.any(-1).astype(np.float32)
        cid = abs0 // 128
        lok = np.arange(NLAT)[None, :] < cid
        mlat[T] = np.where(lok, 0.0, NEG)
        zlat[T, :, 0] = 1.0 if cid > 0 else 0.0
    for T in range(NT_EXT):
        abs0 = a0 - 128 + 128 * T
        pad[T, :, 0] = ((abs0 + np.arange(128)) >= 0).astype(np.float32)
    out["mask_local"] = ml
    out["zl_local"] = zl
    out["mask_lat"] = mlat
    out["zl_lat"] = zlat
    out["padmask"] = pad
    i = np.arange(64)[:, None]
    c = np.arange(128)[None, :]
    out["mask_l1a"] = np.where((c < 64) | (c - 64 <= i), 0.0, NEG).astype(np.float32)
    def build_mf(qlo, kclamp):
        mf = np.zeros((NT_OUT - qlo, 128, 1024), np.float32)
        for tq in range(qlo, NT_OUT):
            abs0 = a0 + 128 * tq
            kst = max(tq - ((abs0 - a0 + 256) % 512) // 128 - 4, kclamp)
            klo = a0 + 128 * kst
            i = abs0 + np.arange(128)[:, None]
            j = klo + np.arange(1024)[None, :]
            ok = (i >= j) & (i <= j + 512) & (j >= 0)
            mf[tq - qlo] = np.where(ok, 0.0, NEG)
        return mf
    out["mask_l1fA"] = build_mf(L1F_QA_LO, L1F_KV_LO)
    out["mask_l1fB"] = build_mf(L1F_QB_LO, L1F_KVB_LO)
    DIM_HEAD, WINDOW, N = 64, 512, 8192
    half = np.arange(0, DIM_HEAD, 2, dtype=np.float64)
    inv = 1.0 / 10000.0 ** (half / DIM_HEAD)
    base = (half + 0.4 * DIM_HEAD) / (1.4 * DIM_HEAD)
    cosT = np.zeros((26, 128, 64), np.float32); sinT = np.zeros_like(cosT)
    cosIT = np.zeros_like(cosT); sinIT = np.zeros_like(cosT)
    for T in range(0, NT_OUT):
        abs0 = a0 + 128 * T
        t = np.clip(abs0 + np.arange(128), 0, None).astype(np.float64)
        fr = t[:, None] * inv[None, :]
        fr2 = np.concatenate([fr, fr], -1)
        power = (t - N // 2) / float(WINDOW // 2)
        sc = base[None, :] ** power[:, None]
        sc2 = np.concatenate([sc, sc], -1)
        cs, sn = np.cos(fr2), np.sin(fr2)
        sgn = np.concatenate([-np.ones(32), np.ones(32)])[None, :]
        cosT[T] = (cs * sc2).astype(np.float32)
        sinT[T] = (sn * sc2 * sgn).astype(np.float32)
        cosIT[T] = (cs / sc2).astype(np.float32)
        sinIT[T] = (sn / sc2 * sgn).astype(np.float32)
    out["rope_cos"] = cosT; out["rope_sin"] = sinT
    out["rope_cosi"] = cosIT; out["rope_sini"] = sinIT
    out["ident32"] = np.eye(128, dtype=np.float32)
    out["identbf"] = np.eye(128, dtype=np.float32)
    out["iota_row"] = np.broadcast_to(np.arange(128, dtype=np.float32), (128, 128)).copy()
    out["pidx"] = np.arange(128, dtype=np.float32).reshape(128, 1)
    out["ones_bf_row"] = np.ones((1, 128), np.float32)
    oh = np.zeros((8, 8, 128), np.float32)
    for e in range(8):
        oh[e, e, :] = 1.0
    out["onehot8"] = oh
    return out


def shard_inputs(inputs, core):
    b, s = divmod(core, 4)
    t0 = s * T_SH
    x = np.asarray(inputs["L0_attn_x"], np.float32)
    ext = np.zeros((NT_EXT * 128, D), np.float32)
    lo = t0 - HALO - 128
    src_lo = max(lo, 0)
    ext[src_lo - lo:src_lo - lo + (t0 + T_SH - src_lo)] = x[b, src_lo:t0 + T_SH]
    im = {"x_ext": ext, "x_full": np.ascontiguousarray(x[b])}
    im.update({k: np.ascontiguousarray(np.asarray(v)) for k, v in inputs.items()
               if k != "L0_attn_x"})
    im["WrT"] = np.ascontiguousarray(np.asarray(inputs["L0_ffn_Wr"]).T)
    W1 = np.asarray(inputs["L0_ffn_W1"], np.float32)
    im["W1_staged"] = np.ascontiguousarray(
        W1.reshape(8, 8, 128, 32, 128).transpose(3, 2, 0, 1, 4))
    Wo = np.asarray(inputs["L0_ffn_Wo"], np.float32)
    im["Wo_staged"] = np.ascontiguousarray(
        Wo.reshape(32, 128, 8, 128))
    im["g0_gamma_bc"] = np.ascontiguousarray(np.broadcast_to(
        np.asarray(inputs["L0_ffn_gamma"], np.float32), (128, D)))
    im["g0_beta_bc"] = np.ascontiguousarray(np.broadcast_to(
        np.asarray(inputs["L0_ffn_beta"], np.float32), (128, D)))
    def bc(v):
        return np.ascontiguousarray(np.broadcast_to(np.asarray(v, np.float32), (128, D)))
    im["l1a_g1bc"] = bc(inputs["L1_attn_ln1_g"]); im["l1a_b1bc"] = bc(inputs["L1_attn_ln1_b"])
    im["l1a_g2bc"] = bc(inputs["L1_attn_ln2_g"]); im["l1a_b2bc"] = bc(inputs["L1_attn_ln2_b"])
    lng = np.asarray(inputs["L1_ffn_ln_g"], np.float32)
    lnb = np.asarray(inputs["L1_ffn_ln_b"], np.float32)
    im["l1f_gbc"] = np.ascontiguousarray(
        np.broadcast_to(lng[:, None, :], (2, 128, D)))
    im["l1f_bbc"] = np.ascontiguousarray(
        np.broadcast_to(lnb[:, None, :], (2, 128, D)))
    w2 = np.asarray(inputs["L1_ffn_w2"], np.float32)
    w2p = np.zeros((2, 2816, D), np.float32)
    w2p[:, :2730] = w2
    im["L1_ffn_w2_pad"] = w2p
    im.update(build_masks_for_core(s))
    return im


DEBUG_OUT = set()
DEV_STAGE = 99
DEV_GROUPS = 99
DEV_PHASES = 99
DEV_ATTN = 99
DEV_SMM_HH = (0, 1, 2, 3)

def build_program():
    nc = bass.Bass("TRN2", target_bir_lowering=False)
    g = {}

    def din(name, shape, dtype=F32):
        g[name] = nc.dram_tensor(name, list(shape), dtype, kind="ExternalInput").ap()
        return g[name]

    def dint(name, shape, dtype=F32):
        kind = "ExternalOutput" if name in DEBUG_OUT else "Internal"
        g[name] = nc.dram_tensor(name, list(shape), dtype, kind=kind).ap()
        return g[name]

    din("x_ext", (NT_EXT * 128, D)); din("x_full", (8192, D))
    din("L0_attn_qkv_w", (D, 3 * D)); din("L0_attn_qkv_b", (3 * D,))
    din("L0_attn_rq_w", (D, 256)); din("L0_attn_rq_b", (256,))
    din("L0_attn_rkv_w", (D, 512)); din("L0_attn_rkv_b", (512,))
    din("L0_attn_rout_w", (256, D)); din("L0_attn_rout_b", (D,))
    din("L0_attn_out_w", (D, D)); din("L0_attn_out_b", (D,))
    din("L0_attn_gl_w", (64, 1)); din("L0_attn_gr_w", (64, 1))
    din("L0_ffn_Wr", (8, D)); din("WrT", (D, 8))
    din("L0_ffn_W1", (8, D, 4096)); din("L0_ffn_b1", (8, 4096))
    din("L0_ffn_Wo", (4096, D)); din("L0_ffn_bo", (D,))
    din("W1_staged", (32, 128, 8, 8, 128)); din("Wo_staged", (32, 128, 8, 128))
    din("L0_ffn_gamma", (D,)); din("L0_ffn_beta", (D,))
    din("L1_attn_ln1_g", (D,)); din("L1_attn_ln1_b", (D,))
    din("L1_attn_w_qkv", (D, 3 * D)); din("L1_attn_b_qkv", (3 * D,))
    din("L1_attn_w_out", (D, D)); din("L1_attn_b_out", (D,))
    din("L1_attn_ln2_g", (D,)); din("L1_attn_ln2_b", (D,))
    din("L1_attn_w1", (D, 4096)); din("L1_attn_b1", (4096,))
    din("L1_attn_w2", (4096, D)); din("L1_attn_b2", (D,))
    din("L1_ffn_wqkv", (2, D, 1536)); din("L1_ffn_q_scale", (2, 64))
    din("L1_ffn_k_scale", (2, 64)); din("L1_ffn_wo", (2, 512, D))
    din("L1_ffn_ln_g", (2, D)); din("L1_ffn_ln_b", (2, D))
    din("L1_ffn_w1", (2, D, 5460)); din("L1_ffn_b1", (2, 5460))
    din("L1_ffn_w2", (2, 2730, D)); din("L1_ffn_b2", (2, D))
    din("mask_local", (NT_OUT, 128, 256)); din("zl_local", (NT_OUT, 128, 1))
    din("mask_lat", (NT_OUT, 128, NLAT)); din("zl_lat", (NT_OUT, 128, 1))
    din("padmask", (NT_EXT, 128, 1))
    din("mask_l1a", (64, 128))
    din("mask_l1fA", (NT_OUT - L1F_QA_LO, 128, 1024))
    din("mask_l1fB", (NT_OUT - L1F_QB_LO, 128, 1024))
    din("rope_cos", (26, 128, 64)); din("rope_sin", (26, 128, 64))
    din("rope_cosi", (26, 128, 64)); din("rope_sini", (26, 128, 64))
    din("ident32", (128, 128)); din("identbf", (128, 128))
    din("g0_gamma_bc", (128, D)); din("g0_beta_bc", (128, D))
    din("l1a_g1bc", (128, D)); din("l1a_b1bc", (128, D))
    din("l1a_g2bc", (128, D)); din("l1a_b2bc", (128, D))
    din("l1f_gbc", (2, 128, D)); din("l1f_bbc", (2, 128, D))
    din("L1_ffn_w2_pad", (2, 2816, D))
    g["xn1_hbm"] = nc.dram_tensor("xn1_hbm", [NT_OUT, 128, D], F32, kind="Internal").ap()
    din("iota_row", (128, 128)); din("pidx", (128, 1)); din("ones_bf_row", (1, 128))
    din("onehot8", (8, 8, 128))
    dint("g0_hbm", (NT_OUT, 128, D), BF16)
    dint("dbg_logits", (NT_OUT, 128, 8), F32)
    dint("dbg_masks", (NT_OUT, 128, 8), F32)
    dint("dbg_y", (NT_OUT, 128, D), F32)
    dint("dbg_lmax", (NT_OUT, 128, 1), F32)
    dint("dbg_rhs", (128, 8, 8, 512), BF16)
    dint("dbg_inter", (128, 32, 512), BF16)
    dint("dbg_mbc", (128, 512), BF16)
    dint("dbg_local", (64, 16, 128), BF16)
    dint("dbg_remote", (64, 16, 128), BF16)
    dint("dbg_mixed", (64, 16, 128), BF16)
    dint("dbg_gates", (1, 2048), BF16)
    dint("dbg_q", (128, 8, 512), BF16)
    dint("dbg_k", (128, 8, 640), BF16)
    dint("dbg_v", (128, 5, 1024), BF16)
    dint("dbg_pt", (128, 2, 128), BF16)
    dint("dbg_m1", (128, 1024), F32)
    dint("dbg_outw", (64, 16, 1024), BF16)
    dint("dbg_outb", (1, 1024), BF16)
    dint("dbg_g0t", (128, 1024), BF16)
    dint("dbg_s0", (128, 1024), F32)
    dint("dbg_mrow", (1, 4096), BF16)
    dint("xn0T_hbm", (NT_OUT, 128, 8 * 128), BF16)
    dint("x1_hbm", (NT_OUT, 128, D), F32)
    dint("W1bf", (32, 128, 8, 8, 128), BF16)
    dint("Wobf", (32, 128, 8, 128), BF16)
    dint("x2a_hbm", (NT_OUT, 128, D), F32)
    dint("yA_hbm", (NT_OUT, 128, D), F32)
    dint("x2fA_hbm", (NT_OUT, 128, D), F32)
    dint("x2fB_hbm", (NT_OUT, 128, D), F32)
    dint("g1_hbm", (NT_OUT, 128, D), BF16)
    out = nc.dram_tensor("out_shard", [T_SH, D], F32, kind="ExternalOutput").ap()
    g["out_shard"] = out

    with contextlib.ExitStack() as es:
        em = Em(nc)
        for s in ("ld", "wd", "tc", "vc", "sc", "st"):
            em.sem(s)
        _build_body(nc, em, g, es)
        block = es.enter_context(nc.Block())
        em.finish(block)
    return nc


def _build_body(nc, em, g, es):
    A = {}

    def sb(name, shape, dtype):
        A[name] = nc.alloc_sbuf_tensor("sb_" + name, list(shape), dtype)
        return A[name]

    sb("ident32", (128, 128), F32); sb("identbf", (128, 128), BF16)
    sb("eps_1em5", (128, 1), F32); sb("eps_1em12", (128, 1), F32)
    sb("iota_row", (128, 128), F32); sb("pidx", (128, 1), F32)
    sb("ones_bf", (1, 128), BF16); sb("ones_f32", (1, 128), F32)
    PS = []
    for i in range(6):
        PS.append(nc.alloc_psum_tensor(f"ps{i}", [128, 512], F32))
    PS.append(nc.alloc_psum_tensor("psb0", [128, 1024], BF16))
    PS.append(nc.alloc_psum_tensor("psb1", [128, 1024], BF16))

    ld, wd, tc, vc, sc, st = "ld", "wd", "tc", "vc", "sc", "st"

    def mset(e):
        e.memset(A["eps_1em5"][:], 1e-5)
        return e.memset(A["eps_1em12"][:], 1e-12)
    em.step("gpsimd", mset, inc="wd")
    em.dma("sync", A["ident32"][:], g["ident32"], inc=ld)
    em.dma("gpsimd", A["identbf"][:], g["identbf"], inc=wd)
    em.dma("sync", A["iota_row"][:], g["iota_row"], inc=ld)
    em.dma("sync", A["pidx"][:], g["pidx"], inc=ld)
    em.dma("gpsimd", A["ones_bf"][:], g["ones_bf_row"], inc=wd)
    em.dma("sync", A["ones_f32"][:], g["ones_bf_row"], inc=ld)
    base_ld = em.led[ld]; base_wd = em.led[wd]

    st_ffn = _ph_ffn(nc, em, g, A, PS, base_ld, base_wd)
    _ph_attn(nc, em, g, A, PS, st_ffn)
    _ph_xn1(nc, em, g, A, PS)
    if DEV_PHASES >= 4:
        _ph_l1f_layer(nc, em, g, A, PS, 0, g["xn1_hbm"], L1F_KV_LO, L1F_QA_LO,
                      g["yA_hbm"], False, "mask_l1fA")
    if DEV_PHASES >= 5:
        _ph_l1f_layer(nc, em, g, A, PS, 1, g["yA_hbm"], L1F_KVB_LO, L1F_QB_LO,
                      None, True, "mask_l1fB")
    if DEV_PHASES >= 6:
        _ph_l1a(nc, em, g, A, PS)
    _final_store_wait(nc, em, g)


def _final_store_wait(nc, em, g):
    em.step("sync", lambda e: None, waits=[("st", em.led["st"])])


class PhaseAlloc:
    _uid = [0]

    def __init__(self, nc, A):
        import contextlib
        self.nc = nc
        self.A = A
        self.es = contextlib.ExitStack()
        PhaseAlloc._uid[0] += 1
        self.tag = f"p{PhaseAlloc._uid[0]}_"

    def sbuf(self, name, shape, dtype):
        h = self.es.enter_context(self.nc.sbuf_tensor(self.tag + name, list(shape), dtype))
        self.A[name] = h
        return h

    def close(self):
        self.es.close()


def alloc_stats(pa, pref):
    for suf in ("_sum", "_ss", "_mneg", "_m2", "_v", "_sd", "_r", "_negmr"):
        pa.sbuf(pref + suf, (128, 1), F32)


def emit_stats(em, A, x_ap, pref, eps, scratch, extra_waits=()):
    assert eps in ("eps_1em5", "eps_1em12")
    """Per-token LN stats of x_ap [128, D] fp32.  Produces A[pref+'_r'] and
    A[pref+'_negmr'] ([128,1] fp32).  Engine-alternating chain (no same-engine
    dependent small ops).  Ends with a vector step; the consumer should wait
    ('vc', em.led['vc']) then scalar ops may use r/negmr as AP scale/bias."""
    vcs, scs = "vc", "sc"
    D_ = D
    em.step("vector", lambda e: e.reduce_sum(A[pref + "_sum"][:], x_ap, axis=AX.X),
            waits=list(extra_waits), inc=vcs)
    em.step("scalar", lambda e: e.activation(A[scratch][:], x_ap, AF.Square,
                                             accum_out=A[pref + "_ss"][:]),
            waits=list(extra_waits), inc=scs)
    def s_m(e):
        e.activation(A[pref + "_mneg"][:], A[pref + "_sum"][:], AF.Identity,
                     scale=-1.0 / D_)
        return e.activation(A[pref + "_m2"][:], A[pref + "_sum"][:], AF.Square,
                            scale=1.0 / D_)
    em.step("scalar", s_m, waits=[(vcs, em.led[vcs])], inc=scs)
    em.step("vector", lambda e: e.tensor_scalar(
        A[pref + "_v"][:], A[pref + "_ss"][:], 1.0 / D_, A[pref + "_m2"][:, 0:1],
        op0=ALU.mult, op1=ALU.subtract),
        waits=[(scs, em.led[scs])], inc=vcs)
    em.step("scalar", lambda e: e.activation(A[pref + "_sd"][:], A[pref + "_v"][:],
                                             AF.Sqrt, bias=A[eps][:, 0:1]),
            waits=[(vcs, em.led[vcs]), ("wd", 1)], inc=scs)
    em.step("vector", lambda e: e.reciprocal(A[pref + "_r"][:], A[pref + "_sd"][:]),
            waits=[(scs, em.led[scs])], inc=vcs)
    em.step("scalar", lambda e: e.activation(A[pref + "_negmr"][:],
                                             A[pref + "_mneg"][:], AF.Identity,
                                             scale=A[pref + "_r"][:, 0:1]),
            waits=[(vcs, em.led[vcs])], inc=scs)
    em.step("vector", lambda e: None, waits=[(scs, em.led[scs])], inc=vcs)


def _ph_ffn(nc, em, g, A, PS, base_ld, base_wd):
    A = dict(A)
    """Computes g0 (silu of MoE block) for all ext-out tiles; writes g0_hbm,
    xn0T_hbm, and bf16 W1/Wo copies to HBM on the first pass."""
    pa = PhaseAlloc(nc, A)
    sbuf = pa.sbuf
    ld, wd, tcs, vcs, scs, sts = "ld", "wd", "tc", "vc", "sc", "st"
    if DEV_PHASES < 1:
        pa.close()
        return em.led["st"]
    sbuf("x_g", (128, 2, D), F32)
    sbuf("sq_scratch", (128, D), F32)
    alloc_stats(pa, "s0")
    alloc_stats(pa, "sy")
    sbuf("xn32_t", (128, D), F32)
    sbuf("hi_t", (128, D), BF16)
    sbuf("lo_t", (128, D), BF16)
    sbuf("xnT_g", (128, 8, 512), BF16)
    sbuf("loT_t", (128, 8, 128), BF16)
    sbuf("wr_hi", (128, 8, 8), BF16)
    sbuf("wr_lo", (128, 8, 8), BF16)
    sbuf("wr_tmp", (128, 8, 8), F32)
    sbuf("logits", (128, 8), F32)
    sbuf("lmax", (128, 1), F32)
    sbuf("masks", (128, 8), F32)
    sbuf("masks_bf", (128, 128), BF16)
    sbuf("masksT_g", (8, 512), BF16)
    sbuf("onehot8", (8, 8, 128), BF16)
    sbuf("mask_bc", (128, 512), BF16)
    sbuf("rhs_e", (128, 8, 8, 512), BF16)
    sbuf("w1slab", (128, 2, 8, 8, 128), BF16)
    sbuf("woslab", (128, 2, 8, 128), BF16)
    sbuf("inter_g", (128, 32, 512), BF16)
    sbuf("b1sb", (8, 4096), BF16)
    sbuf("xn_tm", (128, D), F32)
    sbuf("y_t", (128, D), F32)
    sbuf("gam_bc", (128, D), F32)
    sbuf("bet_bc", (128, D), F32)
    sbuf("g0_t", (128, D), BF16)
    sbuf("rowbuf2_bf", (1, 1024), BF16)

    em.dma("sync", A["gam_bc"][:], g["g0_gamma_bc"], inc=ld)
    em.dma("sync", A["bet_bc"][:], g["g0_beta_bc"], inc=ld)
    em.dma("gpsimd", A["b1sb"][:], g["L0_ffn_b1"], inc=wd)
    em.dma("gpsimd", A["onehot8"][:], g["onehot8"], inc=wd)
    em.step("gpsimd", lambda e: e.memset(A["masks_bf"][:], 0.0), inc=wd)
    em.dma("sync", A["wr_tmp"][:], g["WrT"].rearrange("(c p) e -> p c e", p=128), inc=ld)
    em.step("vector", lambda e: e.tensor_copy(A["wr_hi"][:], A["wr_tmp"][:]),
            waits=[(ld, em.led[ld])], inc=vcs)
    em.step("scalar", lambda e: e.activation(A["sq_scratch"][:, 0:64],
                                             A["wr_hi"][:].rearrange("p c e -> p (c e)"),
                                             AF.Identity),
            waits=[(vcs, em.led[vcs])], inc=scs)
    em.step("vector", lambda e: e.tensor_tensor(
        A["wr_lo"][:].rearrange("p c e -> p (c e)"),
        A["wr_tmp"][:].rearrange("p c e -> p (c e)"),
        A["sq_scratch"][:, 0:64], op=ALU.subtract),
        waits=[(scs, em.led[scs])], inc=vcs)
    em.dma("gpsimd", A["rowbuf2_bf"][0:1, 0:1024],
           g["L0_ffn_bo"].rearrange("(o d) -> o d", o=1), inc=wd)

    for gi, (g0t, gn) in enumerate(GROUPS):
        if gi >= DEV_GROUPS:
            break
        first_pass = (gi == 0)
        ntok = gn * 128
        for ti in range(gn):
            T = g0t + ti
            xt = A["x_g"][:, ti % 2, :]
            em.dma("sync", xt, g["x_ext"][(T + 1) * 128:(T + 2) * 128, :],
                   waits=[(vcs, em.led[vcs]), (scs, em.led[scs])], inc=ld)
            emit_stats(em, A, xt, "s0", "eps_1em5", "sq_scratch",
                       extra_waits=[(ld, em.led[ld])])
            def norm(e, xt=xt):
                e.activation(A["xn32_t"][:], xt, AF.Identity,
                             bias=A["s0_negmr"][:, 0:1], scale=A["s0_r"][:, 0:1])
                return e.activation(A["hi_t"][:], xt, AF.Identity,
                             bias=A["s0_negmr"][:, 0:1], scale=A["s0_r"][:, 0:1])
            em.step("scalar", norm, waits=[(vcs, em.led[vcs])], inc=scs)
            em.step("vector", lambda e: e.tensor_tensor(
                A["lo_t"][:], A["xn32_t"][:], A["hi_t"][:], op=ALU.subtract),
                waits=[(scs, em.led[scs])], inc=vcs)
            vc_norm = em.led[vcs]
            for cc in range(8):
                bank = PS[4 + (cc % 2)]
                em.step("tensor", lambda e, cc=cc, bank=bank: e.matmul(
                    bank[0:128, 0:128], A["hi_t"][:, cc * 128:(cc + 1) * 128],
                    A["identbf"][:], start=True, stop=True),
                    waits=[(vcs, vc_norm), (wd, base_wd),
                           (scs, em.led[scs]), (vcs, em.led[vcs])], inc=tcs)
                em.step("vector", lambda e, cc=cc, bank=bank, ti=ti: e.tensor_copy(
                    A["xnT_g"][:, cc, ti * 128:(ti + 1) * 128], bank[0:128, 0:128]),
                    waits=[(tcs, em.led[tcs]), (sts, em.led[sts])], inc=vcs)
            for cc in range(8):
                bank = PS[4 + (cc % 2)]
                em.step("tensor", lambda e, cc=cc, bank=bank: e.matmul(
                    bank[0:128, 0:128], A["lo_t"][:, cc * 128:(cc + 1) * 128],
                    A["identbf"][:], start=True, stop=True),
                    waits=[(vcs, em.led[vcs])], inc=tcs)
                em.step("vector", lambda e, cc=cc, bank=bank: e.tensor_copy(
                    A["loT_t"][:, cc, :], bank[0:128, 0:128]),
                    waits=[(tcs, em.led[tcs])], inc=vcs)
            def router(e, ti=ti):
                n = 0
                for cc in range(8):
                    for lh, wr in ((A["xnT_g"][:, cc, ti * 128:(ti + 1) * 128],
                                    A["wr_hi"][:, cc, :]),
                                   (A["xnT_g"][:, cc, ti * 128:(ti + 1) * 128],
                                    A["wr_lo"][:, cc, :]),
                                   (A["loT_t"][:, cc, :], A["wr_hi"][:, cc, :]),
                                   (A["loT_t"][:, cc, :], A["wr_lo"][:, cc, :])):
                        last = e.matmul(PS[5][0:128, 0:8], lh, wr,
                                        start=(n == 0), stop=(n == 31))
                        n += 1
                return last
            em.step("tensor", router, waits=[(vcs, em.led[vcs])], inc=tcs)
            em.step("vector", lambda e: e.tensor_copy(A["logits"][:], PS[5][0:128, 0:8]),
                    waits=[(tcs, em.led[tcs]), (sts, em.led[sts])], inc=vcs)
            em.step("vector", lambda e: e.reduce_max(A["lmax"][:], A["logits"][:],
                                                     axis=AX.X),
                    waits=[(vcs, em.led[vcs])], inc=vcs)
            em.step("scalar", lambda e: None, waits=[(vcs, em.led[vcs])], inc=scs)
            em.step("vector", lambda e: e.tensor_scalar(
                A["masks"][:], A["logits"][:], A["lmax"][:, 0:1], 1.0,
                op0=ALU.is_equal, op1=ALU.mult),
                waits=[(scs, em.led[scs]), (sts, em.led[sts])], inc=vcs)
            em.step("vector", lambda e: e.tensor_copy(A["masks_bf"][:, 0:8],
                                                      A["masks"][:]),
                    waits=[(vcs, em.led[vcs])], inc=vcs)
            em.step("tensor", lambda e: e.matmul(PS[4][0:128, 0:128],
                                                 A["masks_bf"][:], A["identbf"][:],
                                                 start=True, stop=True),
                    waits=[(vcs, em.led[vcs]), ("wd", 3 * 16 + 1)], inc=tcs)
            em.step("vector", lambda e, ti=ti: e.tensor_copy(
                A["masksT_g"][0:8, ti * 128:(ti + 1) * 128], PS[4][0:8, 0:128]),
                waits=[(tcs, em.led[tcs])], inc=vcs)
            em.dma("sync", g["dbg_logits"][T], A["logits"][:],
                   waits=[(vcs, em.led[vcs])], inc=sts)
            em.dma("sync", g["dbg_masks"][T], A["masks"][:],
                   waits=[(vcs, em.led[vcs])], inc=sts)
            em.dma("sync", g["dbg_lmax"][T], A["lmax"][:],
                   waits=[(vcs, em.led[vcs])], inc=sts)
            em.dma("sync", g["xn0T_hbm"][T, :, :].rearrange("p (c t) -> p c t", t=128),
                   A["xnT_g"][:, :, ti * 128:(ti + 1) * 128],
                   waits=[(vcs, em.led[vcs])], inc=sts)

        if DEV_STAGE < 1:
            continue
        for e_i in range(8):
            em.step("tensor", lambda e, e_i=e_i, ntok=ntok: e.matmul(
                PS[5][0:128, 0:ntok], A["onehot8"][:, e_i, :],
                A["masksT_g"][0:8, 0:ntok], start=True, stop=True),
                waits=[(vcs, em.led[vcs])], inc=tcs)
            em.step("vector", lambda e, ntok=ntok: e.tensor_copy(
                A["mask_bc"][:, 0:ntok], PS[5][0:128, 0:ntok]),
                waits=[(tcs, em.led[tcs])], inc=vcs)
            em.step("vector", lambda e, e_i=e_i, ntok=ntok: e.tensor_tensor(
                A["rhs_e"][:, e_i, :, 0:ntok], A["xnT_g"][:, :, 0:ntok],
                A["mask_bc"][:, 0:ntok].rearrange("p (o t) -> p o t", o=1)
                    .broadcast_to([128, 8, ntok]), op=ALU.mult),
                waits=[(vcs, em.led[vcs])], inc=vcs)

        if DEV_STAGE < 2:
            continue
        for isl in range(32):
            dbuf = isl % 2
            wsl = A["w1slab"][:, dbuf, :, :, :]
            if first_pass:
                em.dma("gpsimd", wsl, g["W1_staged"][isl],
                       waits=[(tcs, em.led[tcs])], inc=wd)
                em.dma("gpsimd", g["W1bf"][isl], wsl,
                       waits=[(wd, em.led[wd])], inc=wd)
            else:
                em.dma("gpsimd", wsl, g["W1bf"][isl],
                       waits=[(tcs, em.led[tcs])], inc=wd)
            wdw = em.led[wd]
            def w1mm(e, isl=isl, wsl=wsl, ntok=ntok):
                last = None
                for e_i in range(8):
                    for cc in range(8):
                        last = e.matmul(PS[0][0:128, 0:ntok], wsl[:, e_i, cc, :],
                                        A["rhs_e"][:, e_i, cc, 0:ntok],
                                        start=(e_i == 0 and cc == 0), stop=False)
                last = e.matmul(PS[0][0:128, 0:ntok],
                                A["b1sb"][0:8, isl * 128:(isl + 1) * 128],
                                A["masksT_g"][0:8, 0:ntok], start=False, stop=True)
                return last
            em.step("tensor", w1mm,
                    waits=[(wd, wdw), (vcs, em.led[vcs]), (scs, em.led[scs])], inc=tcs)
            em.step("scalar", lambda e, isl=isl, ntok=ntok: e.activation(
                A["inter_g"][:, isl, 0:ntok], PS[0][0:128, 0:ntok], AF.Gelu),
                waits=[(tcs, em.led[tcs])], inc=scs)

        if DEV_STAGE < 3:
            continue
        if gi == len(GROUPS) - 1:
            em.dma("sync", g["dbg_mbc"], A["mask_bc"][:],
                   waits=[(vcs, em.led[vcs])], inc=sts)
            em.dma("sync", g["dbg_mrow"].rearrange("o (e t) -> (o e) t", e=8),
                   A["masksT_g"][0:8, 0:512],
                   waits=[(vcs, em.led[vcs])], inc=sts)
            em.dma("sync", g["dbg_rhs"], A["rhs_e"][:],
                   waits=[(vcs, em.led[vcs]), (tcs, em.led[tcs])], inc=sts)
            em.dma("sync", g["dbg_inter"], A["inter_g"][:],
                   waits=[(scs, em.led[scs])], inc=sts)
        for pair0 in range(0, gn, 2):
            pn = min(2, gn - pair0)
            for isl in range(32):
                dbuf = isl % 2
                wos = A["woslab"][:, dbuf, :, :]
                if first_pass and pair0 == 0:
                    em.dma("gpsimd", wos, g["Wo_staged"][isl],
                           waits=[(tcs, em.led[tcs])], inc=wd)
                    em.dma("gpsimd", g["Wobf"][isl], wos,
                           waits=[(wd, em.led[wd])], inc=wd)
                else:
                    em.dma("gpsimd", wos, g["Wobf"][isl],
                           waits=[(tcs, em.led[tcs])], inc=wd)
                wdw = em.led[wd]
                def womm(e, isl=isl, wos=wos, pair0=pair0, pn=pn):
                    last = None
                    for ti in range(pn):
                        t = pair0 + ti
                        for nh in range(2):
                            last = e.matmul(
                                PS[1 + 2 * ti + nh][0:128, 0:512],
                                A["inter_g"][:, isl, t * 128:(t + 1) * 128],
                                wos[:, nh * 4:(nh + 1) * 4, :].rearrange("p a b -> p (a b)"),
                                start=(isl == 0), stop=False)
                    return last
                em.step("tensor", womm, waits=[(wd, wdw), (scs, em.led[scs]),
                                               (vcs, em.led[vcs])], inc=tcs)
            def bioadd(e, pair0=pair0, pn=pn):
                last = None
                for ti in range(pn):
                    for nh in range(2):
                        last = e.matmul(PS[1 + 2 * ti + nh][0:128, 0:512],
                                        A["ones_bf"][0:1, :],
                                        A["rowbuf2_bf"][0:1, nh * 512:(nh + 1) * 512],
                                        start=False, stop=True)
                return last
            em.step("tensor", bioadd, waits=[(vcs, em.led[vcs])], inc=tcs)
            tc_done = em.led[tcs]
            for ti in range(pn):
                t = pair0 + ti
                T = g0t + t
                xt = A["x_g"][:, t % 2, :]
                em.dma("sync", xt, g["x_ext"][(T + 1) * 128:(T + 2) * 128, :],
                       waits=[(vcs, em.led[vcs]), (scs, em.led[scs])], inc=ld)
                emit_stats(em, A, xt, "s0", "eps_1em5", "sq_scratch",
                           extra_waits=[(ld, em.led[ld]), (tcs, tc_done)])
                em.step("scalar", lambda e, xt=xt: e.activation(
                    A["xn_tm"][:], xt, AF.Identity, bias=A["s0_negmr"][:, 0:1],
                    scale=A["s0_r"][:, 0:1]),
                    waits=[(vcs, em.led[vcs])], inc=scs)
                def yadd(e, ti=ti):
                    e.tensor_tensor(A["y_t"][:, 0:512], PS[1 + 2 * ti][0:128, 0:512],
                                    A["xn_tm"][:, 0:512], op=ALU.add)
                    return e.tensor_tensor(A["y_t"][:, 512:1024],
                                           PS[2 + 2 * ti][0:128, 0:512],
                                           A["xn_tm"][:, 512:1024], op=ALU.add)
                em.step("vector", yadd, waits=[(scs, em.led[scs]),
                                               (sts, em.led[sts])], inc=vcs)
                em.dma("sync", g["dbg_y"][T], A["y_t"][:],
                       waits=[(vcs, em.led[vcs])], inc=sts)
                emit_stats(em, A, A["y_t"][:], "sy", "eps_1em12", "sq_scratch",
                           extra_waits=[("vc", em.led["vc"])])
                em.step("scalar", lambda e: e.activation(
                    A["xn_tm"][:], A["y_t"][:], AF.Identity,
                    bias=A["sy_negmr"][:, 0:1], scale=A["sy_r"][:, 0:1]),
                    waits=[(vcs, em.led[vcs])], inc=scs)
                em.step("vector", lambda e: e.tensor_tensor(
                    A["xn_tm"][:], A["xn_tm"][:], A["gam_bc"][:], op=ALU.mult),
                    waits=[(scs, em.led[scs]), (vcs, em.led[vcs])], inc=vcs)
                em.step("vector", lambda e: e.tensor_tensor(
                    A["xn_tm"][:], A["xn_tm"][:], A["bet_bc"][:], op=ALU.add),
                    waits=[(vcs, em.led[vcs])], inc=vcs)
                em.step("scalar", lambda e: e.activation(A["g0_t"][:], A["xn_tm"][:],
                                                         AF.Silu),
                        waits=[(vcs, em.led[vcs]), (sts, em.led[sts])], inc=scs)
                em.dma("sync", g["g0_hbm"][T, :, :], A["g0_t"][:],
                       waits=[(scs, em.led[scs])], inc=sts)
    em.barrier()
    pa.close()
    return em.led["st"]


def _ph_attn(nc, em, g, A, PS, st_before):
    A = dict(A)
    if DEV_PHASES < 2:
        return
    pa = PhaseAlloc(nc, A)
    sbuf = pa.sbuf
    ld, wd, tcs, vcs, scs, sts = "ld", "wd", "tc", "vc", "sc", "st"

    sbuf("qkvw", (128, 8, 3072), BF16)
    sbuf("rqw", (128, 8, 256), BF16)
    sbuf("routw", (64, 4, 1024), BF16)
    sbuf("outw", (64, 16, 1024), BF16)
    sbuf("glgr", (64, 2), BF16)
    sbuf("qkb_col", (128, 16), F32)
    sbuf("bv_row", (1, 1024), BF16)
    sbuf("rqb_col", (128, 2), F32)
    sbuf("routb_col", (64, 16), F32)
    sbuf("outb_row", (1, 1024), BF16)
    sbuf("rkvw", (128, 8, 512), BF16)
    sbuf("rkvb_row", (1, 512), BF16)
    sbuf("rkvb_col", (128, 2), F32)
    em.dma("gpsimd", A["qkvw"][:], g["L0_attn_qkv_w"].rearrange("(c p) f -> p c f", p=128), inc=wd)
    em.dma("gpsimd", A["rqw"][:], g["L0_attn_rq_w"].rearrange("(c p) f -> p c f", p=128), inc=wd)
    em.dma("gpsimd", A["routw"][:], g["L0_attn_rout_w"].rearrange("(l p) f -> p l f", p=64), inc=wd)
    em.dma("gpsimd", A["outw"][:], g["L0_attn_out_w"].rearrange("(h p) f -> p h f", p=64), inc=wd)
    em.dma("gpsimd", A["glgr"][:, 0:1], g["L0_attn_gl_w"], inc=wd)
    em.dma("gpsimd", A["glgr"][:, 1:2], g["L0_attn_gr_w"], inc=wd)
    em.dma("sync", A["qkb_col"][:], g["L0_attn_qkv_b"][0:2048].rearrange("(s p) -> p s", p=128),
           inc=ld, slow=True)
    em.dma("gpsimd", A["bv_row"][:], g["L0_attn_qkv_b"][2048:3072].rearrange("(o d) -> o d", o=1), inc=wd)
    em.dma("sync", A["rqb_col"][:], g["L0_attn_rq_b"].rearrange("(s p) -> p s", p=128),
           inc=ld, slow=True)
    em.dma("sync", A["routb_col"][:], g["L0_attn_rout_b"].rearrange("(h p) -> p h", p=64),
           inc=ld, slow=True)
    em.dma("gpsimd", A["outb_row"][:], g["L0_attn_out_b"].rearrange("(o d) -> o d", o=1), inc=wd)
    em.dma("gpsimd", A["rkvw"][:], g["L0_attn_rkv_w"].rearrange("(c p) f -> p c f", p=128), inc=wd)
    em.dma("gpsimd", A["rkvb_row"][:], g["L0_attn_rkv_b"].rearrange("(o d) -> o d", o=1), inc=wd)
    em.dma("sync", A["rkvb_col"][:], g["L0_attn_rkv_b"][0:256].rearrange("(s p) -> p s", p=128),
           inc=ld, slow=True)
    base_w = em.led[wd]; base_l = em.led[ld]

    if DEV_ATTN < -2:
        em.barrier()
        pa.close()
        return
    sbuf("xf_t", (128, 2, D), F32)
    alloc_stats(pa, "s0")
    sbuf("sq_scratch", (128, D), F32)
    sbuf("hi_t", (128, D), BF16)
    sbuf("pooled32", (128, 8, 64), F32)
    sbuf("pooledT", (128, 8, 64), BF16)
    sbuf("rk_fm", (128, 2, 64), BF16)
    sbuf("rv_tm", (64, 256), BF16)
    for j in range(64):
        xt = A["xf_t"][:, j % 2, :]
        em.dma("sync", xt, g["x_full"][j * 128:(j + 1) * 128, :],
               waits=[(vcs, em.led[vcs]), (scs, em.led[scs])], inc=ld)
        emit_stats(em, A, xt, "s0", "eps_1em5", "sq_scratch",
                   extra_waits=[(ld, em.led[ld])])
        em.step("scalar", lambda e, xt=xt: e.activation(
            A["hi_t"][:], xt, AF.Identity, bias=A["s0_negmr"][:, 0:1],
            scale=A["s0_r"][:, 0:1]),
            waits=[(vcs, em.led[vcs])], inc=scs)
        for cc in range(8):
            bank = PS[4 + (cc % 2)]
            em.step("tensor", lambda e, cc=cc, bank=bank: e.matmul(
                bank[0:128, 0:128], A["hi_t"][:, cc * 128:(cc + 1) * 128],
                A["identbf"][:], start=True, stop=True),
                waits=[(scs, em.led[scs]), (vcs, em.led[vcs]), (wd, base_w)], inc=tcs)
            em.step("vector", lambda e, cc=cc, bank=bank, j=j: e.reduce_sum(
                A["pooled32"][:, cc, j:j + 1], bank[0:128, 0:128], axis=AX.X),
                waits=[(tcs, em.led[tcs])], inc=vcs)
    if DEV_ATTN < -1:
        em.barrier()
        pa.close()
        return
    em.step("vector", lambda e: e.tensor_scalar(
        A["pooledT"][:].rearrange("p c l -> p (c l)"),
        A["pooled32"][:].rearrange("p c l -> p (c l)"), 1.0 / 128, None, op0=ALU.mult),
        waits=[(vcs, em.led[vcs])], inc=vcs)
    for sub in range(2):
        def rkmm(e, sub=sub):
            last = None
            for cc in range(8):
                last = e.matmul(PS[0][0:128, 0:64],
                                A["rkvw"][:, cc, sub * 128:(sub + 1) * 128],
                                A["pooledT"][:, cc, :], start=(cc == 0), stop=(cc == 7))
            return last
        em.step("tensor", rkmm, waits=[(vcs, em.led[vcs]), (scs, em.led[scs]),
                                       (wd, base_w), (ld, base_l)], inc=tcs)
        em.step("scalar", lambda e, sub=sub: e.activation(
            A["rk_fm"][:, sub, :], PS[0][0:128, 0:64], AF.Identity,
            bias=A["rkvb_col"][:, sub:sub + 1]),
            waits=[(tcs, em.led[tcs])], inc=scs)
    if DEV_ATTN < 0:
        em.barrier()
        pa.close()
        return
    def rvmm(e):
        last = e.matmul(PS[1][0:64, 0:256], A["ones_bf"][0:1, 0:64],
                        A["rkvb_row"][0:1, 256:512], start=True, stop=False)
        for cc in range(8):
            last = e.matmul(PS[1][0:64, 0:256], A["pooledT"][:, cc, :],
                            A["rkvw"][:, cc, 256:512], start=False, stop=(cc == 7))
        return last
    em.step("tensor", rvmm, waits=[(scs, em.led[scs]), (vcs, em.led[vcs])], inc=tcs)
    em.step("vector", lambda e: e.tensor_copy(A["rv_tm"][:], PS[1][0:64, 0:256]),
            waits=[(tcs, em.led[tcs])], inc=vcs)

    if DEV_ATTN < 1:
        em.barrier()
        pa.close()
        return
    sbuf("xnT_a", (128, 8, 512), BF16)
    sbuf("q_fm", (128, 8, 512), BF16)
    sbuf("k_fm", (128, 8, 640), BF16)
    sbuf("v_tm", (128, 5, D), BF16)
    sbuf("rq_fm", (128, 2, 512), BF16)
    sbuf("msk_t", (128, 2, 256), F32)
    sbuf("mlat_t", (128, 2, 64), F32)
    sbuf("zl_t", (128, 2, 2), F32)
    sbuf("pad_t", (128, 1), F32)
    sbuf("sc4", (128, 4, 256), F32)
    A["scl4"] = A["sc4"][:, 0, :].rearrange("p (l k) -> p l k", k=64)
    sbuf("negmax4", (128, 4), F32)
    sbuf("den4", (128, 4), F32)
    sbuf("rec4", (128, 4), F32)
    sbuf("diag4", (128, 4, 128), BF16)
    sbuf("p4", (128, 4, 256), BF16)
    sbuf("pT4", (128, 8, 128), BF16)
    sbuf("pl4", (128, 4, 64), BF16)
    sbuf("pTl4", (64, 4, 128), BF16)
    sbuf("local_fm", (64, 16, 128), BF16)
    sbuf("lat_fm", (64, 4, 128), BF16)
    sbuf("remote_fm", (64, 16, 128), BF16)
    sbuf("gates_row", (1, 2048), BF16)
    sbuf("gb_fm", (64, 16, 128), BF16)
    sbuf("t1", (64, 16, 128), BF16)
    sbuf("mixed", (64, 16, 128), BF16)
    sbuf("g0_t", (128, D), BF16)
    A["m1_t"] = A["sq_scratch"]
    sbuf("x1_t", (128, D), F32)

    for gi, (g0t, gn) in enumerate(GROUPS):
        if gi >= DEV_GROUPS:
            break
        ntok = gn * 128
        if gi > 0:
            em.step("vector", lambda e: e.tensor_copy(
                A["k_fm"][:, :, 0:128], A["k_fm"][:, :, 512:640]),
                waits=[(vcs, em.led[vcs]), (tcs, em.led[tcs])], inc=vcs)
            em.step("vector", lambda e: e.tensor_copy(
                A["v_tm"][:, 0, :], A["v_tm"][:, 4, :]),
                waits=[(vcs, em.led[vcs])], inc=vcs)
        else:
            em.step("gpsimd", lambda e: (e.memset(A["k_fm"][:, :, 0:128], 0.0),
                                         e.memset(A["v_tm"][:, 0, :], 0.0))[-1],
                    waits=[("wd", base_w)], inc=wd)
        for ti in range(gn):
            T = g0t + ti
            em.dma("sync", A["xnT_a"][:, :, ti * 128:(ti + 1) * 128],
                   g["xn0T_hbm"][T, :, :].rearrange("p (c t) -> p c t", t=128),
                   waits=[(tcs, em.led[tcs]), (vcs, em.led[vcs])], inc=ld)
        ld_x = em.led[ld]
        for half, dst in ((0, "q_fm"), (1, "k_fm")):
            for sub in range(8):
                def qkmm(e, half=half, sub=sub, ntok=ntok):
                    last = None
                    for cc in range(8):
                        last = e.matmul(PS[0][0:128, 0:ntok],
                                        A["qkvw"][:, cc,
                                                  half * 1024 + sub * 128:
                                                  half * 1024 + (sub + 1) * 128],
                                        A["xnT_a"][:, cc, 0:ntok],
                                        start=(cc == 0), stop=(cc == 7))
                    return last
                em.step("tensor", qkmm, waits=[(ld, ld_x), (vcs, em.led[vcs]),
                                               (scs, em.led[scs])], inc=tcs)
                off = 128 if dst == "k_fm" else 0
                em.step("scalar", lambda e, dst=dst, sub=sub, half=half, off=off,
                        ntok=ntok: e.activation(
                    A[dst][:, sub, off:off + ntok], PS[0][0:128, 0:ntok], AF.Identity,
                    bias=A["qkb_col"][:, half * 8 + sub:half * 8 + sub + 1]),
                    waits=[(tcs, em.led[tcs])], inc=scs)
        for sub in range(2):
            def rqmm(e, sub=sub, ntok=ntok):
                last = None
                for cc in range(8):
                    last = e.matmul(PS[0][0:128, 0:ntok],
                                    A["rqw"][:, cc, sub * 128:(sub + 1) * 128],
                                    A["xnT_a"][:, cc, 0:ntok],
                                    start=(cc == 0), stop=(cc == 7))
                return last
            em.step("tensor", rqmm, waits=[(vcs, em.led[vcs]), (scs, em.led[scs])], inc=tcs)
            em.step("scalar", lambda e, sub=sub, ntok=ntok: e.activation(
                A["rq_fm"][:, sub, 0:ntok], PS[0][0:128, 0:ntok], AF.Identity,
                bias=A["rqb_col"][:, sub:sub + 1]),
                waits=[(tcs, em.led[tcs])], inc=scs)
        for ti in range(gn):
            def vmm(e, ti=ti):
                last = e.matmul(PS[1][0:128, 0:512], A["ones_bf"][0:1, :],
                                A["bv_row"][0:1, 0:512], start=True, stop=False)
                last = e.matmul(PS[2][0:128, 0:512], A["ones_bf"][0:1, :],
                                A["bv_row"][0:1, 512:1024], start=True, stop=False)
                for cc in range(8):
                    last = e.matmul(PS[1][0:128, 0:512],
                                    A["xnT_a"][:, cc, ti * 128:(ti + 1) * 128],
                                    A["qkvw"][:, cc, 2048:2560],
                                    start=False, stop=(cc == 7))
                    last = e.matmul(PS[2][0:128, 0:512],
                                    A["xnT_a"][:, cc, ti * 128:(ti + 1) * 128],
                                    A["qkvw"][:, cc, 2560:3072],
                                    start=False, stop=(cc == 7))
                return last
            em.step("tensor", vmm, waits=[(vcs, em.led[vcs]), (scs, em.led[scs])], inc=tcs)
            def vevac(e, ti=ti):
                e.tensor_copy(A["v_tm"][:, 1 + ti, 0:512], PS[1][0:128, 0:512])
                return e.tensor_copy(A["v_tm"][:, 1 + ti, 512:1024], PS[2][0:128, 0:512])
            em.step("vector", vevac, waits=[(tcs, em.led[tcs])], inc=vcs)

        if DEV_ATTN < 2:
            continue
        for ti in range(gn):
            T = g0t + ti
            em.dma("sync", A["msk_t"][:, ti % 2, :], g["mask_local"][T],
                   waits=[(vcs, em.led[vcs])], inc=ld)
            em.dma("sync", A["mlat_t"][:, ti % 2, :], g["mask_lat"][T],
                   waits=[(vcs, em.led[vcs])], inc=ld)
            em.dma("sync", A["zl_t"][:, ti % 2, 0:1], g["zl_local"][T],
                   waits=[(vcs, em.led[vcs])], inc=ld)
            em.dma("sync", A["zl_t"][:, ti % 2, 1:2], g["zl_lat"][T],
                   waits=[(vcs, em.led[vcs])], inc=ld)
            ld_m = em.led[ld]
            if DEV_ATTN == 18:
                continue
            for wv in range(4):
                def smm(e, wv=wv, ti=ti):
                    last = None
                    for hh in DEV_SMM_HH:
                        h = wv * 4 + hh
                        sub = h // 2
                        po = (h % 2) * 64
                        last = e.matmul(
                            PS[hh][0:128, 0:256],
                            A["q_fm"][po:po + 64, sub, ti * 128:(ti + 1) * 128],
                            A["k_fm"][po:po + 64, sub, ti * 128:(ti + 2) * 128],
                            start=True, stop=True)
                    return last
                em.step("tensor", smm, waits=[(vcs, em.led[vcs]),
                                              (scs, em.led[scs])], inc=tcs)
                if DEV_ATTN == 185:
                    continue
                def smask(e, ti=ti):
                    last = None
                    for hh in range(4):
                        last = e.scalar_tensor_tensor(
                            A["sc4"][:, hh, :], PS[hh][0:128, 0:256],
                            0.125, A["msk_t"][:, ti % 2, :], op0=ALU.mult, op1=ALU.add)
                    return last
                em.step("vector", smask, waits=[(tcs, em.led[tcs]), (ld, ld_m)], inc=vcs)
                if DEV_ATTN == 20:
                    continue
                em.step("vector", lambda e: e.reduce_max(
                    A["negmax4"][:], A["sc4"][:], axis=AX.X, negate=True),
                    waits=[(vcs, em.led[vcs])], inc=vcs)
                def sexp(e):
                    last = None
                    for hh in range(4):
                        last = e.activation(A["p4"][:, hh, :], A["sc4"][:, hh, :],
                                            AF.Exp, bias=A["negmax4"][:, hh:hh + 1],
                                            accum_out=A["den4"][:, hh:hh + 1])
                    return last
                em.step("scalar", sexp, waits=[(vcs, em.led[vcs])], inc=scs)
                if DEV_ATTN == 21:
                    continue
                em.step("vector", lambda e: e.reciprocal(A["rec4"][:], A["den4"][:]),
                        waits=[(scs, em.led[scs])], inc=vcs)
                em.step("scalar", lambda e: None, waits=[(vcs, em.led[vcs])], inc=scs)
                em.step("vector", lambda e, ti=ti: e.tensor_scalar(
                    A["rec4"][:], A["rec4"][:], A["zl_t"][:, ti % 2, 0:1], None,
                    op0=ALU.mult),
                    waits=[(scs, em.led[scs])], inc=vcs)
                em.step("scalar", lambda e: None, waits=[(vcs, em.led[vcs])], inc=scs)
                def sdiag(e):
                    last = None
                    for hh in range(4):
                        last = e.tensor_scalar(A["diag4"][:, hh, :], A["iota_row"][:],
                                               A["pidx"][:, 0:1], A["rec4"][:, hh:hh + 1],
                                               op0=ALU.is_equal, op1=ALU.mult)
                    return last
                em.step("vector", sdiag, waits=[(scs, em.led[scs])], inc=vcs)
                if DEV_ATTN == 22:
                    continue
                for kc in range(2):
                    def ptmm(e, kc=kc):
                        last = None
                        for hh in range(4):
                            last = e.matmul(
                                PS[hh][0:128, 0:128],
                                A["p4"][:, hh, kc * 128:(kc + 1) * 128],
                                A["diag4"][:, hh, :], start=True, stop=True)
                        return last
                    em.step("tensor", ptmm, waits=[(vcs, em.led[vcs]),
                                                   (scs, em.led[scs])], inc=tcs)
                    def ptev(e, kc=kc):
                        last = None
                        for hh in range(4):
                            last = e.tensor_copy(A["pT4"][:, 2 * hh + kc, :],
                                                 PS[hh][0:128, 0:128])
                        return last
                    em.step("vector", ptev, waits=[(tcs, em.led[tcs])], inc=vcs)
                if DEV_ATTN == 23:
                    continue
                def lmm(e, wv=wv, ti=ti):
                    last = None
                    for hh in range(4):
                        h = wv * 4 + hh
                        last = e.matmul(PS[hh][0:64, 0:128],
                                        A["v_tm"][:, ti, h * 64:(h + 1) * 64],
                                        A["pT4"][:, 2 * hh, :], start=True, stop=False)
                        last = e.matmul(PS[hh][0:64, 0:128],
                                        A["v_tm"][:, 1 + ti, h * 64:(h + 1) * 64],
                                        A["pT4"][:, 2 * hh + 1, :], start=False, stop=True)
                    return last
                em.step("tensor", lmm, waits=[(vcs, em.led[vcs])], inc=tcs)
                def locev(e, wv=wv):
                    last = None
                    for hh in range(4):
                        last = e.tensor_copy(A["local_fm"][:, wv * 4 + hh, :],
                                             PS[hh][0:64, 0:128])
                    return last
                em.step("vector", locev, waits=[(tcs, em.led[tcs])], inc=vcs)
            if DEV_ATTN < 3 or 20 <= DEV_ATTN < 30:
                continue
            def lsmm(e, ti=ti):
                last = None
                for lh in range(4):
                    sub = lh // 2
                    po = (lh % 2) * 64
                    last = e.matmul(PS[lh][0:128, 0:64],
                                    A["rq_fm"][po:po + 64, sub, ti * 128:(ti + 1) * 128],
                                    A["rk_fm"][po:po + 64, sub, :],
                                    start=True, stop=True)
                return last
            em.step("tensor", lsmm, waits=[(vcs, em.led[vcs]),
                                           (scs, em.led[scs])], inc=tcs)
            def lmask(e, ti=ti):
                last = None
                for lh in range(4):
                    last = e.scalar_tensor_tensor(
                        A["scl4"][:, lh, :], PS[lh][0:128, 0:64],
                        0.125, A["mlat_t"][:, ti % 2, :], op0=ALU.mult, op1=ALU.add)
                return last
            em.step("vector", lmask, waits=[(tcs, em.led[tcs]), (ld, ld_m)], inc=vcs)
            em.step("vector", lambda e: e.reduce_max(
                A["negmax4"][:], A["scl4"][:], axis=AX.X, negate=True),
                waits=[(vcs, em.led[vcs])], inc=vcs)
            def lexp(e):
                last = None
                for lh in range(4):
                    last = e.activation(A["pl4"][:, lh, :], A["scl4"][:, lh, :],
                                        AF.Exp, bias=A["negmax4"][:, lh:lh + 1],
                                        accum_out=A["den4"][:, lh:lh + 1])
                return last
            em.step("scalar", lexp, waits=[(vcs, em.led[vcs])], inc=scs)
            em.step("vector", lambda e: e.reciprocal(A["rec4"][:], A["den4"][:]),
                    waits=[(scs, em.led[scs])], inc=vcs)
            em.step("scalar", lambda e: None, waits=[(vcs, em.led[vcs])], inc=scs)
            em.step("vector", lambda e, ti=ti: e.tensor_scalar(
                A["rec4"][:], A["rec4"][:], A["zl_t"][:, ti % 2, 1:2], None,
                op0=ALU.mult),
                waits=[(scs, em.led[scs])], inc=vcs)
            em.step("scalar", lambda e: None, waits=[(vcs, em.led[vcs])], inc=scs)
            def ldiag(e):
                last = None
                for lh in range(4):
                    last = e.tensor_scalar(A["diag4"][:, lh, :], A["iota_row"][:],
                                           A["pidx"][:, 0:1], A["rec4"][:, lh:lh + 1],
                                           op0=ALU.is_equal, op1=ALU.mult)
                return last
            em.step("vector", ldiag, waits=[(scs, em.led[scs])], inc=vcs)
            def lptmm(e):
                last = None
                for lh in range(4):
                    last = e.matmul(PS[lh][0:64, 0:128],
                                    A["pl4"][:, lh, :], A["diag4"][:, lh, :],
                                    start=True, stop=True)
                return last
            em.step("tensor", lptmm, waits=[(vcs, em.led[vcs]),
                                            (scs, em.led[scs])], inc=tcs)
            def lptev(e):
                last = None
                for lh in range(4):
                    last = e.tensor_copy(A["pTl4"][:, lh, :], PS[lh][0:64, 0:128])
                return last
            em.step("vector", lptev, waits=[(tcs, em.led[tcs])], inc=vcs)
            def lomm(e):
                last = None
                for lh in range(4):
                    last = e.matmul(PS[lh][0:64, 0:128],
                                    A["rv_tm"][0:64, lh * 64:(lh + 1) * 64],
                                    A["pTl4"][:, lh, :], start=True, stop=True)
                return last
            em.step("tensor", lomm, waits=[(vcs, em.led[vcs])], inc=tcs)
            def lomev(e):
                last = None
                for lh in range(4):
                    last = e.tensor_copy(A["lat_fm"][:, lh, :], PS[lh][0:64, 0:128])
                return last
            em.step("vector", lomev, waits=[(tcs, em.led[tcs])], inc=vcs)
            if DEV_ATTN < 4 or 20 <= DEV_ATTN < 30:
                continue
            for wv in range(4):
                def rmm(e, wv=wv):
                    last = None
                    for hh in range(4):
                        h = wv * 4 + hh
                        for lh in range(4):
                            last = e.matmul(PS[hh][0:64, 0:128],
                                            A["routw"][:, lh, h * 64:(h + 1) * 64],
                                            A["lat_fm"][:, lh, :],
                                            start=(lh == 0), stop=(lh == 3))
                    return last
                em.step("tensor", rmm, waits=[(vcs, em.led[vcs]),
                                              (scs, em.led[scs])], inc=tcs)
                def revac(e, wv=wv):
                    last = None
                    for hh in range(4):
                        h = wv * 4 + hh
                        last = e.activation(A["remote_fm"][:, h, :],
                                            PS[hh][0:64, 0:128],
                                            AF.Identity, bias=A["routb_col"][:, h:h + 1])
                    return last
                em.step("scalar", revac, waits=[(tcs, em.led[tcs])], inc=scs)
            if 19 <= DEV_ATTN < 30:
                continue
            for wv in range(4):
                def gmm(e, wv=wv):
                    last = None
                    for hh in range(4):
                        h = wv * 4 + hh
                        last = e.matmul(PS[hh][0:1, 0:128],
                                        A["glgr"][0:64, 0:1], A["local_fm"][0:64, h, :],
                                        start=True, stop=False)
                        last = e.matmul(PS[hh][0:1, 0:128],
                                        A["glgr"][0:64, 1:2], A["remote_fm"][0:64, h, :],
                                        start=False, stop=True)
                    return last
                em.step("tensor", gmm, waits=[(vcs, em.led[vcs]),
                                              (scs, em.led[scs])], inc=tcs)
                def gsig(e, wv=wv):
                    last = None
                    for hh in range(4):
                        h = wv * 4 + hh
                        last = e.activation(A["gates_row"][0:1, h * 128:(h + 1) * 128],
                                            PS[hh][0:1, 0:128], AF.Sigmoid)
                    return last
                em.step("scalar", gsig, waits=[(tcs, em.led[tcs])], inc=scs)
            for wv in range(4):
                def gbmm(e, wv=wv):
                    last = None
                    for hh in range(4):
                        h = wv * 4 + hh
                        last = e.matmul(PS[hh][0:64, 0:128],
                                        A["ones_bf"][0:1, 0:64],
                                        A["gates_row"][0:1, h * 128:(h + 1) * 128],
                                        start=True, stop=True)
                    return last
                em.step("tensor", gbmm, waits=[(scs, em.led[scs]),
                                               (vcs, em.led[vcs])], inc=tcs)
                def gbev(e, wv=wv):
                    last = None
                    for hh in range(4):
                        h = wv * 4 + hh
                        last = e.tensor_copy(A["gb_fm"][:, h, :], PS[hh][0:64, 0:128])
                    return last
                em.step("vector", gbev, waits=[(tcs, em.led[tcs])], inc=vcs)
            em.step("vector", lambda e: e.tensor_tensor(
                A["t1"][:].rearrange("p a b -> p (a b)"),
                A["local_fm"][:].rearrange("p a b -> p (a b)"),
                A["remote_fm"][:].rearrange("p a b -> p (a b)"), op=ALU.subtract),
                waits=[(vcs, em.led[vcs]), (scs, em.led[scs])], inc=vcs)
            em.step("vector", lambda e: e.tensor_tensor(
                A["t1"][:].rearrange("p a b -> p (a b)"),
                A["t1"][:].rearrange("p a b -> p (a b)"),
                A["gb_fm"][:].rearrange("p a b -> p (a b)"), op=ALU.mult),
                waits=[(vcs, em.led[vcs])], inc=vcs)
            em.step("vector", lambda e: e.tensor_tensor(
                A["mixed"][:].rearrange("p a b -> p (a b)"),
                A["t1"][:].rearrange("p a b -> p (a b)"),
                A["remote_fm"][:].rearrange("p a b -> p (a b)"), op=ALU.add),
                waits=[(vcs, em.led[vcs])], inc=vcs)
            def s0mm(e):
                last = e.matmul(PS[1][0:128, 0:512], A["ones_bf"][0:1, :],
                                A["outb_row"][0:1, 0:512], start=True, stop=False)
                last = e.matmul(PS[2][0:128, 0:512], A["ones_bf"][0:1, :],
                                A["outb_row"][0:1, 512:1024], start=True, stop=False)
                for h in range(16):
                    last = e.matmul(PS[1][0:128, 0:512], A["mixed"][:, h, :],
                                    A["outw"][:, h, 0:512], start=False, stop=(h == 15))
                    last = e.matmul(PS[2][0:128, 0:512], A["mixed"][:, h, :],
                                    A["outw"][:, h, 512:1024], start=False, stop=(h == 15))
                return last
            em.step("tensor", s0mm, waits=[(vcs, em.led[vcs]), (scs, em.led[scs])], inc=tcs)
            em.dma("sync", A["g0_t"][:], g["g0_hbm"][T], waits=[(vcs, em.led[vcs])], inc=ld)
            xt = A["xf_t"][:, ti % 2, :]
            em.dma("sync", xt, g["x_ext"][(T + 1) * 128:(T + 2) * 128, :],
                   waits=[(vcs, em.led[vcs]), (scs, em.led[scs])], inc=ld)
            em.dma("sync", A["pad_t"][:], g["padmask"][T + 1],
                   waits=[(vcs, em.led[vcs])], inc=ld)
            ld_e = em.led[ld]
            def s0tap(e):
                e.tensor_copy(A["x1_t"][:, 0:512], PS[1][0:128, 0:512])
                return e.tensor_copy(A["x1_t"][:, 512:1024], PS[2][0:128, 0:512])
            em.step("vector", s0tap, waits=[(tcs, em.led[tcs]), (sts, em.led[sts])],
                    inc=vcs)
            em.dma("sync", g["dbg_s0"], A["x1_t"][:], waits=[(vcs, em.led[vcs])], inc=sts)
            def mul1(e, ti=ti):
                e.tensor_tensor(A["m1_t"][:, 0:512], PS[1][0:128, 0:512],
                                A["g0_t"][:, 0:512], op=ALU.mult)
                return e.tensor_tensor(A["m1_t"][:, 512:1024], PS[2][0:128, 0:512],
                                       A["g0_t"][:, 512:1024], op=ALU.mult)
            em.step("vector", mul1, waits=[(tcs, em.led[tcs]), (ld, ld_e)], inc=vcs)
            em.step("vector", lambda e, xt=xt: e.tensor_tensor(
                A["x1_t"][:], xt, A["m1_t"][:], op=ALU.add),
                waits=[(vcs, em.led[vcs]), (sts, em.led[sts])], inc=vcs)
            em.step("vector", lambda e: e.tensor_scalar(
                A["x1_t"][:], A["x1_t"][:], A["pad_t"][:, 0:1], None, op0=ALU.mult),
                waits=[(vcs, em.led[vcs])], inc=vcs)
            em.dma("sync", g["x1_hbm"][T], A["x1_t"][:],
                   waits=[(vcs, em.led[vcs])], inc=sts)
    for nm, buf in (("dbg_local", "local_fm"), ("dbg_remote", "remote_fm"),
                    ("dbg_mixed", "mixed"), ("dbg_gates", "gates_row"),
                    ("dbg_q", "q_fm"), ("dbg_k", "k_fm"), ("dbg_v", "v_tm"),
                    ("dbg_m1", "m1_t"), ("dbg_outw", "outw"),
                    ("dbg_outb", "outb_row"), ("dbg_g0t", "g0_t")):
        em.dma("sync", g[nm], A[buf][:],
               waits=[(vcs, em.led[vcs]), (scs, em.led[scs]), (tcs, em.led[tcs])],
               inc=sts)
    em.barrier()
    pa.close()


def _ph_xn1(nc, em, g, A, PS):
    A = dict(A)
    if DEV_PHASES < 3:
        return
    pa = PhaseAlloc(nc, A)
    sbuf = pa.sbuf
    ld, wd, tcs, vcs, scs, sts = "ld", "wd", "tc", "vc", "sc", "st"
    sbuf("xt2", (128, 2, D), F32)
    sbuf("sq_scratch", (128, D), F32)
    sbuf("xn1_t", (128, D), F32)
    sbuf("zrow", (1, D), F32)
    alloc_stats(pa, "s1")
    em.step("gpsimd", lambda e: e.memset(A["zrow"][:], 0.0), inc="wd")
    zw = em.led["wd"]
    for tgt, row in (("x2fA_hbm", L1F_QA_LO * 128 - 1),
                     ("yA_hbm", L1F_QA_LO * 128 - 1),
                     ("x2fB_hbm", L1F_QB_LO * 128 - 1)):
        em.dma("sync", g[tgt].rearrange("t p d -> (t p) d")[row:row + 1, :],
               A["zrow"][:], waits=[("wd", zw)], inc="st")
    for T in range(NT_OUT):
        xt = A["xt2"][:, T % 2, :]
        em.dma("sync", xt, g["x1_hbm"][T],
               waits=[(vcs, em.led[vcs]), (scs, em.led[scs])], inc=ld)
        emit_stats(em, A, xt, "s1", "eps_1em5", "sq_scratch",
                   extra_waits=[(ld, em.led[ld])])
        em.step("scalar", lambda e, xt=xt: e.activation(
            A["xn1_t"][:], xt, AF.Identity, bias=A["s1_negmr"][:, 0:1],
            scale=A["s1_r"][:, 0:1]),
            waits=[(vcs, em.led[vcs]), (sts, em.led[sts])], inc=scs)
        em.dma("sync", g["xn1_hbm"][T], A["xn1_t"][:],
               waits=[(scs, em.led[scs])], inc=sts)
    em.barrier()
    pa.close()


def _ph_l1f_layer(nc, em, g, A, PS, l, src_hbm, kvlo, qlo, dst_hbm, store_g1, mask_name):
    A = dict(A)
    pa = PhaseAlloc(nc, A)
    sbuf = pa.sbuf
    ld, wd, tcs, vcs, scs, sts = "ld", "wd", "tc", "vc", "sc", "st"
    nkv = NT_OUT - kvlo
    nq = NT_OUT - qlo

    sbuf("wqkv", (128, 8, 1536), BF16)
    sbuf("h1_bf", (128, D), BF16)
    sbuf("h1T", (128, 8, 128), BF16)
    sbuf("qk_32", (128, 2, 512), F32)
    sbuf("ss8", (128, 16), F32)
    sbuf("nr8", (128, 16), F32)
    sbuf("r8", (128, 16), F32)
    sbuf("qf", (128, 2, 512), BF16)
    sbuf("qsb", (128, 2, 64), BF16)
    sbuf("ksb", (128, 2, 64), BF16)
    sbuf("qsrow", (1, 256), BF16)
    sbuf("ropet", (128, 4, 64), BF16)
    sbuf("sq_l1", (128, 2, 512), F32)
    sbuf("ra", (128, 512), F32)
    sbuf("rb", (128, 512), F32)
    sbuf("v_tm", (128, nkv, 512), BF16)
    sbuf("kT", (128, 4, nkv * 128), BF16)
    sbuf("qT", (128, 4, nq * 128), BF16)
    em.dma("gpsimd", A["wqkv"][:],
           g["L1_ffn_wqkv"][l].rearrange("(c p) f -> p c f", p=128), inc=wd)
    em.dma("gpsimd", A["qsrow"][0:1, 0:64],
           g["L1_ffn_q_scale"][l].rearrange("(o d) -> o d", o=1), inc=wd)
    em.dma("gpsimd", A["qsrow"][0:1, 64:128],
           g["L1_ffn_k_scale"][l].rearrange("(o d) -> o d", o=1), inc=wd)
    em.dma("gpsimd", A["qsrow"][0:1, 128:160],
           g["L1_ffn_q_scale"][l][32:64].rearrange("(o d) -> o d", o=1), inc=wd)
    em.dma("gpsimd", A["qsrow"][0:1, 160:192],
           g["L1_ffn_q_scale"][l][0:32].rearrange("(o d) -> o d", o=1), inc=wd)
    em.dma("gpsimd", A["qsrow"][0:1, 192:224],
           g["L1_ffn_k_scale"][l][32:64].rearrange("(o d) -> o d", o=1), inc=wd)
    em.dma("gpsimd", A["qsrow"][0:1, 224:256],
           g["L1_ffn_k_scale"][l][0:32].rearrange("(o d) -> o d", o=1), inc=wd)
    wd0 = em.led[wd]
    def qsb_mm(e):
        e.matmul(PS[0][0:128, 0:64], A["ones_bf"][0:1, :], A["qsrow"][0:1, 0:64],
                 start=True, stop=True)
        e.matmul(PS[1][0:128, 0:64], A["ones_bf"][0:1, :], A["qsrow"][0:1, 128:192],
                 start=True, stop=True)
        e.matmul(PS[2][0:128, 0:64], A["ones_bf"][0:1, :], A["qsrow"][0:1, 64:128],
                 start=True, stop=True)
        return e.matmul(PS[3][0:128, 0:64], A["ones_bf"][0:1, :],
                        A["qsrow"][0:1, 192:256], start=True, stop=True)
    em.step("tensor", qsb_mm, waits=[(wd, wd0), (vcs, em.led[vcs]),
                                     (scs, em.led[scs])], inc=tcs)
    def qsb_ev(e):
        e.tensor_copy(A["qsb"][:, 0, :], PS[0][0:128, 0:64])
        e.tensor_copy(A["qsb"][:, 1, :], PS[1][0:128, 0:64])
        e.tensor_copy(A["ksb"][:, 0, :], PS[2][0:128, 0:64])
        return e.tensor_copy(A["ksb"][:, 1, :], PS[3][0:128, 0:64])
    em.step("vector", qsb_ev, waits=[(tcs, em.led[tcs])], inc=vcs)

    src_rows = src_hbm.rearrange("t p d -> (t p) d")
    for T in range(kvlo, NT_OUT):
        em.dma("gpsimd", A["h1_bf"][:, 0:512],
               src_rows[T * 128:(T + 1) * 128, 0:512],
               waits=[(tcs, em.led[tcs]), (vcs, em.led[vcs])], inc=wd)
        em.dma("gpsimd", A["h1_bf"][:, 512:1024],
               src_rows[max(T * 128 - 1, 0):(T + 1) * 128 - 1 + (1 if T == 0 else 0),
                        512:1024],
               waits=[], inc=wd)
        em.dma("gpsimd", A["ropet"][:, 0, :], g["rope_cos"][T],
               waits=[(vcs, em.led[vcs])], inc=wd)
        em.dma("gpsimd", A["ropet"][:, 1, :], g["rope_sin"][T], waits=[], inc=wd)
        em.dma("gpsimd", A["ropet"][:, 2, :], g["rope_cosi"][T], waits=[], inc=wd)
        em.dma("gpsimd", A["ropet"][:, 3, :], g["rope_sini"][T], waits=[], inc=wd)
        ld_t = em.led[wd]; wd_t = em.led[wd]
        for cc in range(8):
            bank = PS[4 + (cc % 2)]
            em.step("tensor", lambda e, cc=cc, bank=bank: e.matmul(
                bank[0:128, 0:128], A["h1_bf"][:, cc * 128:(cc + 1) * 128],
                A["identbf"][:], start=True, stop=True),
                waits=[(wd, wd_t), (vcs, em.led[vcs]), (scs, em.led[scs])], inc=tcs)
            em.step("vector", lambda e, cc=cc, bank=bank: e.tensor_copy(
                A["h1T"][:, cc, :], bank[0:128, 0:128]),
                waits=[(tcs, em.led[tcs])], inc=vcs)
        def qkvmm(e):
            last = None
            for cc in range(8):
                last = e.matmul(PS[0][0:128, 0:512], A["h1T"][:, cc, :],
                                A["wqkv"][:, cc, 0:512], start=(cc == 0),
                                stop=(cc == 7))
                last = e.matmul(PS[1][0:128, 0:512], A["h1T"][:, cc, :],
                                A["wqkv"][:, cc, 512:1024], start=(cc == 0),
                                stop=(cc == 7))
                last = e.matmul(PS[2][0:128, 0:512], A["h1T"][:, cc, :],
                                A["wqkv"][:, cc, 1024:1536], start=(cc == 0),
                                stop=(cc == 7))
            return last
        em.step("tensor", qkvmm, waits=[(vcs, em.led[vcs]), (scs, em.led[scs])],
                inc=tcs)
        em.step("vector", lambda e, T=T: e.tensor_copy(
            A["v_tm"][:, T - kvlo, :], PS[2][0:128, 0:512]),
            waits=[(tcs, em.led[tcs])], inc=vcs)
        em.step("vector", lambda e: e.tensor_copy(A["qk_32"][:, 0, :],
                                                  PS[0][0:128, 0:512]),
                waits=[], inc=vcs)
        em.step("vector", lambda e: e.tensor_copy(A["qk_32"][:, 1, :],
                                                  PS[1][0:128, 0:512]),
                waits=[], inc=vcs)
        def ssq(e):
            e.tensor_tensor(A["sq_l1"][:, 0, :], A["qk_32"][:, 0, :],
                            A["qk_32"][:, 0, :], op=ALU.mult)
            return e.tensor_tensor(A["sq_l1"][:, 1, :], A["qk_32"][:, 1, :],
                                   A["qk_32"][:, 1, :], op=ALU.mult)
        em.step("vector", ssq, waits=[(vcs, em.led[vcs])], inc=vcs)
        em.step("vector", lambda e: e.reduce_sum(
            A["ss8"][:], A["sq_l1"][:].rearrange("p a (h c) -> p (a h) c", c=64),
            axis=AX.X),
            waits=[(vcs, em.led[vcs])], inc=vcs)
        em.step("scalar", lambda e: e.activation(A["nr8"][:], A["ss8"][:], AF.Sqrt),
                waits=[(vcs, em.led[vcs])], inc=scs)
        em.step("vector", lambda e: e.tensor_scalar(
            A["nr8"][:], A["nr8"][:], 1e-12, None, op0=ALU.max),
            waits=[(scs, em.led[scs])], inc=vcs)
        em.step("scalar", lambda e: None, waits=[(vcs, em.led[vcs])], inc=scs)
        em.step("vector", lambda e: e.reciprocal(A["r8"][:], A["nr8"][:]),
                waits=[(scs, em.led[scs])], inc=vcs)
        em.step("scalar", lambda e: None, waits=[(vcs, em.led[vcs])], inc=scs)
        def qhat(e):
            e.tensor_tensor(A["qk_32"][:, 0, :].rearrange("p (h c) -> p h c", c=64),
                            A["qk_32"][:, 0, :].rearrange("p (h c) -> p h c", c=64),
                            A["r8"][:, 0:8].rearrange("p (h c) -> p h c", c=1)
                            .broadcast_to([128, 8, 64]), op=ALU.mult)
            return e.tensor_tensor(
                A["qk_32"][:, 1, :].rearrange("p (h c) -> p h c", c=64),
                A["qk_32"][:, 1, :].rearrange("p (h c) -> p h c", c=64),
                A["r8"][:, 8:16].rearrange("p (h c) -> p h c", c=1)
                .broadcast_to([128, 8, 64]), op=ALU.mult)
        em.step("vector", qhat, waits=[(scs, em.led[scs]), (vcs, em.led[vcs])], inc=vcs)
        def qsmul(e):
            e.tensor_tensor(A["qk_32"][:, 0, :].rearrange("p (h c) -> p h c", c=64),
                            A["qk_32"][:, 0, :].rearrange("p (h c) -> p h c", c=64),
                            A["qsb"][:, 0:1, :].broadcast_to([128, 8, 64]), op=ALU.mult)
            return e.tensor_tensor(
                A["qk_32"][:, 1, :].rearrange("p (h c) -> p h c", c=64),
                A["qk_32"][:, 1, :].rearrange("p (h c) -> p h c", c=64),
                A["ksb"][:, 0:1, :].broadcast_to([128, 8, 64]), op=ALU.mult)
        em.step("vector", qsmul, waits=[(vcs, em.led[vcs])], inc=vcs)
        for srcq, ct, st_, dst in ((0, 0, 1, 0), (1, 2, 3, 1)):
            def rope1(e, srcq=srcq, ct=ct, st_=st_, dst=dst, ld_t=ld_t):
                qv = A["qk_32"][:, srcq, :].rearrange("p (h c) -> p h c", c=64)
                e.tensor_tensor(A["ra"][:].rearrange("p (h c) -> p h c", c=64),
                                qv, A["ropet"][:, ct:ct + 1, :]
                                .broadcast_to([128, 8, 64]), op=ALU.mult)
                e.tensor_tensor(
                    A["rb"][:].rearrange("p (h c) -> p h c", c=64)[:, :, 0:32],
                    qv[:, :, 32:64],
                    A["ropet"][:, st_:st_ + 1, 0:32].broadcast_to([128, 8, 32]),
                    op=ALU.mult)
                return e.tensor_tensor(
                    A["rb"][:].rearrange("p (h c) -> p h c", c=64)[:, :, 32:64],
                    qv[:, :, 0:32],
                    A["ropet"][:, st_:st_ + 1, 32:64].broadcast_to([128, 8, 32]),
                    op=ALU.mult)
            em.step("vector", rope1, waits=[(wd, ld_t), (vcs, em.led[vcs])], inc=vcs)
            em.step("vector", lambda e, dst=dst: e.tensor_tensor(
                A["qf"][:, dst, :], A["ra"][:], A["rb"][:], op=ALU.add),
                waits=[(vcs, em.led[vcs])], inc=vcs)
        for cc in range(4):
            bank = PS[4 + (cc % 2)]
            em.step("tensor", lambda e, cc=cc, bank=bank: e.matmul(
                bank[0:128, 0:128], A["qf"][:, 1, cc * 128:(cc + 1) * 128],
                A["identbf"][:], start=True, stop=True),
                waits=[(vcs, em.led[vcs]), (scs, em.led[scs])], inc=tcs)
            em.step("vector", lambda e, cc=cc, bank=bank, T=T: e.tensor_copy(
                A["kT"][:, cc, (T - kvlo) * 128:(T - kvlo + 1) * 128],
                bank[0:128, 0:128]),
                waits=[(tcs, em.led[tcs])], inc=vcs)
        if T >= qlo:
            for cc in range(4):
                bank = PS[4 + (cc % 2)]
                em.step("tensor", lambda e, cc=cc, bank=bank: e.matmul(
                    bank[0:128, 0:128], A["qf"][:, 0, cc * 128:(cc + 1) * 128],
                    A["identbf"][:], start=True, stop=True),
                    waits=[(vcs, em.led[vcs]), (scs, em.led[scs])], inc=tcs)
                em.step("vector", lambda e, cc=cc, bank=bank, T=T: e.tensor_copy(
                    A["qT"][:, cc, (T - qlo) * 128:(T - qlo + 1) * 128],
                    bank[0:128, 0:128]),
                    waits=[(tcs, em.led[tcs])], inc=vcs)

    sbuf("mskf", (128, 1024), F32)
    sbuf("sc_l1", (128, 1024), F32)
    sbuf("negmax1", (128, 1), F32)
    sbuf("den1", (128, 1), F32)
    sbuf("rec1", (128, 1), F32)
    sbuf("diag1", (128, 128), BF16)
    sbuf("p_l1", (128, 1024), BF16)
    sbuf("pT_l1", (128, 8, 128), BF16)
    sbuf("o_fm", (64, 8, 128), BF16)
    sbuf("wo_sb", (64, 8, D), BF16)
    sbuf("xr_t", (128, D), F32)
    sbuf("x2_t", (128, D), F32)
    em.dma("gpsimd", A["wo_sb"][:],
           g["L1_ffn_wo"][l].rearrange("(h p) f -> p h f", p=64), inc=wd)
    a0_mod = (2048 - HALO) % 512
    for T in range(qlo, NT_OUT):
        em.dma("sync", A["mskf"][:], g[mask_name][T - qlo],
               waits=[(vcs, em.led[vcs])], inc=ld)
        ld_t = em.led[ld]
        absmod = (a0_mod + 128 * T) % 512
        kst = max(T - (absmod // 128) - 4, kvlo)
        kb = kst - kvlo
        assert 0 <= kb <= (NT_OUT - kvlo) - 8, (T, kst, kvlo)
        for h in range(8):
            po = (h % 2) * 64
            sl = h // 2
            def smm(e, h=h, po=po, sl=sl, T=T, kb=kb):
                e.matmul(PS[0][0:128, 0:512], A["qT"][po:po + 64, sl,
                         (T - qlo) * 128:(T - qlo + 1) * 128],
                         A["kT"][po:po + 64, sl, kb * 128:(kb + 4) * 128],
                         start=True, stop=True)
                return e.matmul(PS[1][0:128, 0:512],
                                A["qT"][po:po + 64, sl,
                                        (T - qlo) * 128:(T - qlo + 1) * 128],
                                A["kT"][po:po + 64, sl, (kb + 4) * 128:(kb + 8) * 128],
                                start=True, stop=True)
            em.step("tensor", smm, waits=[(vcs, em.led[vcs]), (scs, em.led[scs])],
                    inc=tcs)
            def smask(e):
                e.scalar_tensor_tensor(A["sc_l1"][:, 0:512], PS[0][0:128, 0:512],
                                       8.0, A["mskf"][:, 0:512],
                                       op0=ALU.mult, op1=ALU.add)
                return e.scalar_tensor_tensor(A["sc_l1"][:, 512:1024],
                                              PS[1][0:128, 0:512], 8.0,
                                              A["mskf"][:, 512:1024],
                                              op0=ALU.mult, op1=ALU.add)
            em.step("vector", smask, waits=[(tcs, em.led[tcs]), (ld, ld_t)], inc=vcs)
            em.step("vector", lambda e: e.reduce_max(
                A["negmax1"][:], A["sc_l1"][:], axis=AX.X, negate=True),
                waits=[(vcs, em.led[vcs])], inc=vcs)
            em.step("scalar", lambda e: e.activation(
                A["p_l1"][:], A["sc_l1"][:], AF.Exp, bias=A["negmax1"][:, 0:1],
                accum_out=A["den1"][:]),
                waits=[(vcs, em.led[vcs])], inc=scs)
            em.step("vector", lambda e: e.reciprocal(A["rec1"][:], A["den1"][:]),
                    waits=[(scs, em.led[scs])], inc=vcs)
            em.step("scalar", lambda e: None, waits=[(vcs, em.led[vcs])], inc=scs)
            em.step("vector", lambda e: e.tensor_scalar(
                A["diag1"][:], A["iota_row"][:], A["pidx"][:, 0:1],
                A["rec1"][:, 0:1], op0=ALU.is_equal, op1=ALU.mult),
                waits=[(scs, em.led[scs])], inc=vcs)
            for half in range(2):
                def ptmm(e, half=half):
                    last = None
                    for kc in range(4):
                        last = e.matmul(PS[2 + kc][0:128, 0:128],
                                        A["p_l1"][:, half * 512 + kc * 128:
                                                  half * 512 + (kc + 1) * 128],
                                        A["diag1"][:], start=True, stop=True)
                    return last
                em.step("tensor", ptmm, waits=[(vcs, em.led[vcs]),
                                               (scs, em.led[scs])], inc=tcs)
                def ptev(e, half=half):
                    last = None
                    for kc in range(4):
                        last = e.tensor_copy(A["pT_l1"][:, half * 4 + kc, :],
                                             PS[2 + kc][0:128, 0:128])
                    return last
                em.step("vector", ptev, waits=[(tcs, em.led[tcs])], inc=vcs)
            def avmm(e, h=h, kb=kb):
                last = None
                for kc in range(8):
                    last = e.matmul(PS[0][0:64, 0:128],
                                    A["v_tm"][:, kb + kc, h * 64:(h + 1) * 64],
                                    A["pT_l1"][:, kc, :],
                                    start=(kc == 0), stop=(kc == 7))
                return last
            em.step("tensor", avmm, waits=[(vcs, em.led[vcs])], inc=tcs)
            em.step("vector", lambda e, h=h: e.tensor_copy(
                A["o_fm"][:, h, :], PS[0][0:64, 0:128]),
                waits=[(tcs, em.led[tcs])], inc=vcs)
        em.dma("sync", A["xr_t"][:], src_hbm[T],
               waits=[(vcs, em.led[vcs]), (scs, em.led[scs])], inc=ld)
        ld_x = em.led[ld]
        def owmm(e):
            last = None
            for h in range(8):
                last = e.matmul(PS[1][0:128, 0:512], A["o_fm"][:, h, :],
                                A["wo_sb"][:, h, 0:512], start=(h == 0),
                                stop=(h == 7))
                last = e.matmul(PS[2][0:128, 0:512], A["o_fm"][:, h, :],
                                A["wo_sb"][:, h, 512:1024], start=(h == 0),
                                stop=(h == 7))
            return last
        em.step("tensor", owmm, waits=[(vcs, em.led[vcs]), (scs, em.led[scs])],
                inc=tcs)
        def x2ev(e):
            e.tensor_tensor(A["x2_t"][:, 0:512], PS[1][0:128, 0:512],
                            A["xr_t"][:, 0:512], op=ALU.add)
            return e.tensor_tensor(A["x2_t"][:, 512:1024], PS[2][0:128, 0:512],
                                   A["xr_t"][:, 512:1024], op=ALU.add)
        em.step("vector", x2ev, waits=[(tcs, em.led[tcs]), (ld, ld_x),
                                       (sts, em.led[sts])], inc=vcs)
        em.dma("sync", g["x2fA_hbm" if l == 0 else "x2fB_hbm"][T], A["x2_t"][:],
               waits=[(vcs, em.led[vcs])], inc=sts)
    em.barrier()
    pa.close()

    pa = PhaseAlloc(nc, A)
    sbuf = pa.sbuf
    x2h = g["x2fA_hbm" if l == 0 else "x2fB_hbm"]
    x2rows = x2h.rearrange("t p d -> (t p) d")
    sbuf("w1", (128, 8, 5460), BF16)
    sbuf("w2", (128, 22, D), BF16)
    sbuf("b1row", (1, 5460), BF16)
    sbuf("b2row", (1, D), BF16)
    sbuf("gbc", (128, D), F32)
    sbuf("bbc", (128, D), F32)
    sbuf("h2_32", (128, D), F32)
    sbuf("h2n", (128, D), BF16)
    sbuf("h2nT", (128, 8, 128), BF16)
    sbuf("sq_scratch", (128, D), F32)
    alloc_stats(pa, "sf")
    sbuf("ag", (128, 5504), BF16)
    sbuf("i2T", (128, 22, 128), BF16)
    sbuf("xr2", (128, D), F32)
    sbuf("y2", (128, D), F32)
    sbuf("g1t", (128, D), BF16)
    em.dma("gpsimd", A["w1"][:],
           g["L1_ffn_w1"][l].rearrange("(c p) f -> p c f", p=128), inc=wd)
    em.dma("gpsimd", A["w2"][:],
           g["L1_ffn_w2_pad"][l].rearrange("(c p) f -> p c f", p=128), inc=wd)
    em.dma("gpsimd", A["b1row"][:], g["L1_ffn_b1"][l].rearrange("(o d) -> o d", o=1),
           inc=wd)
    em.dma("gpsimd", A["b2row"][:], g["L1_ffn_b2"][l].rearrange("(o d) -> o d", o=1),
           inc=wd)
    em.dma("sync", A["gbc"][:], g["l1f_gbc"][l], inc=ld)
    em.dma("sync", A["bbc"][:], g["l1f_bbc"][l], inc=ld)
    em.step("gpsimd", lambda e: e.memset(A["i2T"][:, 21, :], 0.0), inc=wd)
    wd1 = em.led[wd]; ld1 = em.led[ld]
    for T in range(qlo, NT_OUT):
        em.dma("sync", A["h2_32"][:, 0:512], x2rows[T * 128:(T + 1) * 128, 0:512],
               waits=[(vcs, em.led[vcs]), (scs, em.led[scs])], inc=ld)
        em.dma("sync", A["h2_32"][:, 512:1024],
               x2rows[T * 128 - 1:(T + 1) * 128 - 1, 512:1024],
               waits=[], inc=ld)
        emit_stats(em, A, A["h2_32"][:], "sf", "eps_1em5", "sq_scratch",
                   extra_waits=[(ld, em.led[ld]), (vcs, em.led[vcs])])
        em.step("scalar", lambda e: e.activation(
            A["sq_scratch"][:], A["h2_32"][:], AF.Identity,
            bias=A["sf_negmr"][:, 0:1], scale=A["sf_r"][:, 0:1]),
            waits=[(vcs, em.led[vcs])], inc=scs)
        em.step("vector", lambda e: e.tensor_tensor(
            A["sq_scratch"][:], A["sq_scratch"][:], A["gbc"][:], op=ALU.mult),
            waits=[(scs, em.led[scs]), (ld, ld1), (vcs, em.led[vcs])], inc=vcs)
        em.step("vector", lambda e: e.tensor_tensor(
            A["h2n"][:], A["sq_scratch"][:], A["bbc"][:], op=ALU.add),
            waits=[(vcs, em.led[vcs])], inc=vcs)
        for cc in range(8):
            bank = PS[4 + (cc % 2)]
            em.step("tensor", lambda e, cc=cc, bank=bank: e.matmul(
                bank[0:128, 0:128], A["h2n"][:, cc * 128:(cc + 1) * 128],
                A["identbf"][:], start=True, stop=True),
                waits=[(vcs, em.led[vcs]), (scs, em.led[scs]), (wd, wd1)], inc=tcs)
            em.step("vector", lambda e, cc=cc, bank=bank: e.tensor_copy(
                A["h2nT"][:, cc, :], bank[0:128, 0:128]),
                waits=[(tcs, em.led[tcs])], inc=vcs)
        for jg in range(11):
            j0 = jg * 512
            jn = min(512, 5460 - j0)
            def ffmm(e, j0=j0, jn=jn):
                last = e.matmul(PS[0][0:128, 0:jn], A["ones_bf"][0:1, :],
                                A["b1row"][0:1, j0:j0 + jn], start=True, stop=False)
                for cc in range(8):
                    last = e.matmul(PS[0][0:128, 0:jn], A["h2nT"][:, cc, :],
                                    A["w1"][:, cc, j0:j0 + jn],
                                    start=False, stop=(cc == 7))
                return last
            em.step("tensor", ffmm, waits=[(vcs, em.led[vcs]), (scs, em.led[scs])],
                    inc=tcs)
            def ffev(e, j0=j0, jn=jn):
                last = None
                if j0 < 2730:
                    an = min(jn, 2730 - j0)
                    last = e.copy(A["ag"][:, j0:j0 + an], PS[0][0:128, 0:an])
                    if an < jn:
                        last = e.activation(A["ag"][:, 2752:2752 + (jn - an)],
                                            PS[0][0:128, an:jn], AF.Gelu)
                else:
                    go = j0 - 2730
                    last = e.activation(A["ag"][:, 2752 + go:2752 + go + jn],
                                        PS[0][0:128, 0:jn], AF.Gelu)
                return last
            em.step("scalar", ffev, waits=[(tcs, em.led[tcs])], inc=scs)
        em.step("vector", lambda e: e.tensor_tensor(
            A["ag"][:, 0:2730], A["ag"][:, 0:2730], A["ag"][:, 2752:5482],
            op=ALU.mult),
            waits=[(scs, em.led[scs]), (vcs, em.led[vcs])], inc=vcs)
        for cc in range(22):
            w = 128 if cc < 21 else 42
            bank = PS[4 + (cc % 2)]
            em.step("tensor", lambda e, cc=cc, w=w, bank=bank: e.matmul(
                bank[0:w, 0:128], A["ag"][:, cc * 128:cc * 128 + w],
                A["identbf"][:], start=True, stop=True),
                waits=[(vcs, em.led[vcs]), (scs, em.led[scs])], inc=tcs)
            em.step("vector", lambda e, cc=cc, w=w, bank=bank: e.tensor_copy(
                A["i2T"][0:w, cc, :], bank[0:w, 0:128]),
                waits=[(tcs, em.led[tcs])], inc=vcs)
        em.dma("sync", A["xr2"][:], x2h[T],
               waits=[(vcs, em.led[vcs]), (scs, em.led[scs])], inc=ld)
        ld_x = em.led[ld]
        def w2mm(e):
            last = e.matmul(PS[1][0:128, 0:512], A["ones_bf"][0:1, :],
                            A["b2row"][0:1, 0:512], start=True, stop=False)
            last = e.matmul(PS[2][0:128, 0:512], A["ones_bf"][0:1, :],
                            A["b2row"][0:1, 512:1024], start=True, stop=False)
            for cc in range(22):
                last = e.matmul(PS[1][0:128, 0:512], A["i2T"][:, cc, :],
                                A["w2"][:, cc, 0:512], start=False, stop=(cc == 21))
                last = e.matmul(PS[2][0:128, 0:512], A["i2T"][:, cc, :],
                                A["w2"][:, cc, 512:1024], start=False, stop=(cc == 21))
            return last
        em.step("tensor", w2mm, waits=[(vcs, em.led[vcs]), (scs, em.led[scs])],
                inc=tcs)
        def yev(e):
            e.tensor_tensor(A["y2"][:, 0:512], PS[1][0:128, 0:512],
                            A["xr2"][:, 0:512], op=ALU.add)
            return e.tensor_tensor(A["y2"][:, 512:1024], PS[2][0:128, 0:512],
                                   A["xr2"][:, 512:1024], op=ALU.add)
        em.step("vector", yev, waits=[(tcs, em.led[tcs]), (ld, ld_x),
                                      (sts, em.led[sts])], inc=vcs)
        if store_g1:
            em.step("scalar", lambda e: e.activation(A["g1t"][:], A["y2"][:],
                                                     AF.Silu),
                    waits=[(vcs, em.led[vcs]), (sts, em.led[sts])], inc=scs)
            em.dma("sync", g["g1_hbm"][T], A["g1t"][:],
                   waits=[(scs, em.led[scs])], inc=sts)
        else:
            em.dma("sync", dst_hbm[T], A["y2"][:],
                   waits=[(vcs, em.led[vcs])], inc=sts)
    em.barrier()
    pa.close()


def _ph_l1a(nc, em, g, A, PS):
    A = dict(A)
    ld, wd, tcs, vcs, scs, sts = "ld", "wd", "tc", "vc", "sc", "st"
    NKV = NT_OUT - L1A_KV_LO
    NQ = NT_OUT - L1A_Q_LO

    pa = PhaseAlloc(nc, A)
    sbuf = pa.sbuf
    sbuf("xt", (128, D), F32)
    sbuf("q_fm", (128, 8, 16 * 128), BF16)
    sbuf("k_fm", (128, 8, 17 * 128), BF16)
    sbuf("v_tm", (128, 17, D), BF16)
    pa_w = PhaseAlloc(nc, A)
    sbufw = pa_w.sbuf
    sbufw("wqkv", (128, 8, 3072), BF16)
    sbufw("g1bc", (128, D), F32); sbufw("b1bc", (128, D), F32)
    sbufw("qb_col", (128, 8), F32)
    sbufw("kv_rows", (1, 2048), BF16)
    sbufw("sq_scratch", (128, D), F32)
    alloc_stats(pa_w, "sa")
    sbufw("h1f", (128, D), BF16)
    sbufw("h1T", (128, 8, 128), BF16)
    sbufw("ktv_bf", (128, D), BF16)
    sbufw("pad_t", (128, 1), F32)
    em.dma("gpsimd", A["wqkv"][:],
           g["L1_attn_w_qkv"].rearrange("(c p) f -> p c f", p=128), inc=wd)
    em.dma("sync", A["g1bc"][:], g["l1a_g1bc"], inc=ld)
    em.dma("sync", A["b1bc"][:], g["l1a_b1bc"], inc=ld)
    em.dma("sync", A["qb_col"][:], g["L1_attn_b_qkv"][0:1024]
           .rearrange("(s p) -> p s", p=128), inc=ld, slow=True)
    em.dma("gpsimd", A["kv_rows"][0:1, 0:1024],
           g["L1_attn_b_qkv"][1024:2048].rearrange("(o d) -> o d", o=1), inc=wd)
    em.dma("gpsimd", A["kv_rows"][0:1, 1024:2048],
           g["L1_attn_b_qkv"][2048:3072].rearrange("(o d) -> o d", o=1), inc=wd)
    wd0 = em.led[wd]; ld0 = em.led[ld]
    for T in range(L1A_KV_LO, NT_OUT):
        em.dma("sync", A["xt"][:], g["xn1_hbm"][T],
               waits=[(vcs, em.led[vcs]), (scs, em.led[scs])], inc=ld)
        em.dma("sync", A["pad_t"][:], g["padmask"][T + 1],
               waits=[], inc=ld)
        emit_stats(em, A, A["xt"][:], "sa", "eps_1em5", "sq_scratch",
                   extra_waits=[(ld, em.led[ld])])
        em.step("scalar", lambda e: e.activation(
            A["sq_scratch"][:], A["xt"][:], AF.Identity,
            bias=A["sa_negmr"][:, 0:1], scale=A["sa_r"][:, 0:1]),
            waits=[(vcs, em.led[vcs])], inc=scs)
        em.step("vector", lambda e: e.tensor_tensor(
            A["sq_scratch"][:], A["sq_scratch"][:], A["g1bc"][:], op=ALU.mult),
            waits=[(scs, em.led[scs]), (ld, ld0), (vcs, em.led[vcs])], inc=vcs)
        em.step("vector", lambda e: e.tensor_tensor(
            A["h1f"][:], A["sq_scratch"][:], A["b1bc"][:], op=ALU.add),
            waits=[(vcs, em.led[vcs])], inc=vcs)
        for cc in range(8):
            bank = PS[4 + (cc % 2)]
            em.step("tensor", lambda e, cc=cc, bank=bank: e.matmul(
                bank[0:128, 0:128], A["h1f"][:, cc * 128:(cc + 1) * 128],
                A["identbf"][:], start=True, stop=True),
                waits=[(vcs, em.led[vcs]), (scs, em.led[scs]), (wd, wd0)], inc=tcs)
            em.step("vector", lambda e, cc=cc, bank=bank: e.tensor_copy(
                A["h1T"][:, cc, :], bank[0:128, 0:128]),
                waits=[(tcs, em.led[tcs])], inc=vcs)
        for sub in range(8):
            def qmm(e, sub=sub):
                last = None
                for cc in range(8):
                    last = e.matmul(PS[0][0:128, 0:128],
                                    A["wqkv"][:, cc, sub * 128:(sub + 1) * 128],
                                    A["h1T"][:, cc, :], start=(cc == 0), stop=(cc == 7))
                return last
            em.step("tensor", qmm, waits=[(vcs, em.led[vcs]), (scs, em.led[scs])],
                    inc=tcs)
            em.step("scalar", lambda e, sub=sub, T=T: e.activation(
                A["q_fm"][:, sub, (T - L1A_Q_LO) * 128:(T - L1A_Q_LO + 1) * 128]
                if T >= L1A_Q_LO else A["ktv_bf"][:, 0:128],
                PS[0][0:128, 0:128], AF.Identity, bias=A["qb_col"][:, sub:sub + 1]),
                waits=[(tcs, em.led[tcs])], inc=scs)
        def kmm(e):
            last = e.matmul(PS[1][0:128, 0:512], A["ones_bf"][0:1, :],
                            A["kv_rows"][0:1, 0:512], start=True, stop=False)
            last = e.matmul(PS[2][0:128, 0:512], A["ones_bf"][0:1, :],
                            A["kv_rows"][0:1, 512:1024], start=True, stop=False)
            for cc in range(8):
                last = e.matmul(PS[1][0:128, 0:512], A["h1T"][:, cc, :],
                                A["wqkv"][:, cc, 1024:1536], start=False,
                                stop=(cc == 7))
                last = e.matmul(PS[2][0:128, 0:512], A["h1T"][:, cc, :],
                                A["wqkv"][:, cc, 1536:2048], start=False,
                                stop=(cc == 7))
            return last
        em.step("tensor", kmm, waits=[(vcs, em.led[vcs]), (scs, em.led[scs])], inc=tcs)
        def kev(e):
            e.tensor_scalar(A["ktv_bf"][:, 0:512], PS[1][0:128, 0:512],
                            A["pad_t"][:, 0:1], None, op0=ALU.mult)
            return e.tensor_scalar(A["ktv_bf"][:, 512:1024], PS[2][0:128, 0:512],
                                   A["pad_t"][:, 0:1], None, op0=ALU.mult)
        em.step("vector", kev, waits=[(tcs, em.led[tcs])], inc=vcs)
        for cc in range(8):
            bank = PS[4 + (cc % 2)]
            em.step("tensor", lambda e, cc=cc, bank=bank: e.matmul(
                bank[0:128, 0:128], A["ktv_bf"][:, cc * 128:(cc + 1) * 128],
                A["identbf"][:], start=True, stop=True),
                waits=[(vcs, em.led[vcs]), (scs, em.led[scs])], inc=tcs)
            em.step("vector", lambda e, cc=cc, bank=bank, T=T: e.tensor_copy(
                A["k_fm"][:, cc, (T - L1A_KV_LO) * 128:(T - L1A_KV_LO + 1) * 128],
                bank[0:128, 0:128]),
                waits=[(tcs, em.led[tcs])], inc=vcs)
        def vmm(e):
            last = e.matmul(PS[1][0:128, 0:512], A["ones_bf"][0:1, :],
                            A["kv_rows"][0:1, 1024:1536], start=True, stop=False)
            last = e.matmul(PS[2][0:128, 0:512], A["ones_bf"][0:1, :],
                            A["kv_rows"][0:1, 1536:2048], start=True, stop=False)
            for cc in range(8):
                last = e.matmul(PS[1][0:128, 0:512], A["h1T"][:, cc, :],
                                A["wqkv"][:, cc, 2048:2560], start=False,
                                stop=(cc == 7))
                last = e.matmul(PS[2][0:128, 0:512], A["h1T"][:, cc, :],
                                A["wqkv"][:, cc, 2560:3072], start=False,
                                stop=(cc == 7))
            return last
        em.step("tensor", vmm, waits=[(vcs, em.led[vcs]), (scs, em.led[scs])], inc=tcs)
        def vev(e, T=T):
            e.tensor_scalar(A["v_tm"][:, T - L1A_KV_LO, 0:512], PS[1][0:128, 0:512],
                            A["pad_t"][:, 0:1], None, op0=ALU.mult)
            return e.tensor_scalar(A["v_tm"][:, T - L1A_KV_LO, 512:1024],
                                   PS[2][0:128, 0:512], A["pad_t"][:, 0:1], None,
                                   op0=ALU.mult)
        em.step("vector", vev, waits=[(tcs, em.led[tcs])], inc=vcs)

    em.barrier()
    pa_w.close()
    sbuf("mska", (64, 128), F32)
    sbuf("wout", (64, 16, D), BF16)
    sbuf("bout_row", (1, D), BF16)
    sbuf("v_sh", (128, D), BF16)
    sbuf("sca", (64, 4, 128), F32)
    sbuf("negmaxa", (64, 4), F32)
    sbuf("dena", (64, 4), F32)
    sbuf("reca", (64, 4), F32)
    sbuf("diaga", (64, 4, 64), BF16)
    sbuf("pa4", (64, 4, 128), BF16)
    sbuf("pTa", (128, 4, 64), BF16)
    sbuf("ob_fm", (64, 16, 128), BF16)
    sbuf("xm_t", (128, D), F32)
    em.dma("sync", A["mska"][:], g["mask_l1a"], inc=ld)
    em.dma("gpsimd", A["wout"][:],
           g["L1_attn_w_out"].rearrange("(h p) f -> p h f", p=64), inc=wd)
    em.dma("gpsimd", A["bout_row"][:],
           g["L1_attn_b_out"].rearrange("(o d) -> o d", o=1), inc=wd)
    ld_b = em.led[ld]; wd_b = em.led[wd]
    for T in range(L1A_Q_LO, NT_OUT):
        kvi = T - L1A_KV_LO
        em.dma("sync", A["v_sh"][0:64, :], A["v_tm"][64:128, kvi - 1, :],
               waits=[(vcs, em.led[vcs]), (tcs, em.led[tcs])], inc=ld)
        em.dma("sync", A["v_sh"][64:128, :], A["v_tm"][0:64, kvi, :],
               waits=[], inc=ld)
        ld_v = em.led[ld]
        for wv in range(8):
            hpair = (wv * 2, wv * 2 + 1)
            def bsmm(e, hpair=hpair, T=T, kvi=kvi):
                last = None
                for i2, h in enumerate(hpair):
                    po = (h % 2) * 64
                    sl = h // 2
                    for b in range(2):
                        qcol = (T - L1A_Q_LO) * 128 + b * 64
                        kcol = kvi * 128 + b * 64 - 64
                        last = e.matmul(
                            PS[i2 * 2 + b][0:64, 0:128],
                            A["q_fm"][po:po + 64, sl, qcol:qcol + 64],
                            A["k_fm"][po:po + 64, sl, kcol:kcol + 128],
                            start=True, stop=True)
                return last
            em.step("tensor", bsmm, waits=[(vcs, em.led[vcs]), (scs, em.led[scs])],
                    inc=tcs)
            def bmask(e):
                last = None
                for u in range(4):
                    last = e.scalar_tensor_tensor(
                        A["sca"][:, u, :], PS[u][0:64, 0:128], 0.125,
                        A["mska"][:], op0=ALU.mult, op1=ALU.add)
                return last
            em.step("vector", bmask, waits=[(tcs, em.led[tcs]), (ld, ld_b)], inc=vcs)
            em.step("vector", lambda e: e.reduce_max(
                A["negmaxa"][:], A["sca"][:], axis=AX.X, negate=True),
                waits=[(vcs, em.led[vcs])], inc=vcs)
            def bexp(e):
                last = None
                for u in range(4):
                    last = e.activation(A["pa4"][:, u, :], A["sca"][:, u, :],
                                        AF.Exp, bias=A["negmaxa"][:, u:u + 1],
                                        accum_out=A["dena"][:, u:u + 1])
                return last
            em.step("scalar", bexp, waits=[(vcs, em.led[vcs])], inc=scs)
            em.step("vector", lambda e: e.reciprocal(A["reca"][:], A["dena"][:]),
                    waits=[(scs, em.led[scs])], inc=vcs)
            em.step("scalar", lambda e: None, waits=[(vcs, em.led[vcs])], inc=scs)
            def bdiag(e):
                last = None
                for u in range(4):
                    last = e.tensor_scalar(A["diaga"][:, u, :],
                                           A["iota_row"][0:64, 0:64],
                                           A["pidx"][0:64, 0:1],
                                           A["reca"][:, u:u + 1],
                                           op0=ALU.is_equal, op1=ALU.mult)
                return last
            em.step("vector", bdiag, waits=[(scs, em.led[scs])], inc=vcs)
            def bptmm(e):
                last = None
                for u in range(4):
                    last = e.matmul(PS[u][0:128, 0:64], A["pa4"][:, u, :],
                                    A["diaga"][:, u, :], start=True, stop=True)
                return last
            em.step("tensor", bptmm, waits=[(vcs, em.led[vcs]),
                                            (scs, em.led[scs])], inc=tcs)
            def bptev(e):
                last = None
                for u in range(4):
                    last = e.tensor_copy(A["pTa"][:, u, :], PS[u][0:128, 0:64])
                return last
            em.step("vector", bptev, waits=[(tcs, em.led[tcs])], inc=vcs)
            def bavmm(e, hpair=hpair, kvi=kvi):
                last = None
                for i2, h in enumerate(hpair):
                    for b in range(2):
                        vsl = (A["v_sh"][:, h * 64:(h + 1) * 64] if b == 0
                               else A["v_tm"][:, kvi, h * 64:(h + 1) * 64])
                        last = e.matmul(PS[i2 * 2 + b][0:64, 0:64], vsl,
                                        A["pTa"][:, i2 * 2 + b, :],
                                        start=True, stop=True)
                return last
            em.step("tensor", bavmm, waits=[(vcs, em.led[vcs]), (ld, ld_v)], inc=tcs)
            def bobev(e, hpair=hpair):
                last = None
                for i2, h in enumerate(hpair):
                    for b in range(2):
                        last = e.tensor_copy(A["ob_fm"][:, h, b * 64:(b + 1) * 64],
                                             PS[i2 * 2 + b][0:64, 0:64])
                return last
            em.step("vector", bobev, waits=[(tcs, em.led[tcs])], inc=vcs)
        em.dma("sync", A["xt"][:], g["xn1_hbm"][T],
               waits=[(vcs, em.led[vcs]), (scs, em.led[scs])], inc=ld)
        ld_x = em.led[ld]
        def owmm(e):
            last = e.matmul(PS[1][0:128, 0:512], A["ones_bf"][0:1, :],
                            A["bout_row"][0:1, 0:512], start=True, stop=False)
            last = e.matmul(PS[2][0:128, 0:512], A["ones_bf"][0:1, :],
                            A["bout_row"][0:1, 512:1024], start=True, stop=False)
            for h in range(16):
                last = e.matmul(PS[1][0:128, 0:512], A["ob_fm"][:, h, :],
                                A["wout"][:, h, 0:512], start=False, stop=(h == 15))
                last = e.matmul(PS[2][0:128, 0:512], A["ob_fm"][:, h, :],
                                A["wout"][:, h, 512:1024], start=False, stop=(h == 15))
            return last
        em.step("tensor", owmm, waits=[(vcs, em.led[vcs]), (scs, em.led[scs]),
                                       (wd, wd_b)], inc=tcs)
        def xmev(e):
            e.tensor_tensor(A["xm_t"][:, 0:512], PS[1][0:128, 0:512],
                            A["xt"][:, 0:512], op=ALU.add)
            return e.tensor_tensor(A["xm_t"][:, 512:1024], PS[2][0:128, 0:512],
                                   A["xt"][:, 512:1024], op=ALU.add)
        em.step("vector", xmev, waits=[(tcs, em.led[tcs]), (ld, ld_x),
                                       (sts, em.led[sts])], inc=vcs)
        em.dma("sync", g["x2a_hbm"][T], A["xm_t"][:],
               waits=[(vcs, em.led[vcs])], inc=sts)
    em.barrier()
    pa.close()

    pa = PhaseAlloc(nc, A)
    sbuf = pa.sbuf
    sbuf("w1", (128, 8, 4096), BF16)
    sbuf("w2", (128, 32, D), BF16)
    sbuf("b1col", (128, 32), F32)
    sbuf("b2row", (1, D), BF16)
    sbuf("g2bc", (128, D), F32); sbuf("b2bc", (128, D), F32)
    sbuf("xm4", (128, 2, D), F32)
    sbuf("sq_scratch_c", (128, D), F32)
    alloc_stats(pa, "sm")
    sbuf("h2nT_g", (128, 8, 256), BF16)
    sbuf("m1_fm", (128, 32, 256), BF16)
    sbuf("s1_t", (128, D), F32)
    sbuf("x1r", (128, D), F32)
    sbuf("g1r", (128, D), BF16)
    sbuf("h2nb", (128, D), BF16)
    A["out_t"] = A["sq_scratch_c"]
    em.dma("gpsimd", A["w1"][:],
           g["L1_attn_w1"].rearrange("(c p) f -> p c f", p=128), inc=wd)
    em.dma("gpsimd", A["w2"][:],
           g["L1_attn_w2"].rearrange("(c p) f -> p c f", p=128), inc=wd)
    em.dma("sync", A["b1col"][:], g["L1_attn_b1"].rearrange("(s p) -> p s", p=128),
           inc=ld, slow=True)
    em.dma("gpsimd", A["b2row"][:], g["L1_attn_b2"].rearrange("(o d) -> o d", o=1),
           inc=wd)
    em.dma("sync", A["g2bc"][:], g["l1a_g2bc"], inc=ld)
    em.dma("sync", A["b2bc"][:], g["l1a_b2bc"], inc=ld)
    wd_c = em.led[wd]; ld_c = em.led[ld]
    for gq in range(8):
        T0 = L1A_Q_LO + gq * 2
        for ti in range(2):
            T = T0 + ti
            xm = A["xm4"][:, ti, :]
            em.dma("sync", xm, g["x2a_hbm"][T],
                   waits=[(vcs, em.led[vcs]), (scs, em.led[scs])], inc=ld)
            emit_stats(em, A, xm, "sm", "eps_1em5", "sq_scratch_c",
                       extra_waits=[(ld, em.led[ld]), (vcs, em.led[vcs]),
                                    (sts, em.led[sts])])
            em.step("scalar", lambda e, xm=xm: e.activation(
                A["sq_scratch_c"][:], xm, AF.Identity,
                bias=A["sm_negmr"][:, 0:1], scale=A["sm_r"][:, 0:1]),
                waits=[(vcs, em.led[vcs])], inc=scs)
            em.step("vector", lambda e: e.tensor_tensor(
                A["sq_scratch_c"][:], A["sq_scratch_c"][:], A["g2bc"][:], op=ALU.mult),
                waits=[(scs, em.led[scs]), (ld, ld_c), (vcs, em.led[vcs])], inc=vcs)
            em.step("vector", lambda e: e.tensor_tensor(
                A["h2nb"][:], A["sq_scratch_c"][:], A["b2bc"][:], op=ALU.add),
                waits=[(vcs, em.led[vcs])], inc=vcs)
            for cc in range(8):
                bank = PS[4 + (cc % 2)]
                em.step("tensor", lambda e, cc=cc, bank=bank: e.matmul(
                    bank[0:128, 0:128], A["h2nb"][:, cc * 128:(cc + 1) * 128],
                    A["identbf"][:], start=True, stop=True),
                    waits=[(vcs, em.led[vcs]), (scs, em.led[scs]), (wd, wd_c)],
                    inc=tcs)
                em.step("vector", lambda e, cc=cc, bank=bank, ti=ti: e.tensor_copy(
                    A["h2nT_g"][:, cc, ti * 128:(ti + 1) * 128], bank[0:128, 0:128]),
                    waits=[(tcs, em.led[tcs])], inc=vcs)
        for js in range(32):
            def m1mm(e, js=js):
                last = None
                for cc in range(8):
                    last = e.matmul(PS[0][0:128, 0:256],
                                    A["w1"][:, cc, js * 128:(js + 1) * 128],
                                    A["h2nT_g"][:, cc, :], start=(cc == 0),
                                    stop=(cc == 7))
                return last
            em.step("tensor", m1mm, waits=[(vcs, em.led[vcs]), (scs, em.led[scs])],
                    inc=tcs)
            em.step("scalar", lambda e, js=js: e.activation(
                A["m1_fm"][:, js, :], PS[0][0:128, 0:256], AF.Gelu,
                bias=A["b1col"][:, js:js + 1]),
                waits=[(tcs, em.led[tcs]), (ld, ld_c)], inc=scs)
        for ti in range(2):
            T = T0 + ti
            def w2mm(e, ti=ti):
                last = e.matmul(PS[1][0:128, 0:512], A["ones_bf"][0:1, :],
                                A["b2row"][0:1, 0:512], start=True, stop=False)
                last = e.matmul(PS[2][0:128, 0:512], A["ones_bf"][0:1, :],
                                A["b2row"][0:1, 512:1024], start=True, stop=False)
                for cc in range(32):
                    last = e.matmul(PS[1][0:128, 0:512],
                                    A["m1_fm"][:, cc, ti * 128:(ti + 1) * 128],
                                    A["w2"][:, cc, 0:512], start=False,
                                    stop=(cc == 31))
                    last = e.matmul(PS[2][0:128, 0:512],
                                    A["m1_fm"][:, cc, ti * 128:(ti + 1) * 128],
                                    A["w2"][:, cc, 512:1024], start=False,
                                    stop=(cc == 31))
                return last
            em.step("tensor", w2mm, waits=[(vcs, em.led[vcs]), (scs, em.led[scs])],
                    inc=tcs)
            def s1ev(e, ti=ti):
                e.tensor_tensor(A["s1_t"][:, 0:512], PS[1][0:128, 0:512],
                                A["xm4"][:, ti, 0:512], op=ALU.add)
                return e.tensor_tensor(A["s1_t"][:, 512:1024], PS[2][0:128, 0:512],
                                       A["xm4"][:, ti, 512:1024], op=ALU.add)
            em.step("vector", s1ev, waits=[(tcs, em.led[tcs])], inc=vcs)
            em.dma("sync", A["x1r"][:], g["x1_hbm"][T],
                   waits=[(vcs, em.led[vcs])], inc=ld)
            em.dma("sync", A["g1r"][:], g["g1_hbm"][T], waits=[], inc=ld)
            ld_f = em.led[ld]
            em.step("vector", lambda e: e.tensor_tensor(
                A["s1_t"][:], A["s1_t"][:], A["g1r"][:], op=ALU.mult),
                waits=[(vcs, em.led[vcs]), (ld, ld_f)], inc=vcs)
            em.step("vector", lambda e: e.tensor_tensor(
                A["out_t"][:], A["s1_t"][:], A["x1r"][:], op=ALU.add),
                waits=[(vcs, em.led[vcs]), (sts, em.led[sts])], inc=vcs)
            em.dma("sync", g["out_shard"][(T - L1A_Q_LO) * 128:(T - L1A_Q_LO + 1) * 128, :],
                   A["out_t"][:], waits=[(vcs, em.led[vcs])], inc=sts)
    em.barrier()
    pa.close()


_NC_CACHE = {}

def kernel(**inputs):
    if "nc" not in _NC_CACHE:
        _NC_CACHE["nc"] = build_program()
    nc = _NC_CACHE["nc"]
    in_maps = [shard_inputs(inputs, c) for c in range(8)]
    res = run_bass_kernel_spmd(nc, in_maps, core_ids=list(range(8)))
    out = np.zeros((2, 8192, D), np.float32)
    for c in range(8):
        b, s = divmod(c, 4)
        out[b, s * T_SH:(s + 1) * T_SH] = res.results[c]["out_shard"]
    return out


# version 05
# speedup vs baseline: 1.1609x; 1.1609x over previous
import contextlib
import numpy as np
import ml_dtypes
import concourse.bass as bass
import concourse.mybir as mybir
from concourse.bass_utils import run_bass_kernel_spmd

F32 = mybir.dt.float32
BF16 = mybir.dt.bfloat16
AF = mybir.ActivationFunctionType
ALU = mybir.AluOpType
AX = mybir.AxisListType
BFD = ml_dtypes.bfloat16

D = 1024
NT_OUT = 26
NT_EXT = 27
HALO = 1280
T_SH = 2048
NLAT = 64
GROUPS = [(0, 4), (4, 4), (8, 4), (12, 4), (16, 4), (20, 4), (24, 2)]
L1F_KV_LO = 0
L1F_QA_LO = 4
L1F_KVB_LO = 4
L1F_QB_LO = 9
L1A_KV_LO = 9
L1A_Q_LO = 10

NEG = np.float32(-1e30)


class Em:
    ENGS = ("sync", "tensor", "vector", "scalar", "gpsimd")

    def __init__(self, nc):
        self.nc = nc
        self.streams = {e: [] for e in self.ENGS}
        self.sems = {}
        self.led = {}

    def sem(self, name):
        h = self.nc.alloc_semaphore(name)
        self.sems[name] = h
        self.led[name] = 0
        return h

    def step(self, eng, fn, waits=(), inc=None, inc_by=1):
        for s, v in waits:
            assert v <= self.led[s], f"wait {s}>={v} but only {self.led[s]} emitted"
        waits = tuple(waits)
        sems = self.sems
        if inc is not None:
            self.led[inc] += inc_by
        def closure(e, fn=fn, waits=waits, inc=inc, inc_by=inc_by):
            for s, v in waits:
                if v > 0:
                    e.wait_ge(sems[s], v)
            last = fn(e)
            if inc is not None:
                if last is None:
                    e.nop().then_inc(sems[inc], inc_by)
                else:
                    last.then_inc(sems[inc], inc_by)
        self.streams[eng].append(closure)

    ENG_SEM = {"sync": "ld", "tensor": "tc", "vector": "vc",
               "scalar": "sc", "gpsimd": "wd"}

    def barrier(self):
        snap = dict(self.led)
        for eng in self.ENGS:
            own = self.ENG_SEM[eng]
            waits = [(s, v) for s, v in snap.items() if s != own and v > 0]
            self.step(eng, lambda e: None, waits=waits, inc=own)

    def dma(self, eng, out, in_, waits=(), inc=None, slow=False):
        self.step(eng, lambda e: e.dma_start(out=out, in_=in_,
                                             allow_slow_non_contiguous=slow),
                  waits=waits, inc=inc, inc_by=16)

    def finish(self, block):
        for name in self.ENGS:
            stream = self.streams[name]
            def runner(eng, stream=stream):
                for c in stream:
                    c(eng)
            getattr(block, name)(runner)


def build_masks_for_core(s):
    t0 = s * T_SH
    a0 = t0 - HALO
    out = {}
    ml = np.zeros((NT_OUT, 128, 256), np.float32)
    zl = np.ones((NT_OUT, 128, 1), np.float32)
    mlat = np.zeros((NT_OUT, 128, NLAT), np.float32)
    zlat = np.ones((NT_OUT, 128, 1), np.float32)
    pad = np.ones((NT_EXT, 128, 1), np.float32)
    for T in range(NT_OUT):
        abs0 = a0 + 128 * T
        i = abs0 + np.arange(128)[:, None]
        j = abs0 - 128 + np.arange(256)[None, :]
        ok = (j >= i - 128) & (j <= i - 1) & (j >= 0)
        ml[T] = np.where(ok, 0.0, NEG)
        zl[T, :, 0] = ok.any(-1).astype(np.float32)
        cid = abs0 // 128
        lok = np.arange(NLAT)[None, :] < cid
        mlat[T] = np.where(lok, 0.0, NEG)
        zlat[T, :, 0] = 1.0 if cid > 0 else 0.0
    for T in range(NT_EXT):
        abs0 = a0 - 128 + 128 * T
        pad[T, :, 0] = ((abs0 + np.arange(128)) >= 0).astype(np.float32)
    out["mask_local"] = ml
    out["zl_local"] = zl
    out["mask_lat"] = mlat
    out["zl_lat"] = zlat
    out["padmask"] = pad
    i = np.arange(64)[:, None]
    c = np.arange(128)[None, :]
    out["mask_l1a"] = np.where((c < 64) | (c - 64 <= i), 0.0, NEG).astype(np.float32)
    def build_mf(qlo, kclamp):
        mf = np.zeros((NT_OUT - qlo, 128, 1024), np.float32)
        for tq in range(qlo, NT_OUT):
            abs0 = a0 + 128 * tq
            kst = max(tq - ((abs0 - a0 + 256) % 512) // 128 - 4, kclamp)
            klo = a0 + 128 * kst
            i = abs0 + np.arange(128)[:, None]
            j = klo + np.arange(1024)[None, :]
            ok = (i >= j) & (i <= j + 512) & (j >= 0)
            mf[tq - qlo] = np.where(ok, 0.0, NEG)
        return mf
    out["mask_l1fA"] = build_mf(L1F_QA_LO, L1F_KV_LO)
    out["mask_l1fB"] = build_mf(L1F_QB_LO, L1F_KVB_LO)
    DIM_HEAD, WINDOW, N = 64, 512, 8192
    half = np.arange(0, DIM_HEAD, 2, dtype=np.float64)
    inv = 1.0 / 10000.0 ** (half / DIM_HEAD)
    base = (half + 0.4 * DIM_HEAD) / (1.4 * DIM_HEAD)
    cosT = np.zeros((26, 128, 64), np.float32); sinT = np.zeros_like(cosT)
    cosIT = np.zeros_like(cosT); sinIT = np.zeros_like(cosT)
    for T in range(0, NT_OUT):
        abs0 = a0 + 128 * T
        t = np.clip(abs0 + np.arange(128), 0, None).astype(np.float64)
        fr = t[:, None] * inv[None, :]
        fr2 = np.concatenate([fr, fr], -1)
        power = (t - N // 2) / float(WINDOW // 2)
        sc = base[None, :] ** power[:, None]
        sc2 = np.concatenate([sc, sc], -1)
        cs, sn = np.cos(fr2), np.sin(fr2)
        sgn = np.concatenate([-np.ones(32), np.ones(32)])[None, :]
        cosT[T] = (cs * sc2).astype(np.float32)
        sinT[T] = (sn * sc2 * sgn).astype(np.float32)
        cosIT[T] = (cs / sc2).astype(np.float32)
        sinIT[T] = (sn / sc2 * sgn).astype(np.float32)
    out["rope_cos"] = cosT; out["rope_sin"] = sinT
    out["rope_cosi"] = cosIT; out["rope_sini"] = sinIT
    out["ident32"] = np.eye(128, dtype=np.float32)
    out["identbf"] = np.eye(128, dtype=np.float32)
    out["iota_row"] = np.broadcast_to(np.arange(128, dtype=np.float32), (128, 128)).copy()
    out["pidx"] = np.arange(128, dtype=np.float32).reshape(128, 1)
    out["ones_bf_row"] = np.ones((1, 128), np.float32)
    oh = np.zeros((8, 8, 128), np.float32)
    for e in range(8):
        oh[e, e, :] = 1.0
    out["onehot8"] = oh
    return out


def shard_inputs(inputs, core):
    b, s = divmod(core, 4)
    t0 = s * T_SH
    x = np.asarray(inputs["L0_attn_x"], np.float32)
    ext = np.zeros((NT_EXT * 128, D), np.float32)
    lo = t0 - HALO - 128
    src_lo = max(lo, 0)
    ext[src_lo - lo:src_lo - lo + (t0 + T_SH - src_lo)] = x[b, src_lo:t0 + T_SH]
    im = {"x_ext": ext, "x_full": np.ascontiguousarray(x[b])}
    im.update({k: np.ascontiguousarray(np.asarray(v)) for k, v in inputs.items()
               if k != "L0_attn_x"})
    im["WrT"] = np.ascontiguousarray(np.asarray(inputs["L0_ffn_Wr"]).T)
    W1 = np.asarray(inputs["L0_ffn_W1"], np.float32)
    im["W1_staged"] = np.ascontiguousarray(
        W1.reshape(8, 8, 128, 32, 128).transpose(3, 2, 0, 1, 4))
    Wo = np.asarray(inputs["L0_ffn_Wo"], np.float32)
    im["Wo_staged"] = np.ascontiguousarray(
        Wo.reshape(32, 128, 8, 128))
    im["g0_gamma_bc"] = np.ascontiguousarray(np.broadcast_to(
        np.asarray(inputs["L0_ffn_gamma"], np.float32), (128, D)))
    im["g0_beta_bc"] = np.ascontiguousarray(np.broadcast_to(
        np.asarray(inputs["L0_ffn_beta"], np.float32), (128, D)))
    def bc(v):
        return np.ascontiguousarray(np.broadcast_to(np.asarray(v, np.float32), (128, D)))
    im["l1a_g1bc"] = bc(inputs["L1_attn_ln1_g"]); im["l1a_b1bc"] = bc(inputs["L1_attn_ln1_b"])
    im["l1a_g2bc"] = bc(inputs["L1_attn_ln2_g"]); im["l1a_b2bc"] = bc(inputs["L1_attn_ln2_b"])
    lng = np.asarray(inputs["L1_ffn_ln_g"], np.float32)
    lnb = np.asarray(inputs["L1_ffn_ln_b"], np.float32)
    im["l1f_gbc"] = np.ascontiguousarray(
        np.broadcast_to(lng[:, None, :], (2, 128, D)))
    im["l1f_bbc"] = np.ascontiguousarray(
        np.broadcast_to(lnb[:, None, :], (2, 128, D)))
    w2 = np.asarray(inputs["L1_ffn_w2"], np.float32)
    w2p = np.zeros((2, 2816, D), np.float32)
    w2p[:, :2730] = w2
    im["L1_ffn_w2_pad"] = w2p
    im.update(build_masks_for_core(s))
    return im


DEBUG_OUT = set()
DEV_STAGE = 99
DEV_GROUPS = 99
DEV_PHASES = 99
DEV_ATTN = 99
DEV_SMM_HH = (0, 1, 2, 3)

def build_program():
    nc = bass.Bass("TRN2", target_bir_lowering=False)
    g = {}

    def din(name, shape, dtype=F32):
        g[name] = nc.dram_tensor(name, list(shape), dtype, kind="ExternalInput").ap()
        return g[name]

    def dint(name, shape, dtype=F32):
        kind = "ExternalOutput" if name in DEBUG_OUT else "Internal"
        g[name] = nc.dram_tensor(name, list(shape), dtype, kind=kind).ap()
        return g[name]

    din("x_ext", (NT_EXT * 128, D)); din("x_full", (8192, D))
    din("L0_attn_qkv_w", (D, 3 * D)); din("L0_attn_qkv_b", (3 * D,))
    din("L0_attn_rq_w", (D, 256)); din("L0_attn_rq_b", (256,))
    din("L0_attn_rkv_w", (D, 512)); din("L0_attn_rkv_b", (512,))
    din("L0_attn_rout_w", (256, D)); din("L0_attn_rout_b", (D,))
    din("L0_attn_out_w", (D, D)); din("L0_attn_out_b", (D,))
    din("L0_attn_gl_w", (64, 1)); din("L0_attn_gr_w", (64, 1))
    din("L0_ffn_Wr", (8, D)); din("WrT", (D, 8))
    din("L0_ffn_W1", (8, D, 4096)); din("L0_ffn_b1", (8, 4096))
    din("L0_ffn_Wo", (4096, D)); din("L0_ffn_bo", (D,))
    din("W1_staged", (32, 128, 8, 8, 128)); din("Wo_staged", (32, 128, 8, 128))
    din("L0_ffn_gamma", (D,)); din("L0_ffn_beta", (D,))
    din("L1_attn_ln1_g", (D,)); din("L1_attn_ln1_b", (D,))
    din("L1_attn_w_qkv", (D, 3 * D)); din("L1_attn_b_qkv", (3 * D,))
    din("L1_attn_w_out", (D, D)); din("L1_attn_b_out", (D,))
    din("L1_attn_ln2_g", (D,)); din("L1_attn_ln2_b", (D,))
    din("L1_attn_w1", (D, 4096)); din("L1_attn_b1", (4096,))
    din("L1_attn_w2", (4096, D)); din("L1_attn_b2", (D,))
    din("L1_ffn_wqkv", (2, D, 1536)); din("L1_ffn_q_scale", (2, 64))
    din("L1_ffn_k_scale", (2, 64)); din("L1_ffn_wo", (2, 512, D))
    din("L1_ffn_ln_g", (2, D)); din("L1_ffn_ln_b", (2, D))
    din("L1_ffn_w1", (2, D, 5460)); din("L1_ffn_b1", (2, 5460))
    din("L1_ffn_w2", (2, 2730, D)); din("L1_ffn_b2", (2, D))
    din("mask_local", (NT_OUT, 128, 256)); din("zl_local", (NT_OUT, 128, 1))
    din("mask_lat", (NT_OUT, 128, NLAT)); din("zl_lat", (NT_OUT, 128, 1))
    din("padmask", (NT_EXT, 128, 1))
    din("mask_l1a", (64, 128))
    din("mask_l1fA", (NT_OUT - L1F_QA_LO, 128, 1024))
    din("mask_l1fB", (NT_OUT - L1F_QB_LO, 128, 1024))
    din("rope_cos", (26, 128, 64)); din("rope_sin", (26, 128, 64))
    din("rope_cosi", (26, 128, 64)); din("rope_sini", (26, 128, 64))
    din("ident32", (128, 128)); din("identbf", (128, 128))
    din("g0_gamma_bc", (128, D)); din("g0_beta_bc", (128, D))
    din("l1a_g1bc", (128, D)); din("l1a_b1bc", (128, D))
    din("l1a_g2bc", (128, D)); din("l1a_b2bc", (128, D))
    din("l1f_gbc", (2, 128, D)); din("l1f_bbc", (2, 128, D))
    din("L1_ffn_w2_pad", (2, 2816, D))
    g["xn1_hbm"] = nc.dram_tensor("xn1_hbm", [NT_OUT, 128, D], F32, kind="Internal").ap()
    din("iota_row", (128, 128)); din("pidx", (128, 1)); din("ones_bf_row", (1, 128))
    din("onehot8", (8, 8, 128))
    dint("g0_hbm", (NT_OUT, 128, D), BF16)
    dint("dbg_logits", (NT_OUT, 128, 8), F32)
    dint("dbg_masks", (NT_OUT, 128, 8), F32)
    dint("dbg_y", (NT_OUT, 128, D), F32)
    dint("dbg_lmax", (NT_OUT, 128, 1), F32)
    dint("dbg_rhs", (128, 8, 8, 512), BF16)
    dint("dbg_inter", (128, 32, 512), BF16)
    dint("dbg_mbc", (128, 512), BF16)
    dint("dbg_local", (64, 16, 128), BF16)
    dint("dbg_remote", (64, 16, 128), BF16)
    dint("dbg_mixed", (64, 16, 128), BF16)
    dint("dbg_gates", (1, 2048), BF16)
    dint("dbg_q", (128, 8, 512), BF16)
    dint("dbg_k", (128, 8, 640), BF16)
    dint("dbg_v", (128, 5, 1024), BF16)
    dint("dbg_pt", (128, 2, 128), BF16)
    dint("dbg_m1", (128, 1024), F32)
    dint("dbg_outw", (64, 16, 1024), BF16)
    dint("dbg_outb", (1, 1024), BF16)
    dint("dbg_g0t", (128, 1024), BF16)
    dint("dbg_s0", (128, 1024), F32)
    dint("dbg_mrow", (1, 4096), BF16)
    dint("xn0T_hbm", (NT_OUT, 128, 8 * 128), BF16)
    dint("x1_hbm", (NT_OUT, 128, D), F32)
    dint("W1bf", (32, 128, 8, 8, 128), BF16)
    dint("Wobf", (32, 128, 8, 128), BF16)
    dint("x2a_hbm", (NT_OUT, 128, D), F32)
    dint("yA_hbm", (NT_OUT, 128, D), F32)
    dint("x2fA_hbm", (NT_OUT, 128, D), F32)
    dint("x2fB_hbm", (NT_OUT, 128, D), F32)
    dint("g1_hbm", (NT_OUT, 128, D), BF16)
    out = nc.dram_tensor("out_shard", [T_SH, D], F32, kind="ExternalOutput").ap()
    g["out_shard"] = out

    with contextlib.ExitStack() as es:
        em = Em(nc)
        for s in ("ld", "wd", "tc", "vc", "sc", "st"):
            em.sem(s)
        _build_body(nc, em, g, es)
        block = es.enter_context(nc.Block())
        em.finish(block)
    return nc


def _build_body(nc, em, g, es):
    A = {}

    def sb(name, shape, dtype):
        A[name] = nc.alloc_sbuf_tensor("sb_" + name, list(shape), dtype)
        return A[name]

    sb("ident32", (128, 128), F32); sb("identbf", (128, 128), BF16)
    sb("eps_1em5", (128, 1), F32); sb("eps_1em12", (128, 1), F32)
    sb("iota_row", (128, 128), F32); sb("pidx", (128, 1), F32)
    sb("ones_bf", (1, 128), BF16); sb("ones_f32", (1, 128), F32)
    PS = []
    for i in range(6):
        PS.append(nc.alloc_psum_tensor(f"ps{i}", [128, 512], F32))
    PS.append(nc.alloc_psum_tensor("psb0", [128, 1024], BF16))
    PS.append(nc.alloc_psum_tensor("psb1", [128, 1024], BF16))

    ld, wd, tc, vc, sc, st = "ld", "wd", "tc", "vc", "sc", "st"

    def mset(e):
        e.memset(A["eps_1em5"][:], 1e-5)
        return e.memset(A["eps_1em12"][:], 1e-12)
    em.step("gpsimd", mset, inc="wd")
    em.dma("sync", A["ident32"][:], g["ident32"], inc=ld)
    em.dma("gpsimd", A["identbf"][:], g["identbf"], inc=wd)
    em.dma("sync", A["iota_row"][:], g["iota_row"], inc=ld)
    em.dma("sync", A["pidx"][:], g["pidx"], inc=ld)
    em.dma("gpsimd", A["ones_bf"][:], g["ones_bf_row"], inc=wd)
    em.dma("sync", A["ones_f32"][:], g["ones_bf_row"], inc=ld)
    base_ld = em.led[ld]; base_wd = em.led[wd]

    st_ffn = _ph_ffn(nc, em, g, A, PS, base_ld, base_wd)
    _ph_attn(nc, em, g, A, PS, st_ffn)
    _ph_xn1(nc, em, g, A, PS)
    if DEV_PHASES >= 4:
        _ph_l1f_layer(nc, em, g, A, PS, 0, g["xn1_hbm"], L1F_KV_LO, L1F_QA_LO,
                      g["yA_hbm"], False, "mask_l1fA")
    if DEV_PHASES >= 5:
        _ph_l1f_layer(nc, em, g, A, PS, 1, g["yA_hbm"], L1F_KVB_LO, L1F_QB_LO,
                      None, True, "mask_l1fB")
    if DEV_PHASES >= 6:
        _ph_l1a(nc, em, g, A, PS)
    _final_store_wait(nc, em, g)


def _final_store_wait(nc, em, g):
    em.step("sync", lambda e: None, waits=[("st", em.led["st"])])


class PhaseAlloc:
    _uid = [0]

    def __init__(self, nc, A):
        import contextlib
        self.nc = nc
        self.A = A
        self.es = contextlib.ExitStack()
        PhaseAlloc._uid[0] += 1
        self.tag = f"p{PhaseAlloc._uid[0]}_"

    def sbuf(self, name, shape, dtype):
        h = self.es.enter_context(self.nc.sbuf_tensor(self.tag + name, list(shape), dtype))
        self.A[name] = h
        return h

    def close(self):
        self.es.close()


def alloc_stats(pa, pref):
    for suf in ("_sum", "_ss", "_mneg", "_m2", "_v", "_sd", "_r", "_negmr"):
        pa.sbuf(pref + suf, (128, 1), F32)


def emit_stats(em, A, x_ap, pref, eps, scratch, extra_waits=()):
    assert eps in ("eps_1em5", "eps_1em12")
    """Per-token LN stats of x_ap [128, D] fp32.  Produces A[pref+'_r'] and
    A[pref+'_negmr'] ([128,1] fp32).  Engine-alternating chain (no same-engine
    dependent small ops).  Ends with a vector step; the consumer should wait
    ('vc', em.led['vc']) then scalar ops may use r/negmr as AP scale/bias."""
    vcs, scs = "vc", "sc"
    D_ = D
    em.step("vector", lambda e: e.reduce_sum(A[pref + "_sum"][:], x_ap, axis=AX.X),
            waits=list(extra_waits), inc=vcs)
    em.step("scalar", lambda e: e.activation(A[scratch][:], x_ap, AF.Square,
                                             accum_out=A[pref + "_ss"][:]),
            waits=list(extra_waits), inc=scs)
    def s_m(e):
        e.activation(A[pref + "_mneg"][:], A[pref + "_sum"][:], AF.Identity,
                     scale=-1.0 / D_)
        return e.activation(A[pref + "_m2"][:], A[pref + "_sum"][:], AF.Square,
                            scale=1.0 / D_)
    em.step("scalar", s_m, waits=[(vcs, em.led[vcs])], inc=scs)
    em.step("vector", lambda e: e.tensor_scalar(
        A[pref + "_v"][:], A[pref + "_ss"][:], 1.0 / D_, A[pref + "_m2"][:, 0:1],
        op0=ALU.mult, op1=ALU.subtract),
        waits=[(scs, em.led[scs])], inc=vcs)
    em.step("scalar", lambda e: e.activation(A[pref + "_sd"][:], A[pref + "_v"][:],
                                             AF.Sqrt, bias=A[eps][:, 0:1]),
            waits=[(vcs, em.led[vcs]), ("wd", 1)], inc=scs)
    em.step("vector", lambda e: e.reciprocal(A[pref + "_r"][:], A[pref + "_sd"][:]),
            waits=[(scs, em.led[scs])], inc=vcs)
    em.step("scalar", lambda e: e.activation(A[pref + "_negmr"][:],
                                             A[pref + "_mneg"][:], AF.Identity,
                                             scale=A[pref + "_r"][:, 0:1]),
            waits=[(vcs, em.led[vcs])], inc=scs)
    em.step("vector", lambda e: None, waits=[(scs, em.led[scs])], inc=vcs)


def _ph_ffn(nc, em, g, A, PS, base_ld, base_wd):
    A = dict(A)
    """Computes g0 (silu of MoE block) for all ext-out tiles; writes g0_hbm,
    xn0T_hbm, and bf16 W1/Wo copies to HBM on the first pass."""
    pa = PhaseAlloc(nc, A)
    sbuf = pa.sbuf
    ld, wd, tcs, vcs, scs, sts = "ld", "wd", "tc", "vc", "sc", "st"
    if DEV_PHASES < 1:
        pa.close()
        return em.led["st"]
    sbuf("x_g", (128, 2, D), F32)
    sbuf("sq_scratch", (128, D), F32)
    alloc_stats(pa, "s0")
    alloc_stats(pa, "sy")
    sbuf("xn32_t", (128, D), F32)
    sbuf("hi_t", (128, D), BF16)
    sbuf("lo_t", (128, D), BF16)
    sbuf("xnT_g", (128, 8, 512), BF16)
    sbuf("loT_t", (128, 8, 128), BF16)
    sbuf("wr_hi", (128, 8, 8), BF16)
    sbuf("wr_lo", (128, 8, 8), BF16)
    sbuf("wr_tmp", (128, 8, 8), F32)
    sbuf("logits", (128, 8), F32)
    sbuf("lmax", (128, 1), F32)
    sbuf("masks", (128, 8), F32)
    sbuf("masks_bf", (128, 128), BF16)
    sbuf("masksT_g", (8, 512), BF16)
    sbuf("onehot8", (8, 8, 128), BF16)
    sbuf("mask_bc", (128, 512), BF16)
    sbuf("rhs_e", (128, 8, 8, 512), BF16)
    sbuf("w1slab", (128, 2, 8, 8, 128), BF16)
    sbuf("woslab", (128, 2, 8, 128), BF16)
    sbuf("inter_g", (128, 32, 512), BF16)
    sbuf("b1sb", (8, 4096), BF16)
    sbuf("xn_tm", (128, D), F32)
    sbuf("y_t", (128, D), F32)
    sbuf("gam_bc", (128, D), F32)
    sbuf("bet_bc", (128, D), F32)
    sbuf("g0_t", (128, D), BF16)
    sbuf("rowbuf2_bf", (1, 1024), BF16)

    em.dma("sync", A["gam_bc"][:], g["g0_gamma_bc"], inc=ld)
    em.dma("sync", A["bet_bc"][:], g["g0_beta_bc"], inc=ld)
    em.dma("gpsimd", A["b1sb"][:], g["L0_ffn_b1"], inc=wd)
    em.dma("gpsimd", A["onehot8"][:], g["onehot8"], inc=wd)
    em.step("gpsimd", lambda e: e.memset(A["masks_bf"][:], 0.0), inc=wd)
    em.dma("sync", A["wr_tmp"][:], g["WrT"].rearrange("(c p) e -> p c e", p=128), inc=ld)
    em.step("vector", lambda e: e.tensor_copy(A["wr_hi"][:], A["wr_tmp"][:]),
            waits=[(ld, em.led[ld])], inc=vcs)
    em.step("scalar", lambda e: e.activation(A["sq_scratch"][:, 0:64],
                                             A["wr_hi"][:].rearrange("p c e -> p (c e)"),
                                             AF.Identity),
            waits=[(vcs, em.led[vcs])], inc=scs)
    em.step("vector", lambda e: e.tensor_tensor(
        A["wr_lo"][:].rearrange("p c e -> p (c e)"),
        A["wr_tmp"][:].rearrange("p c e -> p (c e)"),
        A["sq_scratch"][:, 0:64], op=ALU.subtract),
        waits=[(scs, em.led[scs])], inc=vcs)
    em.dma("gpsimd", A["rowbuf2_bf"][0:1, 0:1024],
           g["L0_ffn_bo"].rearrange("(o d) -> o d", o=1), inc=wd)

    for gi, (g0t, gn) in enumerate(GROUPS):
        if gi >= DEV_GROUPS:
            break
        first_pass = (gi == 0)
        ntok = gn * 128
        for ti in range(gn):
            T = g0t + ti
            xt = A["x_g"][:, ti % 2, :]
            em.dma("sync", xt, g["x_ext"][(T + 1) * 128:(T + 2) * 128, :],
                   waits=[(vcs, em.led[vcs]), (scs, em.led[scs])], inc=ld)
            emit_stats(em, A, xt, "s0", "eps_1em5", "sq_scratch",
                       extra_waits=[(ld, em.led[ld])])
            def norm(e, xt=xt):
                e.activation(A["xn32_t"][:], xt, AF.Identity,
                             bias=A["s0_negmr"][:, 0:1], scale=A["s0_r"][:, 0:1])
                return e.activation(A["hi_t"][:], xt, AF.Identity,
                             bias=A["s0_negmr"][:, 0:1], scale=A["s0_r"][:, 0:1])
            em.step("scalar", norm, waits=[(vcs, em.led[vcs])], inc=scs)
            em.step("vector", lambda e: e.tensor_tensor(
                A["lo_t"][:], A["xn32_t"][:], A["hi_t"][:], op=ALU.subtract),
                waits=[(scs, em.led[scs])], inc=vcs)
            vc_norm = em.led[vcs]
            for cc in range(8):
                bank = PS[4 + (cc % 2)]
                em.step("tensor", lambda e, cc=cc, bank=bank: e.matmul(
                    bank[0:128, 0:128], A["hi_t"][:, cc * 128:(cc + 1) * 128],
                    A["identbf"][:], start=True, stop=True),
                    waits=[(vcs, vc_norm), (wd, base_wd),
                           (scs, em.led[scs]), (vcs, em.led[vcs])], inc=tcs)
                em.step("vector", lambda e, cc=cc, bank=bank, ti=ti: e.tensor_copy(
                    A["xnT_g"][:, cc, ti * 128:(ti + 1) * 128], bank[0:128, 0:128]),
                    waits=[(tcs, em.led[tcs]), (sts, em.led[sts])], inc=vcs)
            for cc in range(8):
                bank = PS[4 + (cc % 2)]
                em.step("tensor", lambda e, cc=cc, bank=bank: e.matmul(
                    bank[0:128, 0:128], A["lo_t"][:, cc * 128:(cc + 1) * 128],
                    A["identbf"][:], start=True, stop=True),
                    waits=[(vcs, em.led[vcs])], inc=tcs)
                em.step("vector", lambda e, cc=cc, bank=bank: e.tensor_copy(
                    A["loT_t"][:, cc, :], bank[0:128, 0:128]),
                    waits=[(tcs, em.led[tcs])], inc=vcs)
            def router(e, ti=ti):
                n = 0
                for cc in range(8):
                    for lh, wr in ((A["xnT_g"][:, cc, ti * 128:(ti + 1) * 128],
                                    A["wr_hi"][:, cc, :]),
                                   (A["xnT_g"][:, cc, ti * 128:(ti + 1) * 128],
                                    A["wr_lo"][:, cc, :]),
                                   (A["loT_t"][:, cc, :], A["wr_hi"][:, cc, :]),
                                   (A["loT_t"][:, cc, :], A["wr_lo"][:, cc, :])):
                        last = e.matmul(PS[5][0:128, 0:8], lh, wr,
                                        start=(n == 0), stop=(n == 31))
                        n += 1
                return last
            em.step("tensor", router, waits=[(vcs, em.led[vcs])], inc=tcs)
            em.step("vector", lambda e: e.tensor_copy(A["logits"][:], PS[5][0:128, 0:8]),
                    waits=[(tcs, em.led[tcs]), (sts, em.led[sts])], inc=vcs)
            em.step("vector", lambda e: e.reduce_max(A["lmax"][:], A["logits"][:],
                                                     axis=AX.X),
                    waits=[(vcs, em.led[vcs])], inc=vcs)
            em.step("scalar", lambda e: None, waits=[(vcs, em.led[vcs])], inc=scs)
            em.step("vector", lambda e: e.tensor_scalar(
                A["masks"][:], A["logits"][:], A["lmax"][:, 0:1], 1.0,
                op0=ALU.is_equal, op1=ALU.mult),
                waits=[(scs, em.led[scs]), (sts, em.led[sts])], inc=vcs)
            em.step("vector", lambda e: e.tensor_copy(A["masks_bf"][:, 0:8],
                                                      A["masks"][:]),
                    waits=[(vcs, em.led[vcs])], inc=vcs)
            em.step("tensor", lambda e: e.matmul(PS[4][0:128, 0:128],
                                                 A["masks_bf"][:], A["identbf"][:],
                                                 start=True, stop=True),
                    waits=[(vcs, em.led[vcs]), ("wd", 3 * 16 + 1)], inc=tcs)
            em.step("vector", lambda e, ti=ti: e.tensor_copy(
                A["masksT_g"][0:8, ti * 128:(ti + 1) * 128], PS[4][0:8, 0:128]),
                waits=[(tcs, em.led[tcs])], inc=vcs)
            em.dma("sync", g["dbg_logits"][T], A["logits"][:],
                   waits=[(vcs, em.led[vcs])], inc=sts)
            em.dma("sync", g["dbg_masks"][T], A["masks"][:],
                   waits=[(vcs, em.led[vcs])], inc=sts)
            em.dma("sync", g["dbg_lmax"][T], A["lmax"][:],
                   waits=[(vcs, em.led[vcs])], inc=sts)
            em.dma("sync", g["xn0T_hbm"][T, :, :].rearrange("p (c t) -> p c t", t=128),
                   A["xnT_g"][:, :, ti * 128:(ti + 1) * 128],
                   waits=[(vcs, em.led[vcs])], inc=sts)

        if DEV_STAGE < 1:
            continue
        for e_i in range(8):
            em.step("tensor", lambda e, e_i=e_i, ntok=ntok: e.matmul(
                PS[5][0:128, 0:ntok], A["onehot8"][:, e_i, :],
                A["masksT_g"][0:8, 0:ntok], start=True, stop=True),
                waits=[(vcs, em.led[vcs])], inc=tcs)
            em.step("vector", lambda e, ntok=ntok: e.tensor_copy(
                A["mask_bc"][:, 0:ntok], PS[5][0:128, 0:ntok]),
                waits=[(tcs, em.led[tcs])], inc=vcs)
            em.step("vector", lambda e, e_i=e_i, ntok=ntok: e.tensor_tensor(
                A["rhs_e"][:, e_i, :, 0:ntok], A["xnT_g"][:, :, 0:ntok],
                A["mask_bc"][:, 0:ntok].rearrange("p (o t) -> p o t", o=1)
                    .broadcast_to([128, 8, ntok]), op=ALU.mult),
                waits=[(vcs, em.led[vcs])], inc=vcs)

        if DEV_STAGE < 2:
            continue
        vcs_rhs = em.led[vcs]
        mm_lev = {}
        ev_lev = {}
        wd_lev = {}
        def w1_load(isl):
            dbuf = isl % 2
            wsl = A["w1slab"][:, dbuf, :, :, :]
            wt = [(tcs, mm_lev[isl - 2])] if isl >= 2 else [(tcs, em.led[tcs])]
            if first_pass:
                em.dma("gpsimd", wsl, g["W1_staged"][isl], waits=wt, inc=wd)
                em.dma("gpsimd", g["W1bf"][isl], wsl, waits=[(wd, em.led[wd])], inc=wd)
            else:
                em.dma("gpsimd", wsl, g["W1bf"][isl], waits=wt, inc=wd)
            wd_lev[isl] = em.led[wd]
        w1_load(0)
        w1_load(1)
        for isl in range(32):
            wsl = A["w1slab"][:, isl % 2, :, :, :]
            bank = PS[0] if isl % 2 == 0 else PS[3]
            def w1mm(e, isl=isl, wsl=wsl, bank=bank, ntok=ntok):
                last = None
                for e_i in range(8):
                    for cc in range(8):
                        last = e.matmul(bank[0:128, 0:ntok], wsl[:, e_i, cc, :],
                                        A["rhs_e"][:, e_i, cc, 0:ntok],
                                        start=(e_i == 0 and cc == 0), stop=False)
                last = e.matmul(bank[0:128, 0:ntok],
                                A["b1sb"][0:8, isl * 128:(isl + 1) * 128],
                                A["masksT_g"][0:8, 0:ntok], start=False, stop=True)
                return last
            wt = [(wd, wd_lev[isl]), (vcs, vcs_rhs)]
            if isl >= 2:
                wt.append((scs, ev_lev[isl - 2]))
            else:
                wt.append((scs, em.led[scs]))
            em.step("tensor", w1mm, waits=wt, inc=tcs)
            mm_lev[isl] = em.led[tcs]
            if isl + 2 < 32:
                w1_load(isl + 2)
            em.step("scalar", lambda e, isl=isl, bank=bank, ntok=ntok: e.activation(
                A["inter_g"][:, isl, 0:ntok], bank[0:128, 0:ntok], AF.Gelu),
                waits=[(tcs, mm_lev[isl])], inc=scs)
            ev_lev[isl] = em.led[scs]

        if DEV_STAGE < 3:
            continue
        for pair0 in range(0, gn, 2):
            pn = min(2, gn - pair0)
            womm_lev = {}
            wold_lev = {}
            def wo_load(isl, pair0=pair0):
                dbuf = isl % 2
                wos = A["woslab"][:, dbuf, :, :]
                wt = [(tcs, womm_lev[isl - 2])] if isl >= 2 else [(tcs, em.led[tcs])]
                if first_pass and pair0 == 0:
                    em.dma("gpsimd", wos, g["Wo_staged"][isl], waits=wt, inc=wd)
                    em.dma("gpsimd", g["Wobf"][isl], wos, waits=[(wd, em.led[wd])],
                           inc=wd)
                else:
                    em.dma("gpsimd", wos, g["Wobf"][isl], waits=wt, inc=wd)
                wold_lev[isl] = em.led[wd]
            wo_load(0)
            wo_load(1)
            sc_pre = em.led[scs]
            vc_pre = em.led[vcs]
            for isl in range(32):
                wos = A["woslab"][:, isl % 2, :, :]
                def womm(e, isl=isl, wos=wos, pair0=pair0, pn=pn):
                    last = None
                    for ti in range(pn):
                        t = pair0 + ti
                        for nh in range(2):
                            last = e.matmul(
                                PS[1 + 2 * ti + nh][0:128, 0:512],
                                A["inter_g"][:, isl, t * 128:(t + 1) * 128],
                                wos[:, nh * 4:(nh + 1) * 4, :].rearrange("p a b -> p (a b)"),
                                start=(isl == 0), stop=False)
                    return last
                em.step("tensor", womm, waits=[(wd, wold_lev[isl]), (scs, sc_pre),
                                               (vcs, vc_pre)], inc=tcs)
                womm_lev[isl] = em.led[tcs]
                if isl + 2 < 32:
                    wo_load(isl + 2)
            def bioadd(e, pair0=pair0, pn=pn):
                last = None
                for ti in range(pn):
                    for nh in range(2):
                        last = e.matmul(PS[1 + 2 * ti + nh][0:128, 0:512],
                                        A["ones_bf"][0:1, :],
                                        A["rowbuf2_bf"][0:1, nh * 512:(nh + 1) * 512],
                                        start=False, stop=True)
                return last
            em.step("tensor", bioadd, waits=[(vcs, em.led[vcs])], inc=tcs)
            tc_done = em.led[tcs]
            for ti in range(pn):
                t = pair0 + ti
                T = g0t + t
                xt = A["x_g"][:, t % 2, :]
                em.dma("sync", xt, g["x_ext"][(T + 1) * 128:(T + 2) * 128, :],
                       waits=[(vcs, em.led[vcs]), (scs, em.led[scs])], inc=ld)
                emit_stats(em, A, xt, "s0", "eps_1em5", "sq_scratch",
                           extra_waits=[(ld, em.led[ld]), (tcs, tc_done)])
                em.step("scalar", lambda e, xt=xt: e.activation(
                    A["xn_tm"][:], xt, AF.Identity, bias=A["s0_negmr"][:, 0:1],
                    scale=A["s0_r"][:, 0:1]),
                    waits=[(vcs, em.led[vcs])], inc=scs)
                def yadd(e, ti=ti):
                    e.tensor_tensor(A["y_t"][:, 0:512], PS[1 + 2 * ti][0:128, 0:512],
                                    A["xn_tm"][:, 0:512], op=ALU.add)
                    return e.tensor_tensor(A["y_t"][:, 512:1024],
                                           PS[2 + 2 * ti][0:128, 0:512],
                                           A["xn_tm"][:, 512:1024], op=ALU.add)
                em.step("vector", yadd, waits=[(scs, em.led[scs]),
                                               (sts, em.led[sts])], inc=vcs)
                em.dma("sync", g["dbg_y"][T], A["y_t"][:],
                       waits=[(vcs, em.led[vcs])], inc=sts)
                emit_stats(em, A, A["y_t"][:], "sy", "eps_1em12", "sq_scratch",
                           extra_waits=[("vc", em.led["vc"])])
                em.step("scalar", lambda e: e.activation(
                    A["xn_tm"][:], A["y_t"][:], AF.Identity,
                    bias=A["sy_negmr"][:, 0:1], scale=A["sy_r"][:, 0:1]),
                    waits=[(vcs, em.led[vcs])], inc=scs)
                em.step("vector", lambda e: e.tensor_tensor(
                    A["xn_tm"][:], A["xn_tm"][:], A["gam_bc"][:], op=ALU.mult),
                    waits=[(scs, em.led[scs]), (vcs, em.led[vcs])], inc=vcs)
                em.step("vector", lambda e: e.tensor_tensor(
                    A["xn_tm"][:], A["xn_tm"][:], A["bet_bc"][:], op=ALU.add),
                    waits=[(vcs, em.led[vcs])], inc=vcs)
                em.step("scalar", lambda e: e.activation(A["g0_t"][:], A["xn_tm"][:],
                                                         AF.Silu),
                        waits=[(vcs, em.led[vcs]), (sts, em.led[sts])], inc=scs)
                em.dma("sync", g["g0_hbm"][T, :, :], A["g0_t"][:],
                       waits=[(scs, em.led[scs])], inc=sts)
    em.barrier()
    pa.close()
    return em.led["st"]


def _ph_attn(nc, em, g, A, PS, st_before):
    A = dict(A)
    if DEV_PHASES < 2:
        return
    pa = PhaseAlloc(nc, A)
    sbuf = pa.sbuf
    ld, wd, tcs, vcs, scs, sts = "ld", "wd", "tc", "vc", "sc", "st"

    sbuf("qkvw", (128, 8, 3072), BF16)
    sbuf("rqw", (128, 8, 256), BF16)
    sbuf("routw", (64, 4, 1024), BF16)
    sbuf("outw", (64, 16, 1024), BF16)
    sbuf("glgr", (64, 2), BF16)
    sbuf("qkb_col", (128, 16), F32)
    sbuf("bv_row", (1, 1024), BF16)
    sbuf("rqb_col", (128, 2), F32)
    sbuf("routb_col", (64, 16), F32)
    sbuf("outb_row", (1, 1024), BF16)
    sbuf("rkvw", (128, 8, 512), BF16)
    sbuf("rkvb_row", (1, 512), BF16)
    sbuf("rkvb_col", (128, 2), F32)
    em.dma("gpsimd", A["qkvw"][:], g["L0_attn_qkv_w"].rearrange("(c p) f -> p c f", p=128), inc=wd)
    em.dma("gpsimd", A["rqw"][:], g["L0_attn_rq_w"].rearrange("(c p) f -> p c f", p=128), inc=wd)
    em.dma("gpsimd", A["routw"][:], g["L0_attn_rout_w"].rearrange("(l p) f -> p l f", p=64), inc=wd)
    em.dma("gpsimd", A["outw"][:], g["L0_attn_out_w"].rearrange("(h p) f -> p h f", p=64), inc=wd)
    em.dma("gpsimd", A["glgr"][:, 0:1], g["L0_attn_gl_w"], inc=wd)
    em.dma("gpsimd", A["glgr"][:, 1:2], g["L0_attn_gr_w"], inc=wd)
    em.dma("sync", A["qkb_col"][:], g["L0_attn_qkv_b"][0:2048].rearrange("(s p) -> p s", p=128),
           inc=ld, slow=True)
    em.dma("gpsimd", A["bv_row"][:], g["L0_attn_qkv_b"][2048:3072].rearrange("(o d) -> o d", o=1), inc=wd)
    em.dma("sync", A["rqb_col"][:], g["L0_attn_rq_b"].rearrange("(s p) -> p s", p=128),
           inc=ld, slow=True)
    em.dma("sync", A["routb_col"][:], g["L0_attn_rout_b"].rearrange("(h p) -> p h", p=64),
           inc=ld, slow=True)
    em.dma("gpsimd", A["outb_row"][:], g["L0_attn_out_b"].rearrange("(o d) -> o d", o=1), inc=wd)
    em.dma("gpsimd", A["rkvw"][:], g["L0_attn_rkv_w"].rearrange("(c p) f -> p c f", p=128), inc=wd)
    em.dma("gpsimd", A["rkvb_row"][:], g["L0_attn_rkv_b"].rearrange("(o d) -> o d", o=1), inc=wd)
    em.dma("sync", A["rkvb_col"][:], g["L0_attn_rkv_b"][0:256].rearrange("(s p) -> p s", p=128),
           inc=ld, slow=True)
    base_w = em.led[wd]; base_l = em.led[ld]

    if DEV_ATTN < -2:
        em.barrier()
        pa.close()
        return
    sbuf("xf_t", (128, 2, D), F32)
    alloc_stats(pa, "s0")
    sbuf("sq_scratch", (128, D), F32)
    sbuf("hi_t", (128, D), BF16)
    sbuf("pooled32", (128, 8, 64), F32)
    sbuf("pooledT", (128, 8, 64), BF16)
    sbuf("rk_fm", (128, 2, 64), BF16)
    sbuf("rv_tm", (64, 256), BF16)
    for j in range(64):
        xt = A["xf_t"][:, j % 2, :]
        em.dma("sync", xt, g["x_full"][j * 128:(j + 1) * 128, :],
               waits=[(vcs, em.led[vcs]), (scs, em.led[scs])], inc=ld)
        emit_stats(em, A, xt, "s0", "eps_1em5", "sq_scratch",
                   extra_waits=[(ld, em.led[ld])])
        em.step("scalar", lambda e, xt=xt: e.activation(
            A["hi_t"][:], xt, AF.Identity, bias=A["s0_negmr"][:, 0:1],
            scale=A["s0_r"][:, 0:1]),
            waits=[(vcs, em.led[vcs])], inc=scs)
        for cc in range(8):
            bank = PS[4 + (cc % 2)]
            em.step("tensor", lambda e, cc=cc, bank=bank: e.matmul(
                bank[0:128, 0:128], A["hi_t"][:, cc * 128:(cc + 1) * 128],
                A["identbf"][:], start=True, stop=True),
                waits=[(scs, em.led[scs]), (vcs, em.led[vcs]), (wd, base_w)], inc=tcs)
            em.step("vector", lambda e, cc=cc, bank=bank, j=j: e.reduce_sum(
                A["pooled32"][:, cc, j:j + 1], bank[0:128, 0:128], axis=AX.X),
                waits=[(tcs, em.led[tcs])], inc=vcs)
    if DEV_ATTN < -1:
        em.barrier()
        pa.close()
        return
    em.step("vector", lambda e: e.tensor_scalar(
        A["pooledT"][:].rearrange("p c l -> p (c l)"),
        A["pooled32"][:].rearrange("p c l -> p (c l)"), 1.0 / 128, None, op0=ALU.mult),
        waits=[(vcs, em.led[vcs])], inc=vcs)
    for sub in range(2):
        def rkmm(e, sub=sub):
            last = None
            for cc in range(8):
                last = e.matmul(PS[0][0:128, 0:64],
                                A["rkvw"][:, cc, sub * 128:(sub + 1) * 128],
                                A["pooledT"][:, cc, :], start=(cc == 0), stop=(cc == 7))
            return last
        em.step("tensor", rkmm, waits=[(vcs, em.led[vcs]), (scs, em.led[scs]),
                                       (wd, base_w), (ld, base_l)], inc=tcs)
        em.step("scalar", lambda e, sub=sub: e.activation(
            A["rk_fm"][:, sub, :], PS[0][0:128, 0:64], AF.Identity,
            bias=A["rkvb_col"][:, sub:sub + 1]),
            waits=[(tcs, em.led[tcs])], inc=scs)
    if DEV_ATTN < 0:
        em.barrier()
        pa.close()
        return
    def rvmm(e):
        last = e.matmul(PS[1][0:64, 0:256], A["ones_bf"][0:1, 0:64],
                        A["rkvb_row"][0:1, 256:512], start=True, stop=False)
        for cc in range(8):
            last = e.matmul(PS[1][0:64, 0:256], A["pooledT"][:, cc, :],
                            A["rkvw"][:, cc, 256:512], start=False, stop=(cc == 7))
        return last
    em.step("tensor", rvmm, waits=[(scs, em.led[scs]), (vcs, em.led[vcs])], inc=tcs)
    em.step("vector", lambda e: e.tensor_copy(A["rv_tm"][:], PS[1][0:64, 0:256]),
            waits=[(tcs, em.led[tcs])], inc=vcs)

    if DEV_ATTN < 1:
        em.barrier()
        pa.close()
        return
    sbuf("xnT_a", (128, 8, 512), BF16)
    sbuf("q_fm", (128, 8, 512), BF16)
    sbuf("k_fm", (128, 8, 640), BF16)
    sbuf("v_tm", (128, 5, D), BF16)
    sbuf("rq_fm", (128, 2, 512), BF16)
    sbuf("msk_t", (128, 2, 256), F32)
    sbuf("mlat_t", (128, 2, 64), F32)
    sbuf("zl_t", (128, 2, 2), F32)
    sbuf("pad_t", (128, 1), F32)
    sbuf("sc4", (128, 4, 256), F32)
    A["scl4"] = A["sc4"][:, 0, :].rearrange("p (l k) -> p l k", k=64)
    sbuf("negmax4", (128, 4), F32)
    sbuf("den4", (128, 4), F32)
    sbuf("rec4", (128, 4), F32)
    sbuf("diag4", (128, 4, 128), BF16)
    sbuf("p4", (128, 4, 256), BF16)
    sbuf("pT4", (128, 8, 128), BF16)
    sbuf("pl4", (128, 4, 64), BF16)
    sbuf("pTl4", (64, 4, 128), BF16)
    sbuf("local_fm", (64, 16, 128), BF16)
    sbuf("lat_fm", (64, 4, 128), BF16)
    sbuf("remote_fm", (64, 16, 128), BF16)
    sbuf("gates_row", (1, 2048), BF16)
    sbuf("gb_fm", (64, 16, 128), BF16)
    sbuf("t1", (64, 16, 128), BF16)
    sbuf("mixed", (64, 16, 128), BF16)
    sbuf("g0_t", (128, D), BF16)
    A["m1_t"] = A["sq_scratch"]
    sbuf("x1_t", (128, D), F32)

    for gi, (g0t, gn) in enumerate(GROUPS):
        if gi >= DEV_GROUPS:
            break
        ntok = gn * 128
        if gi > 0:
            em.step("vector", lambda e: e.tensor_copy(
                A["k_fm"][:, :, 0:128], A["k_fm"][:, :, 512:640]),
                waits=[(vcs, em.led[vcs]), (tcs, em.led[tcs])], inc=vcs)
            em.step("vector", lambda e: e.tensor_copy(
                A["v_tm"][:, 0, :], A["v_tm"][:, 4, :]),
                waits=[(vcs, em.led[vcs])], inc=vcs)
        else:
            em.step("gpsimd", lambda e: (e.memset(A["k_fm"][:, :, 0:128], 0.0),
                                         e.memset(A["v_tm"][:, 0, :], 0.0))[-1],
                    waits=[("wd", base_w)], inc=wd)
        for ti in range(gn):
            T = g0t + ti
            em.dma("sync", A["xnT_a"][:, :, ti * 128:(ti + 1) * 128],
                   g["xn0T_hbm"][T, :, :].rearrange("p (c t) -> p c t", t=128),
                   waits=[(tcs, em.led[tcs]), (vcs, em.led[vcs])], inc=ld)
        ld_x = em.led[ld]
        for half, dst in ((0, "q_fm"), (1, "k_fm")):
            for sub in range(8):
                def qkmm(e, half=half, sub=sub, ntok=ntok):
                    last = None
                    for cc in range(8):
                        last = e.matmul(PS[0][0:128, 0:ntok],
                                        A["qkvw"][:, cc,
                                                  half * 1024 + sub * 128:
                                                  half * 1024 + (sub + 1) * 128],
                                        A["xnT_a"][:, cc, 0:ntok],
                                        start=(cc == 0), stop=(cc == 7))
                    return last
                em.step("tensor", qkmm, waits=[(ld, ld_x), (vcs, em.led[vcs]),
                                               (scs, em.led[scs])], inc=tcs)
                off = 128 if dst == "k_fm" else 0
                em.step("scalar", lambda e, dst=dst, sub=sub, half=half, off=off,
                        ntok=ntok: e.activation(
                    A[dst][:, sub, off:off + ntok], PS[0][0:128, 0:ntok], AF.Identity,
                    bias=A["qkb_col"][:, half * 8 + sub:half * 8 + sub + 1]),
                    waits=[(tcs, em.led[tcs])], inc=scs)
        for sub in range(2):
            def rqmm(e, sub=sub, ntok=ntok):
                last = None
                for cc in range(8):
                    last = e.matmul(PS[0][0:128, 0:ntok],
                                    A["rqw"][:, cc, sub * 128:(sub + 1) * 128],
                                    A["xnT_a"][:, cc, 0:ntok],
                                    start=(cc == 0), stop=(cc == 7))
                return last
            em.step("tensor", rqmm, waits=[(vcs, em.led[vcs]), (scs, em.led[scs])], inc=tcs)
            em.step("scalar", lambda e, sub=sub, ntok=ntok: e.activation(
                A["rq_fm"][:, sub, 0:ntok], PS[0][0:128, 0:ntok], AF.Identity,
                bias=A["rqb_col"][:, sub:sub + 1]),
                waits=[(tcs, em.led[tcs])], inc=scs)
        for ti in range(gn):
            def vmm(e, ti=ti):
                last = e.matmul(PS[1][0:128, 0:512], A["ones_bf"][0:1, :],
                                A["bv_row"][0:1, 0:512], start=True, stop=False)
                last = e.matmul(PS[2][0:128, 0:512], A["ones_bf"][0:1, :],
                                A["bv_row"][0:1, 512:1024], start=True, stop=False)
                for cc in range(8):
                    last = e.matmul(PS[1][0:128, 0:512],
                                    A["xnT_a"][:, cc, ti * 128:(ti + 1) * 128],
                                    A["qkvw"][:, cc, 2048:2560],
                                    start=False, stop=(cc == 7))
                    last = e.matmul(PS[2][0:128, 0:512],
                                    A["xnT_a"][:, cc, ti * 128:(ti + 1) * 128],
                                    A["qkvw"][:, cc, 2560:3072],
                                    start=False, stop=(cc == 7))
                return last
            em.step("tensor", vmm, waits=[(vcs, em.led[vcs]), (scs, em.led[scs])], inc=tcs)
            def vevac(e, ti=ti):
                e.tensor_copy(A["v_tm"][:, 1 + ti, 0:512], PS[1][0:128, 0:512])
                return e.tensor_copy(A["v_tm"][:, 1 + ti, 512:1024], PS[2][0:128, 0:512])
            em.step("vector", vevac, waits=[(tcs, em.led[tcs])], inc=vcs)

        if DEV_ATTN < 2:
            continue
        for ti in range(gn):
            T = g0t + ti
            em.dma("sync", A["msk_t"][:, ti % 2, :], g["mask_local"][T],
                   waits=[(vcs, em.led[vcs])], inc=ld)
            em.dma("sync", A["mlat_t"][:, ti % 2, :], g["mask_lat"][T],
                   waits=[(vcs, em.led[vcs])], inc=ld)
            em.dma("sync", A["zl_t"][:, ti % 2, 0:1], g["zl_local"][T],
                   waits=[(vcs, em.led[vcs])], inc=ld)
            em.dma("sync", A["zl_t"][:, ti % 2, 1:2], g["zl_lat"][T],
                   waits=[(vcs, em.led[vcs])], inc=ld)
            ld_m = em.led[ld]
            if DEV_ATTN == 18:
                continue
            for wv in range(4):
                def smm(e, wv=wv, ti=ti):
                    last = None
                    for hh in DEV_SMM_HH:
                        h = wv * 4 + hh
                        sub = h // 2
                        po = (h % 2) * 64
                        last = e.matmul(
                            PS[hh][0:128, 0:256],
                            A["q_fm"][po:po + 64, sub, ti * 128:(ti + 1) * 128],
                            A["k_fm"][po:po + 64, sub, ti * 128:(ti + 2) * 128],
                            start=True, stop=True)
                    return last
                em.step("tensor", smm, waits=[(vcs, em.led[vcs]),
                                              (scs, em.led[scs])], inc=tcs)
                if DEV_ATTN == 185:
                    continue
                def smask(e, ti=ti):
                    last = None
                    for hh in range(4):
                        last = e.scalar_tensor_tensor(
                            A["sc4"][:, hh, :], PS[hh][0:128, 0:256],
                            0.125, A["msk_t"][:, ti % 2, :], op0=ALU.mult, op1=ALU.add)
                    return last
                em.step("vector", smask, waits=[(tcs, em.led[tcs]), (ld, ld_m)], inc=vcs)
                if DEV_ATTN == 20:
                    continue
                em.step("vector", lambda e: e.reduce_max(
                    A["negmax4"][:], A["sc4"][:], axis=AX.X, negate=True),
                    waits=[(vcs, em.led[vcs])], inc=vcs)
                def sexp(e):
                    last = None
                    for hh in range(4):
                        last = e.activation(A["p4"][:, hh, :], A["sc4"][:, hh, :],
                                            AF.Exp, bias=A["negmax4"][:, hh:hh + 1],
                                            accum_out=A["den4"][:, hh:hh + 1])
                    return last
                em.step("scalar", sexp, waits=[(vcs, em.led[vcs])], inc=scs)
                if DEV_ATTN == 21:
                    continue
                em.step("vector", lambda e: e.reciprocal(A["rec4"][:], A["den4"][:]),
                        waits=[(scs, em.led[scs])], inc=vcs)
                em.step("vector", lambda e, ti=ti: e.tensor_scalar(
                    A["rec4"][:], A["rec4"][:], A["zl_t"][:, ti % 2, 0:1], None,
                    op0=ALU.mult),
                    waits=[(vcs, em.led[vcs])], inc=vcs)
                def sdiag(e):
                    last = None
                    for hh in range(4):
                        last = e.tensor_scalar(A["diag4"][:, hh, :], A["iota_row"][:],
                                               A["pidx"][:, 0:1], A["rec4"][:, hh:hh + 1],
                                               op0=ALU.is_equal, op1=ALU.mult)
                    return last
                em.step("vector", sdiag, waits=[(vcs, em.led[vcs])], inc=vcs)
                if DEV_ATTN == 22:
                    continue
                for kc in range(2):
                    def ptmm(e, kc=kc):
                        last = None
                        for hh in range(4):
                            last = e.matmul(
                                PS[hh][0:128, 0:128],
                                A["p4"][:, hh, kc * 128:(kc + 1) * 128],
                                A["diag4"][:, hh, :], start=True, stop=True)
                        return last
                    em.step("tensor", ptmm, waits=[(vcs, em.led[vcs]),
                                                   (scs, em.led[scs])], inc=tcs)
                    def ptev(e, kc=kc):
                        last = None
                        for hh in range(4):
                            last = e.tensor_copy(A["pT4"][:, 2 * hh + kc, :],
                                                 PS[hh][0:128, 0:128])
                        return last
                    em.step("vector", ptev, waits=[(tcs, em.led[tcs])], inc=vcs)
                if DEV_ATTN == 23:
                    continue
                def lmm(e, wv=wv, ti=ti):
                    last = None
                    for hh in range(4):
                        h = wv * 4 + hh
                        last = e.matmul(PS[hh][0:64, 0:128],
                                        A["v_tm"][:, ti, h * 64:(h + 1) * 64],
                                        A["pT4"][:, 2 * hh, :], start=True, stop=False)
                        last = e.matmul(PS[hh][0:64, 0:128],
                                        A["v_tm"][:, 1 + ti, h * 64:(h + 1) * 64],
                                        A["pT4"][:, 2 * hh + 1, :], start=False, stop=True)
                    return last
                em.step("tensor", lmm, waits=[(vcs, em.led[vcs])], inc=tcs)
                def locev(e, wv=wv):
                    last = None
                    for hh in range(4):
                        last = e.tensor_copy(A["local_fm"][:, wv * 4 + hh, :],
                                             PS[hh][0:64, 0:128])
                    return last
                em.step("vector", locev, waits=[(tcs, em.led[tcs])], inc=vcs)
            if DEV_ATTN < 3 or 20 <= DEV_ATTN < 30:
                continue
            def lsmm(e, ti=ti):
                last = None
                for lh in range(4):
                    sub = lh // 2
                    po = (lh % 2) * 64
                    last = e.matmul(PS[lh][0:128, 0:64],
                                    A["rq_fm"][po:po + 64, sub, ti * 128:(ti + 1) * 128],
                                    A["rk_fm"][po:po + 64, sub, :],
                                    start=True, stop=True)
                return last
            em.step("tensor", lsmm, waits=[(vcs, em.led[vcs]),
                                           (scs, em.led[scs])], inc=tcs)
            def lmask(e, ti=ti):
                last = None
                for lh in range(4):
                    last = e.scalar_tensor_tensor(
                        A["scl4"][:, lh, :], PS[lh][0:128, 0:64],
                        0.125, A["mlat_t"][:, ti % 2, :], op0=ALU.mult, op1=ALU.add)
                return last
            em.step("vector", lmask, waits=[(tcs, em.led[tcs]), (ld, ld_m)], inc=vcs)
            em.step("vector", lambda e: e.reduce_max(
                A["negmax4"][:], A["scl4"][:], axis=AX.X, negate=True),
                waits=[(vcs, em.led[vcs])], inc=vcs)
            def lexp(e):
                last = None
                for lh in range(4):
                    last = e.activation(A["pl4"][:, lh, :], A["scl4"][:, lh, :],
                                        AF.Exp, bias=A["negmax4"][:, lh:lh + 1],
                                        accum_out=A["den4"][:, lh:lh + 1])
                return last
            em.step("scalar", lexp, waits=[(vcs, em.led[vcs])], inc=scs)
            em.step("vector", lambda e: e.reciprocal(A["rec4"][:], A["den4"][:]),
                    waits=[(scs, em.led[scs])], inc=vcs)
            em.step("vector", lambda e, ti=ti: e.tensor_scalar(
                A["rec4"][:], A["rec4"][:], A["zl_t"][:, ti % 2, 1:2], None,
                op0=ALU.mult),
                waits=[(vcs, em.led[vcs])], inc=vcs)
            def ldiag(e):
                last = None
                for lh in range(4):
                    last = e.tensor_scalar(A["diag4"][:, lh, :], A["iota_row"][:],
                                           A["pidx"][:, 0:1], A["rec4"][:, lh:lh + 1],
                                           op0=ALU.is_equal, op1=ALU.mult)
                return last
            em.step("vector", ldiag, waits=[(vcs, em.led[vcs])], inc=vcs)
            def lptmm(e):
                last = None
                for lh in range(4):
                    last = e.matmul(PS[lh][0:64, 0:128],
                                    A["pl4"][:, lh, :], A["diag4"][:, lh, :],
                                    start=True, stop=True)
                return last
            em.step("tensor", lptmm, waits=[(vcs, em.led[vcs]),
                                            (scs, em.led[scs])], inc=tcs)
            def lptev(e):
                last = None
                for lh in range(4):
                    last = e.tensor_copy(A["pTl4"][:, lh, :], PS[lh][0:64, 0:128])
                return last
            em.step("vector", lptev, waits=[(tcs, em.led[tcs])], inc=vcs)
            def lomm(e):
                last = None
                for lh in range(4):
                    last = e.matmul(PS[lh][0:64, 0:128],
                                    A["rv_tm"][0:64, lh * 64:(lh + 1) * 64],
                                    A["pTl4"][:, lh, :], start=True, stop=True)
                return last
            em.step("tensor", lomm, waits=[(vcs, em.led[vcs])], inc=tcs)
            def lomev(e):
                last = None
                for lh in range(4):
                    last = e.tensor_copy(A["lat_fm"][:, lh, :], PS[lh][0:64, 0:128])
                return last
            em.step("vector", lomev, waits=[(tcs, em.led[tcs])], inc=vcs)
            if DEV_ATTN < 4 or 20 <= DEV_ATTN < 30:
                continue
            for wv in range(4):
                def rmm(e, wv=wv):
                    last = None
                    for hh in range(4):
                        h = wv * 4 + hh
                        for lh in range(4):
                            last = e.matmul(PS[hh][0:64, 0:128],
                                            A["routw"][:, lh, h * 64:(h + 1) * 64],
                                            A["lat_fm"][:, lh, :],
                                            start=(lh == 0), stop=(lh == 3))
                    return last
                em.step("tensor", rmm, waits=[(vcs, em.led[vcs]),
                                              (scs, em.led[scs])], inc=tcs)
                def revac(e, wv=wv):
                    last = None
                    for hh in range(4):
                        h = wv * 4 + hh
                        last = e.activation(A["remote_fm"][:, h, :],
                                            PS[hh][0:64, 0:128],
                                            AF.Identity, bias=A["routb_col"][:, h:h + 1])
                    return last
                em.step("scalar", revac, waits=[(tcs, em.led[tcs])], inc=scs)
            if 19 <= DEV_ATTN < 30:
                continue
            for wv in range(4):
                def gmm(e, wv=wv):
                    last = None
                    for hh in range(4):
                        h = wv * 4 + hh
                        last = e.matmul(PS[hh][0:1, 0:128],
                                        A["glgr"][0:64, 0:1], A["local_fm"][0:64, h, :],
                                        start=True, stop=False)
                        last = e.matmul(PS[hh][0:1, 0:128],
                                        A["glgr"][0:64, 1:2], A["remote_fm"][0:64, h, :],
                                        start=False, stop=True)
                    return last
                em.step("tensor", gmm, waits=[(vcs, em.led[vcs]),
                                              (scs, em.led[scs])], inc=tcs)
                def gsig(e, wv=wv):
                    last = None
                    for hh in range(4):
                        h = wv * 4 + hh
                        last = e.activation(A["gates_row"][0:1, h * 128:(h + 1) * 128],
                                            PS[hh][0:1, 0:128], AF.Sigmoid)
                    return last
                em.step("scalar", gsig, waits=[(tcs, em.led[tcs])], inc=scs)
            for wv in range(4):
                def gbmm(e, wv=wv):
                    last = None
                    for hh in range(4):
                        h = wv * 4 + hh
                        last = e.matmul(PS[hh][0:64, 0:128],
                                        A["ones_bf"][0:1, 0:64],
                                        A["gates_row"][0:1, h * 128:(h + 1) * 128],
                                        start=True, stop=True)
                    return last
                em.step("tensor", gbmm, waits=[(scs, em.led[scs]),
                                               (vcs, em.led[vcs])], inc=tcs)
                def gbev(e, wv=wv):
                    last = None
                    for hh in range(4):
                        h = wv * 4 + hh
                        last = e.tensor_copy(A["gb_fm"][:, h, :], PS[hh][0:64, 0:128])
                    return last
                em.step("vector", gbev, waits=[(tcs, em.led[tcs])], inc=vcs)
            em.step("vector", lambda e: e.tensor_tensor(
                A["t1"][:].rearrange("p a b -> p (a b)"),
                A["local_fm"][:].rearrange("p a b -> p (a b)"),
                A["remote_fm"][:].rearrange("p a b -> p (a b)"), op=ALU.subtract),
                waits=[(vcs, em.led[vcs]), (scs, em.led[scs])], inc=vcs)
            em.step("vector", lambda e: e.tensor_tensor(
                A["t1"][:].rearrange("p a b -> p (a b)"),
                A["t1"][:].rearrange("p a b -> p (a b)"),
                A["gb_fm"][:].rearrange("p a b -> p (a b)"), op=ALU.mult),
                waits=[(vcs, em.led[vcs])], inc=vcs)
            em.step("vector", lambda e: e.tensor_tensor(
                A["mixed"][:].rearrange("p a b -> p (a b)"),
                A["t1"][:].rearrange("p a b -> p (a b)"),
                A["remote_fm"][:].rearrange("p a b -> p (a b)"), op=ALU.add),
                waits=[(vcs, em.led[vcs])], inc=vcs)
            def s0mm(e):
                last = e.matmul(PS[1][0:128, 0:512], A["ones_bf"][0:1, :],
                                A["outb_row"][0:1, 0:512], start=True, stop=False)
                last = e.matmul(PS[2][0:128, 0:512], A["ones_bf"][0:1, :],
                                A["outb_row"][0:1, 512:1024], start=True, stop=False)
                for h in range(16):
                    last = e.matmul(PS[1][0:128, 0:512], A["mixed"][:, h, :],
                                    A["outw"][:, h, 0:512], start=False, stop=(h == 15))
                    last = e.matmul(PS[2][0:128, 0:512], A["mixed"][:, h, :],
                                    A["outw"][:, h, 512:1024], start=False, stop=(h == 15))
                return last
            em.step("tensor", s0mm, waits=[(vcs, em.led[vcs]), (scs, em.led[scs])], inc=tcs)
            em.dma("sync", A["g0_t"][:], g["g0_hbm"][T], waits=[(vcs, em.led[vcs])], inc=ld)
            xt = A["xf_t"][:, ti % 2, :]
            em.dma("sync", xt, g["x_ext"][(T + 1) * 128:(T + 2) * 128, :],
                   waits=[(vcs, em.led[vcs]), (scs, em.led[scs])], inc=ld)
            em.dma("sync", A["pad_t"][:], g["padmask"][T + 1],
                   waits=[(vcs, em.led[vcs])], inc=ld)
            ld_e = em.led[ld]
            def s0tap(e):
                e.tensor_copy(A["x1_t"][:, 0:512], PS[1][0:128, 0:512])
                return e.tensor_copy(A["x1_t"][:, 512:1024], PS[2][0:128, 0:512])
            em.step("vector", s0tap, waits=[(tcs, em.led[tcs]), (sts, em.led[sts])],
                    inc=vcs)
            em.dma("sync", g["dbg_s0"], A["x1_t"][:], waits=[(vcs, em.led[vcs])], inc=sts)
            def mul1(e, ti=ti):
                e.tensor_tensor(A["m1_t"][:, 0:512], PS[1][0:128, 0:512],
                                A["g0_t"][:, 0:512], op=ALU.mult)
                return e.tensor_tensor(A["m1_t"][:, 512:1024], PS[2][0:128, 0:512],
                                       A["g0_t"][:, 512:1024], op=ALU.mult)
            em.step("vector", mul1, waits=[(tcs, em.led[tcs]), (ld, ld_e)], inc=vcs)
            em.step("vector", lambda e, xt=xt: e.tensor_tensor(
                A["x1_t"][:], xt, A["m1_t"][:], op=ALU.add),
                waits=[(vcs, em.led[vcs]), (sts, em.led[sts])], inc=vcs)
            em.step("vector", lambda e: e.tensor_scalar(
                A["x1_t"][:], A["x1_t"][:], A["pad_t"][:, 0:1], None, op0=ALU.mult),
                waits=[(vcs, em.led[vcs])], inc=vcs)
            em.dma("sync", g["x1_hbm"][T], A["x1_t"][:],
                   waits=[(vcs, em.led[vcs])], inc=sts)
    for nm, buf in (("dbg_local", "local_fm"), ("dbg_remote", "remote_fm"),
                    ("dbg_mixed", "mixed"), ("dbg_gates", "gates_row"),
                    ("dbg_q", "q_fm"), ("dbg_k", "k_fm"), ("dbg_v", "v_tm"),
                    ("dbg_m1", "m1_t"), ("dbg_outw", "outw"),
                    ("dbg_outb", "outb_row"), ("dbg_g0t", "g0_t")):
        em.dma("sync", g[nm], A[buf][:],
               waits=[(vcs, em.led[vcs]), (scs, em.led[scs]), (tcs, em.led[tcs])],
               inc=sts)
    em.barrier()
    pa.close()


def _ph_xn1(nc, em, g, A, PS):
    A = dict(A)
    if DEV_PHASES < 3:
        return
    pa = PhaseAlloc(nc, A)
    sbuf = pa.sbuf
    ld, wd, tcs, vcs, scs, sts = "ld", "wd", "tc", "vc", "sc", "st"
    sbuf("xt2", (128, 2, D), F32)
    sbuf("sq_scratch", (128, D), F32)
    sbuf("xn1_t", (128, D), F32)
    sbuf("zrow", (1, D), F32)
    alloc_stats(pa, "s1")
    em.step("gpsimd", lambda e: e.memset(A["zrow"][:], 0.0), inc="wd")
    zw = em.led["wd"]
    for tgt, row in (("x2fA_hbm", L1F_QA_LO * 128 - 1),
                     ("yA_hbm", L1F_QA_LO * 128 - 1),
                     ("x2fB_hbm", L1F_QB_LO * 128 - 1)):
        em.dma("sync", g[tgt].rearrange("t p d -> (t p) d")[row:row + 1, :],
               A["zrow"][:], waits=[("wd", zw)], inc="st")
    for T in range(NT_OUT):
        xt = A["xt2"][:, T % 2, :]
        em.dma("sync", xt, g["x1_hbm"][T],
               waits=[(vcs, em.led[vcs]), (scs, em.led[scs])], inc=ld)
        emit_stats(em, A, xt, "s1", "eps_1em5", "sq_scratch",
                   extra_waits=[(ld, em.led[ld])])
        em.step("scalar", lambda e, xt=xt: e.activation(
            A["xn1_t"][:], xt, AF.Identity, bias=A["s1_negmr"][:, 0:1],
            scale=A["s1_r"][:, 0:1]),
            waits=[(vcs, em.led[vcs]), (sts, em.led[sts])], inc=scs)
        em.dma("sync", g["xn1_hbm"][T], A["xn1_t"][:],
               waits=[(scs, em.led[scs])], inc=sts)
    em.barrier()
    pa.close()


def _ph_l1f_layer(nc, em, g, A, PS, l, src_hbm, kvlo, qlo, dst_hbm, store_g1, mask_name):
    A = dict(A)
    pa = PhaseAlloc(nc, A)
    sbuf = pa.sbuf
    ld, wd, tcs, vcs, scs, sts = "ld", "wd", "tc", "vc", "sc", "st"
    nkv = NT_OUT - kvlo
    nq = NT_OUT - qlo

    sbuf("wqkv", (128, 8, 1536), BF16)
    sbuf("h1_bf", (128, D), BF16)
    sbuf("h1T", (128, 8, 128), BF16)
    sbuf("qk_32", (128, 2, 512), F32)
    sbuf("ss8", (128, 16), F32)
    sbuf("nr8", (128, 16), F32)
    sbuf("r8", (128, 16), F32)
    sbuf("qf", (128, 2, 512), BF16)
    sbuf("qsb", (128, 2, 64), BF16)
    sbuf("ksb", (128, 2, 64), BF16)
    sbuf("qsrow", (1, 256), BF16)
    sbuf("ropet", (128, 4, 64), BF16)
    sbuf("sq_l1", (128, 2, 512), F32)
    sbuf("ra", (128, 512), F32)
    sbuf("rb", (128, 512), F32)
    sbuf("v_tm", (128, nkv, 512), BF16)
    sbuf("kT", (128, 4, nkv * 128), BF16)
    sbuf("qT", (128, 4, nq * 128), BF16)
    em.dma("gpsimd", A["wqkv"][:],
           g["L1_ffn_wqkv"][l].rearrange("(c p) f -> p c f", p=128), inc=wd)
    em.dma("gpsimd", A["qsrow"][0:1, 0:64],
           g["L1_ffn_q_scale"][l].rearrange("(o d) -> o d", o=1), inc=wd)
    em.dma("gpsimd", A["qsrow"][0:1, 64:128],
           g["L1_ffn_k_scale"][l].rearrange("(o d) -> o d", o=1), inc=wd)
    em.dma("gpsimd", A["qsrow"][0:1, 128:160],
           g["L1_ffn_q_scale"][l][32:64].rearrange("(o d) -> o d", o=1), inc=wd)
    em.dma("gpsimd", A["qsrow"][0:1, 160:192],
           g["L1_ffn_q_scale"][l][0:32].rearrange("(o d) -> o d", o=1), inc=wd)
    em.dma("gpsimd", A["qsrow"][0:1, 192:224],
           g["L1_ffn_k_scale"][l][32:64].rearrange("(o d) -> o d", o=1), inc=wd)
    em.dma("gpsimd", A["qsrow"][0:1, 224:256],
           g["L1_ffn_k_scale"][l][0:32].rearrange("(o d) -> o d", o=1), inc=wd)
    wd0 = em.led[wd]
    def qsb_mm(e):
        e.matmul(PS[0][0:128, 0:64], A["ones_bf"][0:1, :], A["qsrow"][0:1, 0:64],
                 start=True, stop=True)
        e.matmul(PS[1][0:128, 0:64], A["ones_bf"][0:1, :], A["qsrow"][0:1, 128:192],
                 start=True, stop=True)
        e.matmul(PS[2][0:128, 0:64], A["ones_bf"][0:1, :], A["qsrow"][0:1, 64:128],
                 start=True, stop=True)
        return e.matmul(PS[3][0:128, 0:64], A["ones_bf"][0:1, :],
                        A["qsrow"][0:1, 192:256], start=True, stop=True)
    em.step("tensor", qsb_mm, waits=[(wd, wd0), (vcs, em.led[vcs]),
                                     (scs, em.led[scs])], inc=tcs)
    def qsb_ev(e):
        e.tensor_copy(A["qsb"][:, 0, :], PS[0][0:128, 0:64])
        e.tensor_copy(A["qsb"][:, 1, :], PS[1][0:128, 0:64])
        e.tensor_copy(A["ksb"][:, 0, :], PS[2][0:128, 0:64])
        return e.tensor_copy(A["ksb"][:, 1, :], PS[3][0:128, 0:64])
    em.step("vector", qsb_ev, waits=[(tcs, em.led[tcs])], inc=vcs)

    src_rows = src_hbm.rearrange("t p d -> (t p) d")
    for T in range(kvlo, NT_OUT):
        em.dma("gpsimd", A["h1_bf"][:, 0:512],
               src_rows[T * 128:(T + 1) * 128, 0:512],
               waits=[(tcs, em.led[tcs]), (vcs, em.led[vcs])], inc=wd)
        em.dma("gpsimd", A["h1_bf"][:, 512:1024],
               src_rows[max(T * 128 - 1, 0):(T + 1) * 128 - 1 + (1 if T == 0 else 0),
                        512:1024],
               waits=[], inc=wd)
        em.dma("gpsimd", A["ropet"][:, 0, :], g["rope_cos"][T],
               waits=[(vcs, em.led[vcs])], inc=wd)
        em.dma("gpsimd", A["ropet"][:, 1, :], g["rope_sin"][T], waits=[], inc=wd)
        em.dma("gpsimd", A["ropet"][:, 2, :], g["rope_cosi"][T], waits=[], inc=wd)
        em.dma("gpsimd", A["ropet"][:, 3, :], g["rope_sini"][T], waits=[], inc=wd)
        ld_t = em.led[wd]; wd_t = em.led[wd]
        for cc in range(8):
            bank = PS[4 + (cc % 2)]
            em.step("tensor", lambda e, cc=cc, bank=bank: e.matmul(
                bank[0:128, 0:128], A["h1_bf"][:, cc * 128:(cc + 1) * 128],
                A["identbf"][:], start=True, stop=True),
                waits=[(wd, wd_t), (vcs, em.led[vcs]), (scs, em.led[scs])], inc=tcs)
            em.step("vector", lambda e, cc=cc, bank=bank: e.tensor_copy(
                A["h1T"][:, cc, :], bank[0:128, 0:128]),
                waits=[(tcs, em.led[tcs])], inc=vcs)
        def qkvmm(e):
            last = None
            for cc in range(8):
                last = e.matmul(PS[0][0:128, 0:512], A["h1T"][:, cc, :],
                                A["wqkv"][:, cc, 0:512], start=(cc == 0),
                                stop=(cc == 7))
                last = e.matmul(PS[1][0:128, 0:512], A["h1T"][:, cc, :],
                                A["wqkv"][:, cc, 512:1024], start=(cc == 0),
                                stop=(cc == 7))
                last = e.matmul(PS[2][0:128, 0:512], A["h1T"][:, cc, :],
                                A["wqkv"][:, cc, 1024:1536], start=(cc == 0),
                                stop=(cc == 7))
            return last
        em.step("tensor", qkvmm, waits=[(vcs, em.led[vcs]), (scs, em.led[scs])],
                inc=tcs)
        em.step("vector", lambda e, T=T: e.tensor_copy(
            A["v_tm"][:, T - kvlo, :], PS[2][0:128, 0:512]),
            waits=[(tcs, em.led[tcs])], inc=vcs)
        em.step("vector", lambda e: e.tensor_copy(A["qk_32"][:, 0, :],
                                                  PS[0][0:128, 0:512]),
                waits=[], inc=vcs)
        em.step("vector", lambda e: e.tensor_copy(A["qk_32"][:, 1, :],
                                                  PS[1][0:128, 0:512]),
                waits=[], inc=vcs)
        def ssq(e):
            e.tensor_tensor(A["sq_l1"][:, 0, :], A["qk_32"][:, 0, :],
                            A["qk_32"][:, 0, :], op=ALU.mult)
            return e.tensor_tensor(A["sq_l1"][:, 1, :], A["qk_32"][:, 1, :],
                                   A["qk_32"][:, 1, :], op=ALU.mult)
        em.step("vector", ssq, waits=[(vcs, em.led[vcs])], inc=vcs)
        em.step("vector", lambda e: e.reduce_sum(
            A["ss8"][:], A["sq_l1"][:].rearrange("p a (h c) -> p (a h) c", c=64),
            axis=AX.X),
            waits=[(vcs, em.led[vcs])], inc=vcs)
        em.step("scalar", lambda e: e.activation(A["nr8"][:], A["ss8"][:], AF.Sqrt),
                waits=[(vcs, em.led[vcs])], inc=scs)
        em.step("vector", lambda e: e.tensor_scalar(
            A["nr8"][:], A["nr8"][:], 1e-12, None, op0=ALU.max),
            waits=[(scs, em.led[scs])], inc=vcs)
        em.step("scalar", lambda e: None, waits=[(vcs, em.led[vcs])], inc=scs)
        em.step("vector", lambda e: e.reciprocal(A["r8"][:], A["nr8"][:]),
                waits=[(scs, em.led[scs])], inc=vcs)
        em.step("scalar", lambda e: None, waits=[(vcs, em.led[vcs])], inc=scs)
        def qhat(e):
            e.tensor_tensor(A["qk_32"][:, 0, :].rearrange("p (h c) -> p h c", c=64),
                            A["qk_32"][:, 0, :].rearrange("p (h c) -> p h c", c=64),
                            A["r8"][:, 0:8].rearrange("p (h c) -> p h c", c=1)
                            .broadcast_to([128, 8, 64]), op=ALU.mult)
            return e.tensor_tensor(
                A["qk_32"][:, 1, :].rearrange("p (h c) -> p h c", c=64),
                A["qk_32"][:, 1, :].rearrange("p (h c) -> p h c", c=64),
                A["r8"][:, 8:16].rearrange("p (h c) -> p h c", c=1)
                .broadcast_to([128, 8, 64]), op=ALU.mult)
        em.step("vector", qhat, waits=[(scs, em.led[scs]), (vcs, em.led[vcs])], inc=vcs)
        def qsmul(e):
            e.tensor_tensor(A["qk_32"][:, 0, :].rearrange("p (h c) -> p h c", c=64),
                            A["qk_32"][:, 0, :].rearrange("p (h c) -> p h c", c=64),
                            A["qsb"][:, 0:1, :].broadcast_to([128, 8, 64]), op=ALU.mult)
            return e.tensor_tensor(
                A["qk_32"][:, 1, :].rearrange("p (h c) -> p h c", c=64),
                A["qk_32"][:, 1, :].rearrange("p (h c) -> p h c", c=64),
                A["ksb"][:, 0:1, :].broadcast_to([128, 8, 64]), op=ALU.mult)
        em.step("vector", qsmul, waits=[(vcs, em.led[vcs])], inc=vcs)
        for srcq, ct, st_, dst in ((0, 0, 1, 0), (1, 2, 3, 1)):
            def rope1(e, srcq=srcq, ct=ct, st_=st_, dst=dst, ld_t=ld_t):
                qv = A["qk_32"][:, srcq, :].rearrange("p (h c) -> p h c", c=64)
                e.tensor_tensor(A["ra"][:].rearrange("p (h c) -> p h c", c=64),
                                qv, A["ropet"][:, ct:ct + 1, :]
                                .broadcast_to([128, 8, 64]), op=ALU.mult)
                e.tensor_tensor(
                    A["rb"][:].rearrange("p (h c) -> p h c", c=64)[:, :, 0:32],
                    qv[:, :, 32:64],
                    A["ropet"][:, st_:st_ + 1, 0:32].broadcast_to([128, 8, 32]),
                    op=ALU.mult)
                return e.tensor_tensor(
                    A["rb"][:].rearrange("p (h c) -> p h c", c=64)[:, :, 32:64],
                    qv[:, :, 0:32],
                    A["ropet"][:, st_:st_ + 1, 32:64].broadcast_to([128, 8, 32]),
                    op=ALU.mult)
            em.step("vector", rope1, waits=[(wd, ld_t), (vcs, em.led[vcs])], inc=vcs)
            em.step("vector", lambda e, dst=dst: e.tensor_tensor(
                A["qf"][:, dst, :], A["ra"][:], A["rb"][:], op=ALU.add),
                waits=[(vcs, em.led[vcs])], inc=vcs)
        for cc in range(4):
            bank = PS[4 + (cc % 2)]
            em.step("tensor", lambda e, cc=cc, bank=bank: e.matmul(
                bank[0:128, 0:128], A["qf"][:, 1, cc * 128:(cc + 1) * 128],
                A["identbf"][:], start=True, stop=True),
                waits=[(vcs, em.led[vcs]), (scs, em.led[scs])], inc=tcs)
            em.step("vector", lambda e, cc=cc, bank=bank, T=T: e.tensor_copy(
                A["kT"][:, cc, (T - kvlo) * 128:(T - kvlo + 1) * 128],
                bank[0:128, 0:128]),
                waits=[(tcs, em.led[tcs])], inc=vcs)
        if T >= qlo:
            for cc in range(4):
                bank = PS[4 + (cc % 2)]
                em.step("tensor", lambda e, cc=cc, bank=bank: e.matmul(
                    bank[0:128, 0:128], A["qf"][:, 0, cc * 128:(cc + 1) * 128],
                    A["identbf"][:], start=True, stop=True),
                    waits=[(vcs, em.led[vcs]), (scs, em.led[scs])], inc=tcs)
                em.step("vector", lambda e, cc=cc, bank=bank, T=T: e.tensor_copy(
                    A["qT"][:, cc, (T - qlo) * 128:(T - qlo + 1) * 128],
                    bank[0:128, 0:128]),
                    waits=[(tcs, em.led[tcs])], inc=vcs)

    sbuf("mskf", (128, 1024), F32)
    sbuf("sc_l1", (128, 1024), F32)
    sbuf("negmax1", (128, 1), F32)
    sbuf("den1", (128, 1), F32)
    sbuf("rec1", (128, 1), F32)
    sbuf("diag1", (128, 128), BF16)
    sbuf("p_l1", (128, 1024), BF16)
    sbuf("pT_l1", (128, 8, 128), BF16)
    sbuf("o_fm", (64, 8, 128), BF16)
    sbuf("wo_sb", (64, 8, D), BF16)
    sbuf("xr_t", (128, D), F32)
    sbuf("x2_t", (128, D), F32)
    em.dma("gpsimd", A["wo_sb"][:],
           g["L1_ffn_wo"][l].rearrange("(h p) f -> p h f", p=64), inc=wd)
    a0_mod = (2048 - HALO) % 512
    for T in range(qlo, NT_OUT):
        em.dma("sync", A["mskf"][:], g[mask_name][T - qlo],
               waits=[(vcs, em.led[vcs])], inc=ld)
        ld_t = em.led[ld]
        absmod = (a0_mod + 128 * T) % 512
        kst = max(T - (absmod // 128) - 4, kvlo)
        kb = kst - kvlo
        assert 0 <= kb <= (NT_OUT - kvlo) - 8, (T, kst, kvlo)
        for h in range(8):
            po = (h % 2) * 64
            sl = h // 2
            def smm(e, h=h, po=po, sl=sl, T=T, kb=kb):
                e.matmul(PS[0][0:128, 0:512], A["qT"][po:po + 64, sl,
                         (T - qlo) * 128:(T - qlo + 1) * 128],
                         A["kT"][po:po + 64, sl, kb * 128:(kb + 4) * 128],
                         start=True, stop=True)
                return e.matmul(PS[1][0:128, 0:512],
                                A["qT"][po:po + 64, sl,
                                        (T - qlo) * 128:(T - qlo + 1) * 128],
                                A["kT"][po:po + 64, sl, (kb + 4) * 128:(kb + 8) * 128],
                                start=True, stop=True)
            em.step("tensor", smm, waits=[(vcs, em.led[vcs]), (scs, em.led[scs])],
                    inc=tcs)
            def smask(e):
                e.scalar_tensor_tensor(A["sc_l1"][:, 0:512], PS[0][0:128, 0:512],
                                       8.0, A["mskf"][:, 0:512],
                                       op0=ALU.mult, op1=ALU.add)
                return e.scalar_tensor_tensor(A["sc_l1"][:, 512:1024],
                                              PS[1][0:128, 0:512], 8.0,
                                              A["mskf"][:, 512:1024],
                                              op0=ALU.mult, op1=ALU.add)
            em.step("vector", smask, waits=[(tcs, em.led[tcs]), (ld, ld_t)], inc=vcs)
            em.step("vector", lambda e: e.reduce_max(
                A["negmax1"][:], A["sc_l1"][:], axis=AX.X, negate=True),
                waits=[(vcs, em.led[vcs])], inc=vcs)
            em.step("scalar", lambda e: e.activation(
                A["p_l1"][:], A["sc_l1"][:], AF.Exp, bias=A["negmax1"][:, 0:1],
                accum_out=A["den1"][:]),
                waits=[(vcs, em.led[vcs])], inc=scs)
            em.step("vector", lambda e: e.reciprocal(A["rec1"][:], A["den1"][:]),
                    waits=[(scs, em.led[scs])], inc=vcs)
            em.step("vector", lambda e: e.tensor_scalar(
                A["diag1"][:], A["iota_row"][:], A["pidx"][:, 0:1],
                A["rec1"][:, 0:1], op0=ALU.is_equal, op1=ALU.mult),
                waits=[(vcs, em.led[vcs])], inc=vcs)
            for half in range(2):
                def ptmm(e, half=half):
                    last = None
                    for kc in range(4):
                        last = e.matmul(PS[2 + kc][0:128, 0:128],
                                        A["p_l1"][:, half * 512 + kc * 128:
                                                  half * 512 + (kc + 1) * 128],
                                        A["diag1"][:], start=True, stop=True)
                    return last
                em.step("tensor", ptmm, waits=[(vcs, em.led[vcs]),
                                               (scs, em.led[scs])], inc=tcs)
                def ptev(e, half=half):
                    last = None
                    for kc in range(4):
                        last = e.tensor_copy(A["pT_l1"][:, half * 4 + kc, :],
                                             PS[2 + kc][0:128, 0:128])
                    return last
                em.step("vector", ptev, waits=[(tcs, em.led[tcs])], inc=vcs)
            def avmm(e, h=h, kb=kb):
                last = None
                for kc in range(8):
                    last = e.matmul(PS[0][0:64, 0:128],
                                    A["v_tm"][:, kb + kc, h * 64:(h + 1) * 64],
                                    A["pT_l1"][:, kc, :],
                                    start=(kc == 0), stop=(kc == 7))
                return last
            em.step("tensor", avmm, waits=[(vcs, em.led[vcs])], inc=tcs)
            em.step("vector", lambda e, h=h: e.tensor_copy(
                A["o_fm"][:, h, :], PS[0][0:64, 0:128]),
                waits=[(tcs, em.led[tcs])], inc=vcs)
        em.dma("sync", A["xr_t"][:], src_hbm[T],
               waits=[(vcs, em.led[vcs]), (scs, em.led[scs])], inc=ld)
        ld_x = em.led[ld]
        def owmm(e):
            last = None
            for h in range(8):
                last = e.matmul(PS[1][0:128, 0:512], A["o_fm"][:, h, :],
                                A["wo_sb"][:, h, 0:512], start=(h == 0),
                                stop=(h == 7))
                last = e.matmul(PS[2][0:128, 0:512], A["o_fm"][:, h, :],
                                A["wo_sb"][:, h, 512:1024], start=(h == 0),
                                stop=(h == 7))
            return last
        em.step("tensor", owmm, waits=[(vcs, em.led[vcs]), (scs, em.led[scs])],
                inc=tcs)
        def x2ev(e):
            e.tensor_tensor(A["x2_t"][:, 0:512], PS[1][0:128, 0:512],
                            A["xr_t"][:, 0:512], op=ALU.add)
            return e.tensor_tensor(A["x2_t"][:, 512:1024], PS[2][0:128, 0:512],
                                   A["xr_t"][:, 512:1024], op=ALU.add)
        em.step("vector", x2ev, waits=[(tcs, em.led[tcs]), (ld, ld_x),
                                       (sts, em.led[sts])], inc=vcs)
        em.dma("sync", g["x2fA_hbm" if l == 0 else "x2fB_hbm"][T], A["x2_t"][:],
               waits=[(vcs, em.led[vcs])], inc=sts)
    em.barrier()
    pa.close()

    pa = PhaseAlloc(nc, A)
    sbuf = pa.sbuf
    x2h = g["x2fA_hbm" if l == 0 else "x2fB_hbm"]
    x2rows = x2h.rearrange("t p d -> (t p) d")
    sbuf("w1", (128, 8, 5460), BF16)
    sbuf("w2", (128, 22, D), BF16)
    sbuf("b1row", (1, 5460), BF16)
    sbuf("b2row", (1, D), BF16)
    sbuf("gbc", (128, D), F32)
    sbuf("bbc", (128, D), F32)
    sbuf("h2_32", (128, D), F32)
    sbuf("h2n", (128, D), BF16)
    sbuf("h2nT", (128, 8, 128), BF16)
    sbuf("sq_scratch", (128, D), F32)
    alloc_stats(pa, "sf")
    sbuf("ag", (128, 5504), BF16)
    sbuf("i2T", (128, 22, 128), BF16)
    sbuf("xr2", (128, D), F32)
    sbuf("y2", (128, D), F32)
    sbuf("g1t", (128, D), BF16)
    em.dma("gpsimd", A["w1"][:],
           g["L1_ffn_w1"][l].rearrange("(c p) f -> p c f", p=128), inc=wd)
    em.dma("gpsimd", A["w2"][:],
           g["L1_ffn_w2_pad"][l].rearrange("(c p) f -> p c f", p=128), inc=wd)
    em.dma("gpsimd", A["b1row"][:], g["L1_ffn_b1"][l].rearrange("(o d) -> o d", o=1),
           inc=wd)
    em.dma("gpsimd", A["b2row"][:], g["L1_ffn_b2"][l].rearrange("(o d) -> o d", o=1),
           inc=wd)
    em.dma("sync", A["gbc"][:], g["l1f_gbc"][l], inc=ld)
    em.dma("sync", A["bbc"][:], g["l1f_bbc"][l], inc=ld)
    em.step("gpsimd", lambda e: e.memset(A["i2T"][:, 21, :], 0.0), inc=wd)
    wd1 = em.led[wd]; ld1 = em.led[ld]
    for T in range(qlo, NT_OUT):
        em.dma("sync", A["h2_32"][:, 0:512], x2rows[T * 128:(T + 1) * 128, 0:512],
               waits=[(vcs, em.led[vcs]), (scs, em.led[scs])], inc=ld)
        em.dma("sync", A["h2_32"][:, 512:1024],
               x2rows[T * 128 - 1:(T + 1) * 128 - 1, 512:1024],
               waits=[], inc=ld)
        emit_stats(em, A, A["h2_32"][:], "sf", "eps_1em5", "sq_scratch",
                   extra_waits=[(ld, em.led[ld]), (vcs, em.led[vcs])])
        em.step("scalar", lambda e: e.activation(
            A["sq_scratch"][:], A["h2_32"][:], AF.Identity,
            bias=A["sf_negmr"][:, 0:1], scale=A["sf_r"][:, 0:1]),
            waits=[(vcs, em.led[vcs])], inc=scs)
        em.step("vector", lambda e: e.tensor_tensor(
            A["sq_scratch"][:], A["sq_scratch"][:], A["gbc"][:], op=ALU.mult),
            waits=[(scs, em.led[scs]), (ld, ld1), (vcs, em.led[vcs])], inc=vcs)
        em.step("vector", lambda e: e.tensor_tensor(
            A["h2n"][:], A["sq_scratch"][:], A["bbc"][:], op=ALU.add),
            waits=[(vcs, em.led[vcs])], inc=vcs)
        for cc in range(8):
            bank = PS[4 + (cc % 2)]
            em.step("tensor", lambda e, cc=cc, bank=bank: e.matmul(
                bank[0:128, 0:128], A["h2n"][:, cc * 128:(cc + 1) * 128],
                A["identbf"][:], start=True, stop=True),
                waits=[(vcs, em.led[vcs]), (scs, em.led[scs]), (wd, wd1)], inc=tcs)
            em.step("vector", lambda e, cc=cc, bank=bank: e.tensor_copy(
                A["h2nT"][:, cc, :], bank[0:128, 0:128]),
                waits=[(tcs, em.led[tcs])], inc=vcs)
        for jg in range(11):
            j0 = jg * 512
            jn = min(512, 5460 - j0)
            def ffmm(e, j0=j0, jn=jn):
                last = e.matmul(PS[0][0:128, 0:jn], A["ones_bf"][0:1, :],
                                A["b1row"][0:1, j0:j0 + jn], start=True, stop=False)
                for cc in range(8):
                    last = e.matmul(PS[0][0:128, 0:jn], A["h2nT"][:, cc, :],
                                    A["w1"][:, cc, j0:j0 + jn],
                                    start=False, stop=(cc == 7))
                return last
            em.step("tensor", ffmm, waits=[(vcs, em.led[vcs]), (scs, em.led[scs])],
                    inc=tcs)
            def ffev(e, j0=j0, jn=jn):
                last = None
                if j0 < 2730:
                    an = min(jn, 2730 - j0)
                    last = e.copy(A["ag"][:, j0:j0 + an], PS[0][0:128, 0:an])
                    if an < jn:
                        last = e.activation(A["ag"][:, 2752:2752 + (jn - an)],
                                            PS[0][0:128, an:jn], AF.Gelu)
                else:
                    go = j0 - 2730
                    last = e.activation(A["ag"][:, 2752 + go:2752 + go + jn],
                                        PS[0][0:128, 0:jn], AF.Gelu)
                return last
            em.step("scalar", ffev, waits=[(tcs, em.led[tcs])], inc=scs)
        em.step("vector", lambda e: e.tensor_tensor(
            A["ag"][:, 0:2730], A["ag"][:, 0:2730], A["ag"][:, 2752:5482],
            op=ALU.mult),
            waits=[(scs, em.led[scs]), (vcs, em.led[vcs])], inc=vcs)
        for cc in range(22):
            w = 128 if cc < 21 else 42
            bank = PS[4 + (cc % 2)]
            em.step("tensor", lambda e, cc=cc, w=w, bank=bank: e.matmul(
                bank[0:w, 0:128], A["ag"][:, cc * 128:cc * 128 + w],
                A["identbf"][:], start=True, stop=True),
                waits=[(vcs, em.led[vcs]), (scs, em.led[scs])], inc=tcs)
            em.step("vector", lambda e, cc=cc, w=w, bank=bank: e.tensor_copy(
                A["i2T"][0:w, cc, :], bank[0:w, 0:128]),
                waits=[(tcs, em.led[tcs])], inc=vcs)
        em.dma("sync", A["xr2"][:], x2h[T],
               waits=[(vcs, em.led[vcs]), (scs, em.led[scs])], inc=ld)
        ld_x = em.led[ld]
        def w2mm(e):
            last = e.matmul(PS[1][0:128, 0:512], A["ones_bf"][0:1, :],
                            A["b2row"][0:1, 0:512], start=True, stop=False)
            last = e.matmul(PS[2][0:128, 0:512], A["ones_bf"][0:1, :],
                            A["b2row"][0:1, 512:1024], start=True, stop=False)
            for cc in range(22):
                last = e.matmul(PS[1][0:128, 0:512], A["i2T"][:, cc, :],
                                A["w2"][:, cc, 0:512], start=False, stop=(cc == 21))
                last = e.matmul(PS[2][0:128, 0:512], A["i2T"][:, cc, :],
                                A["w2"][:, cc, 512:1024], start=False, stop=(cc == 21))
            return last
        em.step("tensor", w2mm, waits=[(vcs, em.led[vcs]), (scs, em.led[scs])],
                inc=tcs)
        def yev(e):
            e.tensor_tensor(A["y2"][:, 0:512], PS[1][0:128, 0:512],
                            A["xr2"][:, 0:512], op=ALU.add)
            return e.tensor_tensor(A["y2"][:, 512:1024], PS[2][0:128, 0:512],
                                   A["xr2"][:, 512:1024], op=ALU.add)
        em.step("vector", yev, waits=[(tcs, em.led[tcs]), (ld, ld_x),
                                      (sts, em.led[sts])], inc=vcs)
        if store_g1:
            em.step("scalar", lambda e: e.activation(A["g1t"][:], A["y2"][:],
                                                     AF.Silu),
                    waits=[(vcs, em.led[vcs]), (sts, em.led[sts])], inc=scs)
            em.dma("sync", g["g1_hbm"][T], A["g1t"][:],
                   waits=[(scs, em.led[scs])], inc=sts)
        else:
            em.dma("sync", dst_hbm[T], A["y2"][:],
                   waits=[(vcs, em.led[vcs])], inc=sts)
    em.barrier()
    pa.close()


def _ph_l1a(nc, em, g, A, PS):
    A = dict(A)
    ld, wd, tcs, vcs, scs, sts = "ld", "wd", "tc", "vc", "sc", "st"
    NKV = NT_OUT - L1A_KV_LO
    NQ = NT_OUT - L1A_Q_LO

    pa = PhaseAlloc(nc, A)
    sbuf = pa.sbuf
    sbuf("xt", (128, D), F32)
    sbuf("q_fm", (128, 8, 16 * 128), BF16)
    sbuf("k_fm", (128, 8, 17 * 128), BF16)
    sbuf("v_tm", (128, 17, D), BF16)
    pa_w = PhaseAlloc(nc, A)
    sbufw = pa_w.sbuf
    sbufw("wqkv", (128, 8, 3072), BF16)
    sbufw("g1bc", (128, D), F32); sbufw("b1bc", (128, D), F32)
    sbufw("qb_col", (128, 8), F32)
    sbufw("kv_rows", (1, 2048), BF16)
    sbufw("sq_scratch", (128, D), F32)
    alloc_stats(pa_w, "sa")
    sbufw("h1f", (128, D), BF16)
    sbufw("h1T", (128, 8, 128), BF16)
    sbufw("ktv_bf", (128, D), BF16)
    sbufw("pad_t", (128, 1), F32)
    em.dma("gpsimd", A["wqkv"][:],
           g["L1_attn_w_qkv"].rearrange("(c p) f -> p c f", p=128), inc=wd)
    em.dma("sync", A["g1bc"][:], g["l1a_g1bc"], inc=ld)
    em.dma("sync", A["b1bc"][:], g["l1a_b1bc"], inc=ld)
    em.dma("sync", A["qb_col"][:], g["L1_attn_b_qkv"][0:1024]
           .rearrange("(s p) -> p s", p=128), inc=ld, slow=True)
    em.dma("gpsimd", A["kv_rows"][0:1, 0:1024],
           g["L1_attn_b_qkv"][1024:2048].rearrange("(o d) -> o d", o=1), inc=wd)
    em.dma("gpsimd", A["kv_rows"][0:1, 1024:2048],
           g["L1_attn_b_qkv"][2048:3072].rearrange("(o d) -> o d", o=1), inc=wd)
    wd0 = em.led[wd]; ld0 = em.led[ld]
    for T in range(L1A_KV_LO, NT_OUT):
        em.dma("sync", A["xt"][:], g["xn1_hbm"][T],
               waits=[(vcs, em.led[vcs]), (scs, em.led[scs])], inc=ld)
        em.dma("sync", A["pad_t"][:], g["padmask"][T + 1],
               waits=[], inc=ld)
        emit_stats(em, A, A["xt"][:], "sa", "eps_1em5", "sq_scratch",
                   extra_waits=[(ld, em.led[ld])])
        em.step("scalar", lambda e: e.activation(
            A["sq_scratch"][:], A["xt"][:], AF.Identity,
            bias=A["sa_negmr"][:, 0:1], scale=A["sa_r"][:, 0:1]),
            waits=[(vcs, em.led[vcs])], inc=scs)
        em.step("vector", lambda e: e.tensor_tensor(
            A["sq_scratch"][:], A["sq_scratch"][:], A["g1bc"][:], op=ALU.mult),
            waits=[(scs, em.led[scs]), (ld, ld0), (vcs, em.led[vcs])], inc=vcs)
        em.step("vector", lambda e: e.tensor_tensor(
            A["h1f"][:], A["sq_scratch"][:], A["b1bc"][:], op=ALU.add),
            waits=[(vcs, em.led[vcs])], inc=vcs)
        for cc in range(8):
            bank = PS[4 + (cc % 2)]
            em.step("tensor", lambda e, cc=cc, bank=bank: e.matmul(
                bank[0:128, 0:128], A["h1f"][:, cc * 128:(cc + 1) * 128],
                A["identbf"][:], start=True, stop=True),
                waits=[(vcs, em.led[vcs]), (scs, em.led[scs]), (wd, wd0)], inc=tcs)
            em.step("vector", lambda e, cc=cc, bank=bank: e.tensor_copy(
                A["h1T"][:, cc, :], bank[0:128, 0:128]),
                waits=[(tcs, em.led[tcs])], inc=vcs)
        for sub in range(8):
            def qmm(e, sub=sub):
                last = None
                for cc in range(8):
                    last = e.matmul(PS[0][0:128, 0:128],
                                    A["wqkv"][:, cc, sub * 128:(sub + 1) * 128],
                                    A["h1T"][:, cc, :], start=(cc == 0), stop=(cc == 7))
                return last
            em.step("tensor", qmm, waits=[(vcs, em.led[vcs]), (scs, em.led[scs])],
                    inc=tcs)
            em.step("scalar", lambda e, sub=sub, T=T: e.activation(
                A["q_fm"][:, sub, (T - L1A_Q_LO) * 128:(T - L1A_Q_LO + 1) * 128]
                if T >= L1A_Q_LO else A["ktv_bf"][:, 0:128],
                PS[0][0:128, 0:128], AF.Identity, bias=A["qb_col"][:, sub:sub + 1]),
                waits=[(tcs, em.led[tcs])], inc=scs)
        def kmm(e):
            last = e.matmul(PS[1][0:128, 0:512], A["ones_bf"][0:1, :],
                            A["kv_rows"][0:1, 0:512], start=True, stop=False)
            last = e.matmul(PS[2][0:128, 0:512], A["ones_bf"][0:1, :],
                            A["kv_rows"][0:1, 512:1024], start=True, stop=False)
            for cc in range(8):
                last = e.matmul(PS[1][0:128, 0:512], A["h1T"][:, cc, :],
                                A["wqkv"][:, cc, 1024:1536], start=False,
                                stop=(cc == 7))
                last = e.matmul(PS[2][0:128, 0:512], A["h1T"][:, cc, :],
                                A["wqkv"][:, cc, 1536:2048], start=False,
                                stop=(cc == 7))
            return last
        em.step("tensor", kmm, waits=[(vcs, em.led[vcs]), (scs, em.led[scs])], inc=tcs)
        def kev(e):
            e.tensor_scalar(A["ktv_bf"][:, 0:512], PS[1][0:128, 0:512],
                            A["pad_t"][:, 0:1], None, op0=ALU.mult)
            return e.tensor_scalar(A["ktv_bf"][:, 512:1024], PS[2][0:128, 0:512],
                                   A["pad_t"][:, 0:1], None, op0=ALU.mult)
        em.step("vector", kev, waits=[(tcs, em.led[tcs])], inc=vcs)
        for cc in range(8):
            bank = PS[4 + (cc % 2)]
            em.step("tensor", lambda e, cc=cc, bank=bank: e.matmul(
                bank[0:128, 0:128], A["ktv_bf"][:, cc * 128:(cc + 1) * 128],
                A["identbf"][:], start=True, stop=True),
                waits=[(vcs, em.led[vcs]), (scs, em.led[scs])], inc=tcs)
            em.step("vector", lambda e, cc=cc, bank=bank, T=T: e.tensor_copy(
                A["k_fm"][:, cc, (T - L1A_KV_LO) * 128:(T - L1A_KV_LO + 1) * 128],
                bank[0:128, 0:128]),
                waits=[(tcs, em.led[tcs])], inc=vcs)
        def vmm(e):
            last = e.matmul(PS[1][0:128, 0:512], A["ones_bf"][0:1, :],
                            A["kv_rows"][0:1, 1024:1536], start=True, stop=False)
            last = e.matmul(PS[2][0:128, 0:512], A["ones_bf"][0:1, :],
                            A["kv_rows"][0:1, 1536:2048], start=True, stop=False)
            for cc in range(8):
                last = e.matmul(PS[1][0:128, 0:512], A["h1T"][:, cc, :],
                                A["wqkv"][:, cc, 2048:2560], start=False,
                                stop=(cc == 7))
                last = e.matmul(PS[2][0:128, 0:512], A["h1T"][:, cc, :],
                                A["wqkv"][:, cc, 2560:3072], start=False,
                                stop=(cc == 7))
            return last
        em.step("tensor", vmm, waits=[(vcs, em.led[vcs]), (scs, em.led[scs])], inc=tcs)
        def vev(e, T=T):
            e.tensor_scalar(A["v_tm"][:, T - L1A_KV_LO, 0:512], PS[1][0:128, 0:512],
                            A["pad_t"][:, 0:1], None, op0=ALU.mult)
            return e.tensor_scalar(A["v_tm"][:, T - L1A_KV_LO, 512:1024],
                                   PS[2][0:128, 0:512], A["pad_t"][:, 0:1], None,
                                   op0=ALU.mult)
        em.step("vector", vev, waits=[(tcs, em.led[tcs])], inc=vcs)

    em.barrier()
    pa_w.close()
    sbuf("mska", (64, 128), F32)
    sbuf("wout", (64, 16, D), BF16)
    sbuf("bout_row", (1, D), BF16)
    sbuf("v_sh", (128, D), BF16)
    sbuf("sca", (64, 4, 128), F32)
    sbuf("negmaxa", (64, 4), F32)
    sbuf("dena", (64, 4), F32)
    sbuf("reca", (64, 4), F32)
    sbuf("diaga", (64, 4, 64), BF16)
    sbuf("pa4", (64, 4, 128), BF16)
    sbuf("pTa", (128, 4, 64), BF16)
    sbuf("ob_fm", (64, 16, 128), BF16)
    sbuf("xm_t", (128, D), F32)
    em.dma("sync", A["mska"][:], g["mask_l1a"], inc=ld)
    em.dma("gpsimd", A["wout"][:],
           g["L1_attn_w_out"].rearrange("(h p) f -> p h f", p=64), inc=wd)
    em.dma("gpsimd", A["bout_row"][:],
           g["L1_attn_b_out"].rearrange("(o d) -> o d", o=1), inc=wd)
    ld_b = em.led[ld]; wd_b = em.led[wd]
    for T in range(L1A_Q_LO, NT_OUT):
        kvi = T - L1A_KV_LO
        em.dma("sync", A["v_sh"][0:64, :], A["v_tm"][64:128, kvi - 1, :],
               waits=[(vcs, em.led[vcs]), (tcs, em.led[tcs])], inc=ld)
        em.dma("sync", A["v_sh"][64:128, :], A["v_tm"][0:64, kvi, :],
               waits=[], inc=ld)
        ld_v = em.led[ld]
        for wv in range(8):
            hpair = (wv * 2, wv * 2 + 1)
            def bsmm(e, hpair=hpair, T=T, kvi=kvi):
                last = None
                for i2, h in enumerate(hpair):
                    po = (h % 2) * 64
                    sl = h // 2
                    for b in range(2):
                        qcol = (T - L1A_Q_LO) * 128 + b * 64
                        kcol = kvi * 128 + b * 64 - 64
                        last = e.matmul(
                            PS[i2 * 2 + b][0:64, 0:128],
                            A["q_fm"][po:po + 64, sl, qcol:qcol + 64],
                            A["k_fm"][po:po + 64, sl, kcol:kcol + 128],
                            start=True, stop=True)
                return last
            em.step("tensor", bsmm, waits=[(vcs, em.led[vcs]), (scs, em.led[scs])],
                    inc=tcs)
            def bmask(e):
                last = None
                for u in range(4):
                    last = e.scalar_tensor_tensor(
                        A["sca"][:, u, :], PS[u][0:64, 0:128], 0.125,
                        A["mska"][:], op0=ALU.mult, op1=ALU.add)
                return last
            em.step("vector", bmask, waits=[(tcs, em.led[tcs]), (ld, ld_b)], inc=vcs)
            em.step("vector", lambda e: e.reduce_max(
                A["negmaxa"][:], A["sca"][:], axis=AX.X, negate=True),
                waits=[(vcs, em.led[vcs])], inc=vcs)
            def bexp(e):
                last = None
                for u in range(4):
                    last = e.activation(A["pa4"][:, u, :], A["sca"][:, u, :],
                                        AF.Exp, bias=A["negmaxa"][:, u:u + 1],
                                        accum_out=A["dena"][:, u:u + 1])
                return last
            em.step("scalar", bexp, waits=[(vcs, em.led[vcs])], inc=scs)
            em.step("vector", lambda e: e.reciprocal(A["reca"][:], A["dena"][:]),
                    waits=[(scs, em.led[scs])], inc=vcs)
            def bdiag(e):
                last = None
                for u in range(4):
                    last = e.tensor_scalar(A["diaga"][:, u, :],
                                           A["iota_row"][0:64, 0:64],
                                           A["pidx"][0:64, 0:1],
                                           A["reca"][:, u:u + 1],
                                           op0=ALU.is_equal, op1=ALU.mult)
                return last
            em.step("vector", bdiag, waits=[(vcs, em.led[vcs])], inc=vcs)
            def bptmm(e):
                last = None
                for u in range(4):
                    last = e.matmul(PS[u][0:128, 0:64], A["pa4"][:, u, :],
                                    A["diaga"][:, u, :], start=True, stop=True)
                return last
            em.step("tensor", bptmm, waits=[(vcs, em.led[vcs]),
                                            (scs, em.led[scs])], inc=tcs)
            def bptev(e):
                last = None
                for u in range(4):
                    last = e.tensor_copy(A["pTa"][:, u, :], PS[u][0:128, 0:64])
                return last
            em.step("vector", bptev, waits=[(tcs, em.led[tcs])], inc=vcs)
            def bavmm(e, hpair=hpair, kvi=kvi):
                last = None
                for i2, h in enumerate(hpair):
                    for b in range(2):
                        vsl = (A["v_sh"][:, h * 64:(h + 1) * 64] if b == 0
                               else A["v_tm"][:, kvi, h * 64:(h + 1) * 64])
                        last = e.matmul(PS[i2 * 2 + b][0:64, 0:64], vsl,
                                        A["pTa"][:, i2 * 2 + b, :],
                                        start=True, stop=True)
                return last
            em.step("tensor", bavmm, waits=[(vcs, em.led[vcs]), (ld, ld_v)], inc=tcs)
            def bobev(e, hpair=hpair):
                last = None
                for i2, h in enumerate(hpair):
                    for b in range(2):
                        last = e.tensor_copy(A["ob_fm"][:, h, b * 64:(b + 1) * 64],
                                             PS[i2 * 2 + b][0:64, 0:64])
                return last
            em.step("vector", bobev, waits=[(tcs, em.led[tcs])], inc=vcs)
        em.dma("sync", A["xt"][:], g["xn1_hbm"][T],
               waits=[(vcs, em.led[vcs]), (scs, em.led[scs])], inc=ld)
        ld_x = em.led[ld]
        def owmm(e):
            last = e.matmul(PS[1][0:128, 0:512], A["ones_bf"][0:1, :],
                            A["bout_row"][0:1, 0:512], start=True, stop=False)
            last = e.matmul(PS[2][0:128, 0:512], A["ones_bf"][0:1, :],
                            A["bout_row"][0:1, 512:1024], start=True, stop=False)
            for h in range(16):
                last = e.matmul(PS[1][0:128, 0:512], A["ob_fm"][:, h, :],
                                A["wout"][:, h, 0:512], start=False, stop=(h == 15))
                last = e.matmul(PS[2][0:128, 0:512], A["ob_fm"][:, h, :],
                                A["wout"][:, h, 512:1024], start=False, stop=(h == 15))
            return last
        em.step("tensor", owmm, waits=[(vcs, em.led[vcs]), (scs, em.led[scs]),
                                       (wd, wd_b)], inc=tcs)
        def xmev(e):
            e.tensor_tensor(A["xm_t"][:, 0:512], PS[1][0:128, 0:512],
                            A["xt"][:, 0:512], op=ALU.add)
            return e.tensor_tensor(A["xm_t"][:, 512:1024], PS[2][0:128, 0:512],
                                   A["xt"][:, 512:1024], op=ALU.add)
        em.step("vector", xmev, waits=[(tcs, em.led[tcs]), (ld, ld_x),
                                       (sts, em.led[sts])], inc=vcs)
        em.dma("sync", g["x2a_hbm"][T], A["xm_t"][:],
               waits=[(vcs, em.led[vcs])], inc=sts)
    em.barrier()
    pa.close()

    pa = PhaseAlloc(nc, A)
    sbuf = pa.sbuf
    sbuf("w1", (128, 8, 4096), BF16)
    sbuf("w2", (128, 32, D), BF16)
    sbuf("b1col", (128, 32), F32)
    sbuf("b2row", (1, D), BF16)
    sbuf("g2bc", (128, D), F32); sbuf("b2bc", (128, D), F32)
    sbuf("xm4", (128, 2, D), F32)
    sbuf("sq_scratch_c", (128, D), F32)
    alloc_stats(pa, "sm")
    sbuf("h2nT_g", (128, 8, 256), BF16)
    sbuf("m1_fm", (128, 32, 256), BF16)
    sbuf("s1_t", (128, D), F32)
    sbuf("x1r", (128, D), F32)
    sbuf("g1r", (128, D), BF16)
    sbuf("h2nb", (128, D), BF16)
    A["out_t"] = A["sq_scratch_c"]
    em.dma("gpsimd", A["w1"][:],
           g["L1_attn_w1"].rearrange("(c p) f -> p c f", p=128), inc=wd)
    em.dma("gpsimd", A["w2"][:],
           g["L1_attn_w2"].rearrange("(c p) f -> p c f", p=128), inc=wd)
    em.dma("sync", A["b1col"][:], g["L1_attn_b1"].rearrange("(s p) -> p s", p=128),
           inc=ld, slow=True)
    em.dma("gpsimd", A["b2row"][:], g["L1_attn_b2"].rearrange("(o d) -> o d", o=1),
           inc=wd)
    em.dma("sync", A["g2bc"][:], g["l1a_g2bc"], inc=ld)
    em.dma("sync", A["b2bc"][:], g["l1a_b2bc"], inc=ld)
    wd_c = em.led[wd]; ld_c = em.led[ld]
    for gq in range(8):
        T0 = L1A_Q_LO + gq * 2
        for ti in range(2):
            T = T0 + ti
            xm = A["xm4"][:, ti, :]
            em.dma("sync", xm, g["x2a_hbm"][T],
                   waits=[(vcs, em.led[vcs]), (scs, em.led[scs])], inc=ld)
            emit_stats(em, A, xm, "sm", "eps_1em5", "sq_scratch_c",
                       extra_waits=[(ld, em.led[ld]), (vcs, em.led[vcs]),
                                    (sts, em.led[sts])])
            em.step("scalar", lambda e, xm=xm: e.activation(
                A["sq_scratch_c"][:], xm, AF.Identity,
                bias=A["sm_negmr"][:, 0:1], scale=A["sm_r"][:, 0:1]),
                waits=[(vcs, em.led[vcs])], inc=scs)
            em.step("vector", lambda e: e.tensor_tensor(
                A["sq_scratch_c"][:], A["sq_scratch_c"][:], A["g2bc"][:], op=ALU.mult),
                waits=[(scs, em.led[scs]), (ld, ld_c), (vcs, em.led[vcs])], inc=vcs)
            em.step("vector", lambda e: e.tensor_tensor(
                A["h2nb"][:], A["sq_scratch_c"][:], A["b2bc"][:], op=ALU.add),
                waits=[(vcs, em.led[vcs])], inc=vcs)
            for cc in range(8):
                bank = PS[4 + (cc % 2)]
                em.step("tensor", lambda e, cc=cc, bank=bank: e.matmul(
                    bank[0:128, 0:128], A["h2nb"][:, cc * 128:(cc + 1) * 128],
                    A["identbf"][:], start=True, stop=True),
                    waits=[(vcs, em.led[vcs]), (scs, em.led[scs]), (wd, wd_c)],
                    inc=tcs)
                em.step("vector", lambda e, cc=cc, bank=bank, ti=ti: e.tensor_copy(
                    A["h2nT_g"][:, cc, ti * 128:(ti + 1) * 128], bank[0:128, 0:128]),
                    waits=[(tcs, em.led[tcs])], inc=vcs)
        for js in range(32):
            def m1mm(e, js=js):
                last = None
                for cc in range(8):
                    last = e.matmul(PS[0][0:128, 0:256],
                                    A["w1"][:, cc, js * 128:(js + 1) * 128],
                                    A["h2nT_g"][:, cc, :], start=(cc == 0),
                                    stop=(cc == 7))
                return last
            em.step("tensor", m1mm, waits=[(vcs, em.led[vcs]), (scs, em.led[scs])],
                    inc=tcs)
            em.step("scalar", lambda e, js=js: e.activation(
                A["m1_fm"][:, js, :], PS[0][0:128, 0:256], AF.Gelu,
                bias=A["b1col"][:, js:js + 1]),
                waits=[(tcs, em.led[tcs]), (ld, ld_c)], inc=scs)
        for ti in range(2):
            T = T0 + ti
            def w2mm(e, ti=ti):
                last = e.matmul(PS[1][0:128, 0:512], A["ones_bf"][0:1, :],
                                A["b2row"][0:1, 0:512], start=True, stop=False)
                last = e.matmul(PS[2][0:128, 0:512], A["ones_bf"][0:1, :],
                                A["b2row"][0:1, 512:1024], start=True, stop=False)
                for cc in range(32):
                    last = e.matmul(PS[1][0:128, 0:512],
                                    A["m1_fm"][:, cc, ti * 128:(ti + 1) * 128],
                                    A["w2"][:, cc, 0:512], start=False,
                                    stop=(cc == 31))
                    last = e.matmul(PS[2][0:128, 0:512],
                                    A["m1_fm"][:, cc, ti * 128:(ti + 1) * 128],
                                    A["w2"][:, cc, 512:1024], start=False,
                                    stop=(cc == 31))
                return last
            em.step("tensor", w2mm, waits=[(vcs, em.led[vcs]), (scs, em.led[scs])],
                    inc=tcs)
            def s1ev(e, ti=ti):
                e.tensor_tensor(A["s1_t"][:, 0:512], PS[1][0:128, 0:512],
                                A["xm4"][:, ti, 0:512], op=ALU.add)
                return e.tensor_tensor(A["s1_t"][:, 512:1024], PS[2][0:128, 0:512],
                                       A["xm4"][:, ti, 512:1024], op=ALU.add)
            em.step("vector", s1ev, waits=[(tcs, em.led[tcs])], inc=vcs)
            em.dma("sync", A["x1r"][:], g["x1_hbm"][T],
                   waits=[(vcs, em.led[vcs])], inc=ld)
            em.dma("sync", A["g1r"][:], g["g1_hbm"][T], waits=[], inc=ld)
            ld_f = em.led[ld]
            em.step("vector", lambda e: e.tensor_tensor(
                A["s1_t"][:], A["s1_t"][:], A["g1r"][:], op=ALU.mult),
                waits=[(vcs, em.led[vcs]), (ld, ld_f)], inc=vcs)
            em.step("vector", lambda e: e.tensor_tensor(
                A["out_t"][:], A["s1_t"][:], A["x1r"][:], op=ALU.add),
                waits=[(vcs, em.led[vcs]), (sts, em.led[sts])], inc=vcs)
            em.dma("sync", g["out_shard"][(T - L1A_Q_LO) * 128:(T - L1A_Q_LO + 1) * 128, :],
                   A["out_t"][:], waits=[(vcs, em.led[vcs])], inc=sts)
    em.barrier()
    pa.close()


_NC_CACHE = {}

def kernel(**inputs):
    if "nc" not in _NC_CACHE:
        _NC_CACHE["nc"] = build_program()
    nc = _NC_CACHE["nc"]
    in_maps = [shard_inputs(inputs, c) for c in range(8)]
    res = run_bass_kernel_spmd(nc, in_maps, core_ids=list(range(8)))
    out = np.zeros((2, 8192, D), np.float32)
    for c in range(8):
        b, s = divmod(c, 4)
        out[b, s * T_SH:(s + 1) * T_SH] = res.results[c]["out_shard"]
    return out


# version 06
# speedup vs baseline: 1.1877x; 1.0231x over previous
import contextlib
import numpy as np
import ml_dtypes
import concourse.bass as bass
import concourse.mybir as mybir
from concourse.bass_utils import run_bass_kernel_spmd

F32 = mybir.dt.float32
BF16 = mybir.dt.bfloat16
AF = mybir.ActivationFunctionType
ALU = mybir.AluOpType
AX = mybir.AxisListType
BFD = ml_dtypes.bfloat16

D = 1024
NT_OUT = 26
NT_EXT = 27
HALO = 1280
T_SH = 2048
NLAT = 64
GROUPS = [(0, 4), (4, 4), (8, 4), (12, 4), (16, 4), (20, 4), (24, 2)]
L1F_KV_LO = 0
L1F_QA_LO = 4
L1F_KVB_LO = 4
L1F_QB_LO = 9
L1A_KV_LO = 9
L1A_Q_LO = 10

NEG = np.float32(-1e30)


class Em:
    ENGS = ("sync", "tensor", "vector", "scalar", "gpsimd")

    def __init__(self, nc):
        self.nc = nc
        self.streams = {e: [] for e in self.ENGS}
        self.sems = {}
        self.led = {}

    def sem(self, name):
        h = self.nc.alloc_semaphore(name)
        self.sems[name] = h
        self.led[name] = 0
        return h

    def step(self, eng, fn, waits=(), inc=None, inc_by=1):
        for s, v in waits:
            assert v <= self.led[s], f"wait {s}>={v} but only {self.led[s]} emitted"
        waits = tuple(waits)
        sems = self.sems
        if inc is not None:
            self.led[inc] += inc_by
        def closure(e, fn=fn, waits=waits, inc=inc, inc_by=inc_by):
            for s, v in waits:
                if v > 0:
                    e.wait_ge(sems[s], v)
            last = fn(e)
            if inc is not None:
                if last is None:
                    e.nop().then_inc(sems[inc], inc_by)
                else:
                    last.then_inc(sems[inc], inc_by)
        self.streams[eng].append(closure)

    ENG_SEM = {"sync": "ld", "tensor": "tc", "vector": "vc",
               "scalar": "sc", "gpsimd": "wd"}

    def barrier(self):
        snap = dict(self.led)
        for eng in self.ENGS:
            own = self.ENG_SEM[eng]
            waits = [(s, v) for s, v in snap.items() if s != own and v > 0]
            self.step(eng, lambda e: None, waits=waits, inc=own)

    def dma(self, eng, out, in_, waits=(), inc=None, slow=False):
        self.step(eng, lambda e: e.dma_start(out=out, in_=in_,
                                             allow_slow_non_contiguous=slow),
                  waits=waits, inc=inc, inc_by=16)

    def finish(self, block):
        for name in self.ENGS:
            stream = self.streams[name]
            def runner(eng, stream=stream):
                for c in stream:
                    c(eng)
            getattr(block, name)(runner)


def build_masks_for_core(s):
    t0 = s * T_SH
    a0 = t0 - HALO
    out = {}
    ml = np.zeros((NT_OUT, 128, 256), np.float32)
    zl = np.ones((NT_OUT, 128, 1), np.float32)
    mlat = np.zeros((NT_OUT, 128, NLAT), np.float32)
    zlat = np.ones((NT_OUT, 128, 1), np.float32)
    pad = np.ones((NT_EXT, 128, 1), np.float32)
    for T in range(NT_OUT):
        abs0 = a0 + 128 * T
        i = abs0 + np.arange(128)[:, None]
        j = abs0 - 128 + np.arange(256)[None, :]
        ok = (j >= i - 128) & (j <= i - 1) & (j >= 0)
        ml[T] = np.where(ok, 0.0, NEG)
        zl[T, :, 0] = ok.any(-1).astype(np.float32)
        cid = abs0 // 128
        lok = np.arange(NLAT)[None, :] < cid
        mlat[T] = np.where(lok, 0.0, NEG)
        zlat[T, :, 0] = 1.0 if cid > 0 else 0.0
    for T in range(NT_EXT):
        abs0 = a0 - 128 + 128 * T
        pad[T, :, 0] = ((abs0 + np.arange(128)) >= 0).astype(np.float32)
    out["mask_local"] = ml
    out["zl_local"] = zl
    out["mask_lat"] = mlat
    out["zl_lat"] = zlat
    out["padmask"] = pad
    i = np.arange(64)[:, None]
    c = np.arange(128)[None, :]
    out["mask_l1a"] = np.where((c < 64) | (c - 64 <= i), 0.0, NEG).astype(np.float32)
    def build_mf(qlo, kclamp):
        mf = np.zeros((NT_OUT - qlo, 128, 1024), np.float32)
        for tq in range(qlo, NT_OUT):
            abs0 = a0 + 128 * tq
            kst = max(tq - ((abs0 - a0 + 256) % 512) // 128 - 4, kclamp)
            klo = a0 + 128 * kst
            i = abs0 + np.arange(128)[:, None]
            j = klo + np.arange(1024)[None, :]
            ok = (i >= j) & (i <= j + 512) & (j >= 0)
            mf[tq - qlo] = np.where(ok, 0.0, NEG)
        return mf
    out["mask_l1fA"] = build_mf(L1F_QA_LO, L1F_KV_LO)
    out["mask_l1fB"] = build_mf(L1F_QB_LO, L1F_KVB_LO)
    DIM_HEAD, WINDOW, N = 64, 512, 8192
    half = np.arange(0, DIM_HEAD, 2, dtype=np.float64)
    inv = 1.0 / 10000.0 ** (half / DIM_HEAD)
    base = (half + 0.4 * DIM_HEAD) / (1.4 * DIM_HEAD)
    cosT = np.zeros((26, 128, 64), np.float32); sinT = np.zeros_like(cosT)
    cosIT = np.zeros_like(cosT); sinIT = np.zeros_like(cosT)
    for T in range(0, NT_OUT):
        abs0 = a0 + 128 * T
        t = np.clip(abs0 + np.arange(128), 0, None).astype(np.float64)
        fr = t[:, None] * inv[None, :]
        fr2 = np.concatenate([fr, fr], -1)
        power = (t - N // 2) / float(WINDOW // 2)
        sc = base[None, :] ** power[:, None]
        sc2 = np.concatenate([sc, sc], -1)
        cs, sn = np.cos(fr2), np.sin(fr2)
        sgn = np.concatenate([-np.ones(32), np.ones(32)])[None, :]
        cosT[T] = (cs * sc2).astype(np.float32)
        sinT[T] = (sn * sc2 * sgn).astype(np.float32)
        cosIT[T] = (cs / sc2).astype(np.float32)
        sinIT[T] = (sn / sc2 * sgn).astype(np.float32)
    out["rope_cos"] = cosT; out["rope_sin"] = sinT
    out["rope_cosi"] = cosIT; out["rope_sini"] = sinIT
    out["ident32"] = np.eye(128, dtype=np.float32)
    out["identbf"] = np.eye(128, dtype=np.float32)
    out["iota_row"] = np.broadcast_to(np.arange(128, dtype=np.float32), (128, 128)).copy()
    out["pidx"] = np.arange(128, dtype=np.float32).reshape(128, 1)
    out["ones_bf_row"] = np.ones((1, 128), np.float32)
    oh = np.zeros((8, 8, 128), np.float32)
    for e in range(8):
        oh[e, e, :] = 1.0
    out["onehot8"] = oh
    return out


def shard_inputs(inputs, core):
    b, s = divmod(core, 4)
    t0 = s * T_SH
    x = np.asarray(inputs["L0_attn_x"], np.float32)
    ext = np.zeros((NT_EXT * 128, D), np.float32)
    lo = t0 - HALO - 128
    src_lo = max(lo, 0)
    ext[src_lo - lo:src_lo - lo + (t0 + T_SH - src_lo)] = x[b, src_lo:t0 + T_SH]
    im = {"x_ext": ext, "x_full": np.ascontiguousarray(x[b])}
    im.update({k: np.ascontiguousarray(np.asarray(v)) for k, v in inputs.items()
               if k != "L0_attn_x"})
    im["WrT"] = np.ascontiguousarray(np.asarray(inputs["L0_ffn_Wr"]).T)
    W1 = np.asarray(inputs["L0_ffn_W1"], np.float32)
    im["W1_staged"] = np.ascontiguousarray(
        W1.reshape(8, 8, 128, 32, 128).transpose(3, 2, 0, 1, 4))
    Wo = np.asarray(inputs["L0_ffn_Wo"], np.float32)
    im["Wo_staged"] = np.ascontiguousarray(
        Wo.reshape(32, 128, 8, 128))
    im["g0_gamma_bc"] = np.ascontiguousarray(np.broadcast_to(
        np.asarray(inputs["L0_ffn_gamma"], np.float32), (128, D)))
    im["g0_beta_bc"] = np.ascontiguousarray(np.broadcast_to(
        np.asarray(inputs["L0_ffn_beta"], np.float32), (128, D)))
    def bc(v):
        return np.ascontiguousarray(np.broadcast_to(np.asarray(v, np.float32), (128, D)))
    im["l1a_g1bc"] = bc(inputs["L1_attn_ln1_g"]); im["l1a_b1bc"] = bc(inputs["L1_attn_ln1_b"])
    im["l1a_g2bc"] = bc(inputs["L1_attn_ln2_g"]); im["l1a_b2bc"] = bc(inputs["L1_attn_ln2_b"])
    lng = np.asarray(inputs["L1_ffn_ln_g"], np.float32)
    lnb = np.asarray(inputs["L1_ffn_ln_b"], np.float32)
    im["l1f_gbc"] = np.ascontiguousarray(
        np.broadcast_to(lng[:, None, :], (2, 128, D)))
    im["l1f_bbc"] = np.ascontiguousarray(
        np.broadcast_to(lnb[:, None, :], (2, 128, D)))
    w2 = np.asarray(inputs["L1_ffn_w2"], np.float32)
    w2p = np.zeros((2, 2816, D), np.float32)
    w2p[:, :2730] = w2
    im["L1_ffn_w2_pad"] = w2p
    im.update(build_masks_for_core(s))
    return im


DEBUG_OUT = set()
DEV_STAGE = 99
DEV_GROUPS = 99
DEV_PHASES = 99
DEV_ATTN = 99
DEV_SMM_HH = (0, 1, 2, 3)

def build_program():
    nc = bass.Bass("TRN2", target_bir_lowering=False)
    g = {}

    def din(name, shape, dtype=F32):
        g[name] = nc.dram_tensor(name, list(shape), dtype, kind="ExternalInput").ap()
        return g[name]

    def dint(name, shape, dtype=F32):
        kind = "ExternalOutput" if name in DEBUG_OUT else "Internal"
        g[name] = nc.dram_tensor(name, list(shape), dtype, kind=kind).ap()
        return g[name]

    din("x_ext", (NT_EXT * 128, D)); din("x_full", (8192, D))
    din("L0_attn_qkv_w", (D, 3 * D)); din("L0_attn_qkv_b", (3 * D,))
    din("L0_attn_rq_w", (D, 256)); din("L0_attn_rq_b", (256,))
    din("L0_attn_rkv_w", (D, 512)); din("L0_attn_rkv_b", (512,))
    din("L0_attn_rout_w", (256, D)); din("L0_attn_rout_b", (D,))
    din("L0_attn_out_w", (D, D)); din("L0_attn_out_b", (D,))
    din("L0_attn_gl_w", (64, 1)); din("L0_attn_gr_w", (64, 1))
    din("L0_ffn_Wr", (8, D)); din("WrT", (D, 8))
    din("L0_ffn_W1", (8, D, 4096)); din("L0_ffn_b1", (8, 4096))
    din("L0_ffn_Wo", (4096, D)); din("L0_ffn_bo", (D,))
    din("W1_staged", (32, 128, 8, 8, 128)); din("Wo_staged", (32, 128, 8, 128))
    din("L0_ffn_gamma", (D,)); din("L0_ffn_beta", (D,))
    din("L1_attn_ln1_g", (D,)); din("L1_attn_ln1_b", (D,))
    din("L1_attn_w_qkv", (D, 3 * D)); din("L1_attn_b_qkv", (3 * D,))
    din("L1_attn_w_out", (D, D)); din("L1_attn_b_out", (D,))
    din("L1_attn_ln2_g", (D,)); din("L1_attn_ln2_b", (D,))
    din("L1_attn_w1", (D, 4096)); din("L1_attn_b1", (4096,))
    din("L1_attn_w2", (4096, D)); din("L1_attn_b2", (D,))
    din("L1_ffn_wqkv", (2, D, 1536)); din("L1_ffn_q_scale", (2, 64))
    din("L1_ffn_k_scale", (2, 64)); din("L1_ffn_wo", (2, 512, D))
    din("L1_ffn_ln_g", (2, D)); din("L1_ffn_ln_b", (2, D))
    din("L1_ffn_w1", (2, D, 5460)); din("L1_ffn_b1", (2, 5460))
    din("L1_ffn_w2", (2, 2730, D)); din("L1_ffn_b2", (2, D))
    din("mask_local", (NT_OUT, 128, 256)); din("zl_local", (NT_OUT, 128, 1))
    din("mask_lat", (NT_OUT, 128, NLAT)); din("zl_lat", (NT_OUT, 128, 1))
    din("padmask", (NT_EXT, 128, 1))
    din("mask_l1a", (64, 128))
    din("mask_l1fA", (NT_OUT - L1F_QA_LO, 128, 1024))
    din("mask_l1fB", (NT_OUT - L1F_QB_LO, 128, 1024))
    din("rope_cos", (26, 128, 64)); din("rope_sin", (26, 128, 64))
    din("rope_cosi", (26, 128, 64)); din("rope_sini", (26, 128, 64))
    din("ident32", (128, 128)); din("identbf", (128, 128))
    din("g0_gamma_bc", (128, D)); din("g0_beta_bc", (128, D))
    din("l1a_g1bc", (128, D)); din("l1a_b1bc", (128, D))
    din("l1a_g2bc", (128, D)); din("l1a_b2bc", (128, D))
    din("l1f_gbc", (2, 128, D)); din("l1f_bbc", (2, 128, D))
    din("L1_ffn_w2_pad", (2, 2816, D))
    g["xn1_hbm"] = nc.dram_tensor("xn1_hbm", [NT_OUT, 128, D], F32, kind="Internal").ap()
    din("iota_row", (128, 128)); din("pidx", (128, 1)); din("ones_bf_row", (1, 128))
    din("onehot8", (8, 8, 128))
    dint("g0_hbm", (NT_OUT, 128, D), BF16)
    dint("dbg_logits", (NT_OUT, 128, 8), F32)
    dint("dbg_masks", (NT_OUT, 128, 8), F32)
    dint("dbg_y", (NT_OUT, 128, D), F32)
    dint("dbg_lmax", (NT_OUT, 128, 1), F32)
    dint("dbg_rhs", (128, 8, 8, 512), BF16)
    dint("dbg_inter", (128, 32, 512), BF16)
    dint("dbg_mbc", (128, 512), BF16)
    dint("dbg_local", (64, 16, 128), BF16)
    dint("dbg_remote", (64, 16, 128), BF16)
    dint("dbg_mixed", (64, 16, 128), BF16)
    dint("dbg_gates", (1, 2048), BF16)
    dint("dbg_q", (128, 8, 512), BF16)
    dint("dbg_k", (128, 8, 640), BF16)
    dint("dbg_v", (128, 5, 1024), BF16)
    dint("dbg_pt", (128, 2, 128), BF16)
    dint("dbg_m1", (128, 1024), F32)
    dint("dbg_outw", (64, 16, 1024), BF16)
    dint("dbg_outb", (1, 1024), BF16)
    dint("dbg_g0t", (128, 1024), BF16)
    dint("dbg_s0", (128, 1024), F32)
    dint("dbg_mrow", (1, 4096), BF16)
    dint("xn0T_hbm", (NT_OUT, 128, 8 * 128), BF16)
    dint("x1_hbm", (NT_OUT, 128, D), F32)
    dint("W1bf", (32, 128, 8, 8, 128), BF16)
    dint("Wobf", (32, 128, 8, 128), BF16)
    dint("x2a_hbm", (NT_OUT, 128, D), F32)
    dint("yA_hbm", (NT_OUT, 128, D), F32)
    dint("x2fA_hbm", (NT_OUT, 128, D), F32)
    dint("x2fB_hbm", (NT_OUT, 128, D), F32)
    dint("g1_hbm", (NT_OUT, 128, D), BF16)
    out = nc.dram_tensor("out_shard", [T_SH, D], F32, kind="ExternalOutput").ap()
    g["out_shard"] = out

    with contextlib.ExitStack() as es:
        em = Em(nc)
        for s in ("ld", "wd", "tc", "vc", "sc", "st"):
            em.sem(s)
        _build_body(nc, em, g, es)
        block = es.enter_context(nc.Block())
        em.finish(block)
    return nc


def _build_body(nc, em, g, es):
    A = {}

    def sb(name, shape, dtype):
        A[name] = nc.alloc_sbuf_tensor("sb_" + name, list(shape), dtype)
        return A[name]

    sb("ident32", (128, 128), F32); sb("identbf", (128, 128), BF16)
    sb("eps_1em5", (128, 1), F32); sb("eps_1em12", (128, 1), F32)
    sb("iota_row", (128, 128), F32); sb("pidx", (128, 1), F32)
    sb("ones_bf", (1, 128), BF16); sb("ones_f32", (1, 128), F32)
    PS = []
    for i in range(6):
        PS.append(nc.alloc_psum_tensor(f"ps{i}", [128, 512], F32))
    PS.append(nc.alloc_psum_tensor("psb0", [128, 1024], BF16))
    PS.append(nc.alloc_psum_tensor("psb1", [128, 1024], BF16))

    ld, wd, tc, vc, sc, st = "ld", "wd", "tc", "vc", "sc", "st"

    def mset(e):
        e.memset(A["eps_1em5"][:], 1e-5)
        return e.memset(A["eps_1em12"][:], 1e-12)
    em.step("gpsimd", mset, inc="wd")
    em.dma("sync", A["ident32"][:], g["ident32"], inc=ld)
    em.dma("gpsimd", A["identbf"][:], g["identbf"], inc=wd)
    em.dma("sync", A["iota_row"][:], g["iota_row"], inc=ld)
    em.dma("sync", A["pidx"][:], g["pidx"], inc=ld)
    em.dma("gpsimd", A["ones_bf"][:], g["ones_bf_row"], inc=wd)
    em.dma("sync", A["ones_f32"][:], g["ones_bf_row"], inc=ld)
    base_ld = em.led[ld]; base_wd = em.led[wd]

    st_ffn = _ph_ffn(nc, em, g, A, PS, base_ld, base_wd)
    _ph_attn(nc, em, g, A, PS, st_ffn)
    _ph_xn1(nc, em, g, A, PS)
    if DEV_PHASES >= 4:
        _ph_l1f_layer(nc, em, g, A, PS, 0, g["xn1_hbm"], L1F_KV_LO, L1F_QA_LO,
                      g["yA_hbm"], False, "mask_l1fA")
    if DEV_PHASES >= 5:
        _ph_l1f_layer(nc, em, g, A, PS, 1, g["yA_hbm"], L1F_KVB_LO, L1F_QB_LO,
                      None, True, "mask_l1fB")
    if DEV_PHASES >= 6:
        _ph_l1a(nc, em, g, A, PS)
    _final_store_wait(nc, em, g)


def _final_store_wait(nc, em, g):
    em.step("sync", lambda e: None, waits=[("st", em.led["st"])])


class PhaseAlloc:
    _uid = [0]

    def __init__(self, nc, A):
        import contextlib
        self.nc = nc
        self.A = A
        self.es = contextlib.ExitStack()
        PhaseAlloc._uid[0] += 1
        self.tag = f"p{PhaseAlloc._uid[0]}_"

    def sbuf(self, name, shape, dtype):
        h = self.es.enter_context(self.nc.sbuf_tensor(self.tag + name, list(shape), dtype))
        self.A[name] = h
        return h

    def close(self):
        self.es.close()


def alloc_stats(pa, pref):
    for suf in ("_sum", "_ss", "_mneg", "_m2", "_v", "_sd", "_r", "_negmr"):
        pa.sbuf(pref + suf, (128, 1), F32)


def emit_stats(em, A, x_ap, pref, eps, scratch, extra_waits=()):
    assert eps in ("eps_1em5", "eps_1em12")
    """Per-token LN stats of x_ap [128, D] fp32.  Produces A[pref+'_r'] and
    A[pref+'_negmr'] ([128,1] fp32).  Engine-alternating chain (no same-engine
    dependent small ops).  Ends with a vector step; the consumer should wait
    ('vc', em.led['vc']) then scalar ops may use r/negmr as AP scale/bias."""
    vcs, scs = "vc", "sc"
    D_ = D
    em.step("vector", lambda e: e.reduce_sum(A[pref + "_sum"][:], x_ap, axis=AX.X),
            waits=list(extra_waits), inc=vcs)
    em.step("scalar", lambda e: e.activation(A[scratch][:], x_ap, AF.Square,
                                             accum_out=A[pref + "_ss"][:]),
            waits=list(extra_waits), inc=scs)
    def s_m(e):
        e.activation(A[pref + "_mneg"][:], A[pref + "_sum"][:], AF.Identity,
                     scale=-1.0 / D_)
        return e.activation(A[pref + "_m2"][:], A[pref + "_sum"][:], AF.Square,
                            scale=1.0 / D_)
    em.step("scalar", s_m, waits=[(vcs, em.led[vcs])], inc=scs)
    em.step("vector", lambda e: e.tensor_scalar(
        A[pref + "_v"][:], A[pref + "_ss"][:], 1.0 / D_, A[pref + "_m2"][:, 0:1],
        op0=ALU.mult, op1=ALU.subtract),
        waits=[(scs, em.led[scs])], inc=vcs)
    em.step("scalar", lambda e: e.activation(A[pref + "_sd"][:], A[pref + "_v"][:],
                                             AF.Sqrt, bias=A[eps][:, 0:1]),
            waits=[(vcs, em.led[vcs]), ("wd", 1)], inc=scs)
    em.step("vector", lambda e: e.reciprocal(A[pref + "_r"][:], A[pref + "_sd"][:]),
            waits=[(scs, em.led[scs])], inc=vcs)
    em.step("scalar", lambda e: e.activation(A[pref + "_negmr"][:],
                                             A[pref + "_mneg"][:], AF.Identity,
                                             scale=A[pref + "_r"][:, 0:1]),
            waits=[(vcs, em.led[vcs])], inc=scs)
    em.step("vector", lambda e: None, waits=[(scs, em.led[scs])], inc=vcs)


def _ph_ffn(nc, em, g, A, PS, base_ld, base_wd):
    A = dict(A)
    """Computes g0 (silu of MoE block) for all ext-out tiles; writes g0_hbm,
    xn0T_hbm, and bf16 W1/Wo copies to HBM on the first pass."""
    pa = PhaseAlloc(nc, A)
    sbuf = pa.sbuf
    ld, wd, tcs, vcs, scs, sts = "ld", "wd", "tc", "vc", "sc", "st"
    if DEV_PHASES < 1:
        pa.close()
        return em.led["st"]
    sbuf("x_g", (128, 2, D), F32)
    sbuf("sq_scratch", (128, D), F32)
    alloc_stats(pa, "s0")
    alloc_stats(pa, "sy")
    sbuf("xn32_t", (128, D), F32)
    sbuf("hi_t", (128, D), BF16)
    sbuf("lo_t", (128, D), BF16)
    sbuf("xnT_g", (128, 8, 512), BF16)
    sbuf("loT_t", (128, 8, 128), BF16)
    sbuf("wr_hi", (128, 8, 8), BF16)
    sbuf("wr_lo", (128, 8, 8), BF16)
    sbuf("wr_tmp", (128, 8, 8), F32)
    sbuf("logits", (128, 8), F32)
    sbuf("lmax", (128, 1), F32)
    sbuf("masks", (128, 8), F32)
    sbuf("masks_bf", (128, 128), BF16)
    sbuf("masksT_g", (8, 512), BF16)
    sbuf("onehot8", (8, 8, 128), BF16)
    sbuf("mask_bc", (128, 512), BF16)
    sbuf("rhs_e", (128, 8, 8, 512), BF16)
    sbuf("w1slab", (128, 2, 8, 8, 128), BF16)
    sbuf("woslab", (128, 2, 8, 128), BF16)
    sbuf("inter_g", (128, 32, 512), BF16)
    sbuf("b1sb", (8, 4096), BF16)
    sbuf("xn_tm", (128, D), F32)
    sbuf("y_t", (128, D), F32)
    sbuf("gam_bc", (128, D), F32)
    sbuf("bet_bc", (128, D), F32)
    sbuf("g0_t", (128, D), BF16)
    sbuf("rowbuf2_bf", (1, 1024), BF16)

    em.dma("sync", A["gam_bc"][:], g["g0_gamma_bc"], inc=ld)
    em.dma("sync", A["bet_bc"][:], g["g0_beta_bc"], inc=ld)
    em.dma("gpsimd", A["b1sb"][:], g["L0_ffn_b1"], inc=wd)
    em.dma("gpsimd", A["onehot8"][:], g["onehot8"], inc=wd)
    em.step("gpsimd", lambda e: e.memset(A["masks_bf"][:], 0.0), inc=wd)
    em.dma("sync", A["wr_tmp"][:], g["WrT"].rearrange("(c p) e -> p c e", p=128), inc=ld)
    em.step("vector", lambda e: e.tensor_copy(A["wr_hi"][:], A["wr_tmp"][:]),
            waits=[(ld, em.led[ld])], inc=vcs)
    em.step("scalar", lambda e: e.activation(A["sq_scratch"][:, 0:64],
                                             A["wr_hi"][:].rearrange("p c e -> p (c e)"),
                                             AF.Identity),
            waits=[(vcs, em.led[vcs])], inc=scs)
    em.step("vector", lambda e: e.tensor_tensor(
        A["wr_lo"][:].rearrange("p c e -> p (c e)"),
        A["wr_tmp"][:].rearrange("p c e -> p (c e)"),
        A["sq_scratch"][:, 0:64], op=ALU.subtract),
        waits=[(scs, em.led[scs])], inc=vcs)
    em.dma("gpsimd", A["rowbuf2_bf"][0:1, 0:1024],
           g["L0_ffn_bo"].rearrange("(o d) -> o d", o=1), inc=wd)

    for gi, (g0t, gn) in enumerate(GROUPS):
        if gi >= DEV_GROUPS:
            break
        first_pass = (gi == 0)
        ntok = gn * 128
        for ti in range(gn):
            T = g0t + ti
            xt = A["x_g"][:, ti % 2, :]
            em.dma("sync", xt, g["x_ext"][(T + 1) * 128:(T + 2) * 128, :],
                   waits=[(vcs, em.led[vcs]), (scs, em.led[scs])], inc=ld)
            emit_stats(em, A, xt, "s0", "eps_1em5", "sq_scratch",
                       extra_waits=[(ld, em.led[ld])])
            def norm(e, xt=xt):
                e.activation(A["xn32_t"][:], xt, AF.Identity,
                             bias=A["s0_negmr"][:, 0:1], scale=A["s0_r"][:, 0:1])
                return e.activation(A["hi_t"][:], xt, AF.Identity,
                             bias=A["s0_negmr"][:, 0:1], scale=A["s0_r"][:, 0:1])
            em.step("scalar", norm, waits=[(vcs, em.led[vcs])], inc=scs)
            em.step("vector", lambda e: e.tensor_tensor(
                A["lo_t"][:], A["xn32_t"][:], A["hi_t"][:], op=ALU.subtract),
                waits=[(scs, em.led[scs])], inc=vcs)
            vc_norm = em.led[vcs]
            for cc in range(8):
                bank = PS[4 + (cc % 2)]
                em.step("tensor", lambda e, cc=cc, bank=bank: e.matmul(
                    bank[0:128, 0:128], A["hi_t"][:, cc * 128:(cc + 1) * 128],
                    A["identbf"][:], start=True, stop=True),
                    waits=[(vcs, vc_norm), (wd, base_wd),
                           (scs, em.led[scs]), (vcs, em.led[vcs])], inc=tcs)
                em.step("vector", lambda e, cc=cc, bank=bank, ti=ti: e.tensor_copy(
                    A["xnT_g"][:, cc, ti * 128:(ti + 1) * 128], bank[0:128, 0:128]),
                    waits=[(tcs, em.led[tcs]), (sts, em.led[sts])], inc=vcs)
            for cc in range(8):
                bank = PS[4 + (cc % 2)]
                em.step("tensor", lambda e, cc=cc, bank=bank: e.matmul(
                    bank[0:128, 0:128], A["lo_t"][:, cc * 128:(cc + 1) * 128],
                    A["identbf"][:], start=True, stop=True),
                    waits=[(vcs, em.led[vcs])], inc=tcs)
                em.step("vector", lambda e, cc=cc, bank=bank: e.tensor_copy(
                    A["loT_t"][:, cc, :], bank[0:128, 0:128]),
                    waits=[(tcs, em.led[tcs])], inc=vcs)
            def router(e, ti=ti):
                n = 0
                for cc in range(8):
                    for lh, wr in ((A["xnT_g"][:, cc, ti * 128:(ti + 1) * 128],
                                    A["wr_hi"][:, cc, :]),
                                   (A["xnT_g"][:, cc, ti * 128:(ti + 1) * 128],
                                    A["wr_lo"][:, cc, :]),
                                   (A["loT_t"][:, cc, :], A["wr_hi"][:, cc, :]),
                                   (A["loT_t"][:, cc, :], A["wr_lo"][:, cc, :])):
                        last = e.matmul(PS[5][0:128, 0:8], lh, wr,
                                        start=(n == 0), stop=(n == 31))
                        n += 1
                return last
            em.step("tensor", router, waits=[(vcs, em.led[vcs])], inc=tcs)
            em.step("vector", lambda e: e.tensor_copy(A["logits"][:], PS[5][0:128, 0:8]),
                    waits=[(tcs, em.led[tcs]), (sts, em.led[sts])], inc=vcs)
            em.step("vector", lambda e: e.reduce_max(A["lmax"][:], A["logits"][:],
                                                     axis=AX.X),
                    waits=[(vcs, em.led[vcs])], inc=vcs)
            em.step("scalar", lambda e: None, waits=[(vcs, em.led[vcs])], inc=scs)
            em.step("vector", lambda e: e.tensor_scalar(
                A["masks"][:], A["logits"][:], A["lmax"][:, 0:1], 1.0,
                op0=ALU.is_equal, op1=ALU.mult),
                waits=[(scs, em.led[scs]), (sts, em.led[sts])], inc=vcs)
            em.step("vector", lambda e: e.tensor_copy(A["masks_bf"][:, 0:8],
                                                      A["masks"][:]),
                    waits=[(vcs, em.led[vcs])], inc=vcs)
            em.step("tensor", lambda e: e.matmul(PS[4][0:128, 0:128],
                                                 A["masks_bf"][:], A["identbf"][:],
                                                 start=True, stop=True),
                    waits=[(vcs, em.led[vcs]), ("wd", 3 * 16 + 1)], inc=tcs)
            em.step("vector", lambda e, ti=ti: e.tensor_copy(
                A["masksT_g"][0:8, ti * 128:(ti + 1) * 128], PS[4][0:8, 0:128]),
                waits=[(tcs, em.led[tcs])], inc=vcs)
            em.dma("sync", g["dbg_logits"][T], A["logits"][:],
                   waits=[(vcs, em.led[vcs])], inc=sts)
            em.dma("sync", g["dbg_masks"][T], A["masks"][:],
                   waits=[(vcs, em.led[vcs])], inc=sts)
            em.dma("sync", g["dbg_lmax"][T], A["lmax"][:],
                   waits=[(vcs, em.led[vcs])], inc=sts)
            em.dma("sync", g["xn0T_hbm"][T, :, :].rearrange("p (c t) -> p c t", t=128),
                   A["xnT_g"][:, :, ti * 128:(ti + 1) * 128],
                   waits=[(vcs, em.led[vcs])], inc=sts)

        if DEV_STAGE < 1:
            continue
        for e_i in range(8):
            em.step("tensor", lambda e, e_i=e_i, ntok=ntok: e.matmul(
                PS[5][0:128, 0:ntok], A["onehot8"][:, e_i, :],
                A["masksT_g"][0:8, 0:ntok], start=True, stop=True),
                waits=[(vcs, em.led[vcs])], inc=tcs)
            em.step("vector", lambda e, ntok=ntok: e.tensor_copy(
                A["mask_bc"][:, 0:ntok], PS[5][0:128, 0:ntok]),
                waits=[(tcs, em.led[tcs])], inc=vcs)
            em.step("vector", lambda e, e_i=e_i, ntok=ntok: e.tensor_tensor(
                A["rhs_e"][:, e_i, :, 0:ntok], A["xnT_g"][:, :, 0:ntok],
                A["mask_bc"][:, 0:ntok].rearrange("p (o t) -> p o t", o=1)
                    .broadcast_to([128, 8, ntok]), op=ALU.mult),
                waits=[(vcs, em.led[vcs])], inc=vcs)

        if DEV_STAGE < 2:
            continue
        vcs_rhs = em.led[vcs]
        mm_lev = {}
        ev_lev = {}
        wd_lev = {}
        def w1_load(isl):
            dbuf = isl % 2
            wsl = A["w1slab"][:, dbuf, :, :, :]
            wt = [(tcs, mm_lev[isl - 2])] if isl >= 2 else [(tcs, em.led[tcs])]
            if first_pass:
                em.dma("gpsimd", wsl, g["W1_staged"][isl], waits=wt, inc=wd)
                em.dma("gpsimd", g["W1bf"][isl], wsl, waits=[(wd, em.led[wd])], inc=wd)
            else:
                em.dma("gpsimd", wsl, g["W1bf"][isl], waits=wt, inc=wd)
            wd_lev[isl] = em.led[wd]
        w1_load(0)
        w1_load(1)
        for isl in range(32):
            wsl = A["w1slab"][:, isl % 2, :, :, :]
            bank = PS[0] if isl % 2 == 0 else PS[3]
            def w1mm(e, isl=isl, wsl=wsl, bank=bank, ntok=ntok):
                last = None
                for e_i in range(8):
                    for cc in range(8):
                        last = e.matmul(bank[0:128, 0:ntok], wsl[:, e_i, cc, :],
                                        A["rhs_e"][:, e_i, cc, 0:ntok],
                                        start=(e_i == 0 and cc == 0), stop=False)
                last = e.matmul(bank[0:128, 0:ntok],
                                A["b1sb"][0:8, isl * 128:(isl + 1) * 128],
                                A["masksT_g"][0:8, 0:ntok], start=False, stop=True)
                return last
            wt = [(wd, wd_lev[isl]), (vcs, vcs_rhs)]
            if isl >= 2:
                wt.append((scs, ev_lev[isl - 2]))
            else:
                wt.append((scs, em.led[scs]))
            em.step("tensor", w1mm, waits=wt, inc=tcs)
            mm_lev[isl] = em.led[tcs]
            if isl + 2 < 32:
                w1_load(isl + 2)
            em.step("scalar", lambda e, isl=isl, bank=bank, ntok=ntok: e.activation(
                A["inter_g"][:, isl, 0:ntok], bank[0:128, 0:ntok], AF.Gelu),
                waits=[(tcs, mm_lev[isl])], inc=scs)
            ev_lev[isl] = em.led[scs]

        if DEV_STAGE < 3:
            continue
        for pair0 in range(0, gn, 2):
            pn = min(2, gn - pair0)
            womm_lev = {}
            wold_lev = {}
            def wo_load(isl, pair0=pair0):
                dbuf = isl % 2
                wos = A["woslab"][:, dbuf, :, :]
                wt = [(tcs, womm_lev[isl - 2])] if isl >= 2 else [(tcs, em.led[tcs])]
                if first_pass and pair0 == 0:
                    em.dma("gpsimd", wos, g["Wo_staged"][isl], waits=wt, inc=wd)
                    em.dma("gpsimd", g["Wobf"][isl], wos, waits=[(wd, em.led[wd])],
                           inc=wd)
                else:
                    em.dma("gpsimd", wos, g["Wobf"][isl], waits=wt, inc=wd)
                wold_lev[isl] = em.led[wd]
            wo_load(0)
            wo_load(1)
            sc_pre = em.led[scs]
            vc_pre = em.led[vcs]
            for isl in range(32):
                wos = A["woslab"][:, isl % 2, :, :]
                def womm(e, isl=isl, wos=wos, pair0=pair0, pn=pn):
                    last = None
                    for ti in range(pn):
                        t = pair0 + ti
                        for nh in range(2):
                            last = e.matmul(
                                PS[1 + 2 * ti + nh][0:128, 0:512],
                                A["inter_g"][:, isl, t * 128:(t + 1) * 128],
                                wos[:, nh * 4:(nh + 1) * 4, :].rearrange("p a b -> p (a b)"),
                                start=(isl == 0), stop=False)
                    return last
                em.step("tensor", womm, waits=[(wd, wold_lev[isl]), (scs, sc_pre),
                                               (vcs, vc_pre)], inc=tcs)
                womm_lev[isl] = em.led[tcs]
                if isl + 2 < 32:
                    wo_load(isl + 2)
            def bioadd(e, pair0=pair0, pn=pn):
                last = None
                for ti in range(pn):
                    for nh in range(2):
                        last = e.matmul(PS[1 + 2 * ti + nh][0:128, 0:512],
                                        A["ones_bf"][0:1, :],
                                        A["rowbuf2_bf"][0:1, nh * 512:(nh + 1) * 512],
                                        start=False, stop=True)
                return last
            em.step("tensor", bioadd, waits=[(vcs, em.led[vcs])], inc=tcs)
            tc_done = em.led[tcs]
            for ti in range(pn):
                t = pair0 + ti
                T = g0t + t
                xt = A["x_g"][:, t % 2, :]
                em.dma("sync", xt, g["x_ext"][(T + 1) * 128:(T + 2) * 128, :],
                       waits=[(vcs, em.led[vcs]), (scs, em.led[scs])], inc=ld)
                emit_stats(em, A, xt, "s0", "eps_1em5", "sq_scratch",
                           extra_waits=[(ld, em.led[ld]), (tcs, tc_done)])
                em.step("scalar", lambda e, xt=xt: e.activation(
                    A["xn_tm"][:], xt, AF.Identity, bias=A["s0_negmr"][:, 0:1],
                    scale=A["s0_r"][:, 0:1]),
                    waits=[(vcs, em.led[vcs])], inc=scs)
                def yadd(e, ti=ti):
                    e.tensor_tensor(A["y_t"][:, 0:512], PS[1 + 2 * ti][0:128, 0:512],
                                    A["xn_tm"][:, 0:512], op=ALU.add)
                    return e.tensor_tensor(A["y_t"][:, 512:1024],
                                           PS[2 + 2 * ti][0:128, 0:512],
                                           A["xn_tm"][:, 512:1024], op=ALU.add)
                em.step("vector", yadd, waits=[(scs, em.led[scs]),
                                               (sts, em.led[sts])], inc=vcs)
                em.dma("sync", g["dbg_y"][T], A["y_t"][:],
                       waits=[(vcs, em.led[vcs])], inc=sts)
                emit_stats(em, A, A["y_t"][:], "sy", "eps_1em12", "sq_scratch",
                           extra_waits=[("vc", em.led["vc"])])
                em.step("scalar", lambda e: e.activation(
                    A["xn_tm"][:], A["y_t"][:], AF.Identity,
                    bias=A["sy_negmr"][:, 0:1], scale=A["sy_r"][:, 0:1]),
                    waits=[(vcs, em.led[vcs])], inc=scs)
                em.step("vector", lambda e: e.tensor_tensor(
                    A["xn_tm"][:], A["xn_tm"][:], A["gam_bc"][:], op=ALU.mult),
                    waits=[(scs, em.led[scs]), (vcs, em.led[vcs])], inc=vcs)
                em.step("vector", lambda e: e.tensor_tensor(
                    A["xn_tm"][:], A["xn_tm"][:], A["bet_bc"][:], op=ALU.add),
                    waits=[(vcs, em.led[vcs])], inc=vcs)
                em.step("scalar", lambda e: e.activation(A["g0_t"][:], A["xn_tm"][:],
                                                         AF.Silu),
                        waits=[(vcs, em.led[vcs]), (sts, em.led[sts])], inc=scs)
                em.dma("sync", g["g0_hbm"][T, :, :], A["g0_t"][:],
                       waits=[(scs, em.led[scs])], inc=sts)
    em.barrier()
    pa.close()
    return em.led["st"]


def _ph_attn(nc, em, g, A, PS, st_before):
    A = dict(A)
    if DEV_PHASES < 2:
        return
    pa = PhaseAlloc(nc, A)
    sbuf = pa.sbuf
    ld, wd, tcs, vcs, scs, sts = "ld", "wd", "tc", "vc", "sc", "st"

    sbuf("qkvw", (128, 8, 3072), BF16)
    sbuf("rqw", (128, 8, 256), BF16)
    sbuf("routw", (64, 4, 1024), BF16)
    sbuf("outw", (64, 16, 1024), BF16)
    sbuf("glgr", (64, 2), BF16)
    sbuf("qkb_col", (128, 16), F32)
    sbuf("bv_row", (1, 1024), BF16)
    sbuf("rqb_col", (128, 2), F32)
    sbuf("routb_col", (64, 16), F32)
    sbuf("outb_row", (1, 1024), BF16)
    sbuf("rkvw", (128, 8, 512), BF16)
    sbuf("rkvb_row", (1, 512), BF16)
    sbuf("rkvb_col", (128, 2), F32)
    em.dma("gpsimd", A["qkvw"][:], g["L0_attn_qkv_w"].rearrange("(c p) f -> p c f", p=128), inc=wd)
    em.dma("gpsimd", A["rqw"][:], g["L0_attn_rq_w"].rearrange("(c p) f -> p c f", p=128), inc=wd)
    em.dma("gpsimd", A["routw"][:], g["L0_attn_rout_w"].rearrange("(l p) f -> p l f", p=64), inc=wd)
    em.dma("gpsimd", A["outw"][:], g["L0_attn_out_w"].rearrange("(h p) f -> p h f", p=64), inc=wd)
    em.dma("gpsimd", A["glgr"][:, 0:1], g["L0_attn_gl_w"], inc=wd)
    em.dma("gpsimd", A["glgr"][:, 1:2], g["L0_attn_gr_w"], inc=wd)
    em.dma("sync", A["qkb_col"][:], g["L0_attn_qkv_b"][0:2048].rearrange("(s p) -> p s", p=128),
           inc=ld, slow=True)
    em.dma("gpsimd", A["bv_row"][:], g["L0_attn_qkv_b"][2048:3072].rearrange("(o d) -> o d", o=1), inc=wd)
    em.dma("sync", A["rqb_col"][:], g["L0_attn_rq_b"].rearrange("(s p) -> p s", p=128),
           inc=ld, slow=True)
    em.dma("sync", A["routb_col"][:], g["L0_attn_rout_b"].rearrange("(h p) -> p h", p=64),
           inc=ld, slow=True)
    em.dma("gpsimd", A["outb_row"][:], g["L0_attn_out_b"].rearrange("(o d) -> o d", o=1), inc=wd)
    em.dma("gpsimd", A["rkvw"][:], g["L0_attn_rkv_w"].rearrange("(c p) f -> p c f", p=128), inc=wd)
    em.dma("gpsimd", A["rkvb_row"][:], g["L0_attn_rkv_b"].rearrange("(o d) -> o d", o=1), inc=wd)
    em.dma("sync", A["rkvb_col"][:], g["L0_attn_rkv_b"][0:256].rearrange("(s p) -> p s", p=128),
           inc=ld, slow=True)
    base_w = em.led[wd]; base_l = em.led[ld]

    if DEV_ATTN < -2:
        em.barrier()
        pa.close()
        return
    sbuf("xf_t", (128, 2, D), F32)
    alloc_stats(pa, "s0")
    sbuf("sq_scratch", (128, D), F32)
    sbuf("hi_t", (128, D), BF16)
    sbuf("pooled32", (128, 8, 64), F32)
    sbuf("pooledT", (128, 8, 64), BF16)
    sbuf("rk_fm", (128, 2, 64), BF16)
    sbuf("rv_tm", (64, 256), BF16)
    for j in range(64):
        xt = A["xf_t"][:, j % 2, :]
        em.dma("sync", xt, g["x_full"][j * 128:(j + 1) * 128, :],
               waits=[(vcs, em.led[vcs]), (scs, em.led[scs])], inc=ld)
        emit_stats(em, A, xt, "s0", "eps_1em5", "sq_scratch",
                   extra_waits=[(ld, em.led[ld])])
        em.step("scalar", lambda e, xt=xt: e.activation(
            A["hi_t"][:], xt, AF.Identity, bias=A["s0_negmr"][:, 0:1],
            scale=A["s0_r"][:, 0:1]),
            waits=[(vcs, em.led[vcs])], inc=scs)
        for cc in range(8):
            bank = PS[4 + (cc % 2)]
            em.step("tensor", lambda e, cc=cc, bank=bank: e.matmul(
                bank[0:128, 0:128], A["hi_t"][:, cc * 128:(cc + 1) * 128],
                A["identbf"][:], start=True, stop=True),
                waits=[(scs, em.led[scs]), (vcs, em.led[vcs]), (wd, base_w)], inc=tcs)
            em.step("vector", lambda e, cc=cc, bank=bank, j=j: e.reduce_sum(
                A["pooled32"][:, cc, j:j + 1], bank[0:128, 0:128], axis=AX.X),
                waits=[(tcs, em.led[tcs])], inc=vcs)
    if DEV_ATTN < -1:
        em.barrier()
        pa.close()
        return
    em.step("vector", lambda e: e.tensor_scalar(
        A["pooledT"][:].rearrange("p c l -> p (c l)"),
        A["pooled32"][:].rearrange("p c l -> p (c l)"), 1.0 / 128, None, op0=ALU.mult),
        waits=[(vcs, em.led[vcs])], inc=vcs)
    for sub in range(2):
        def rkmm(e, sub=sub):
            last = None
            for cc in range(8):
                last = e.matmul(PS[0][0:128, 0:64],
                                A["rkvw"][:, cc, sub * 128:(sub + 1) * 128],
                                A["pooledT"][:, cc, :], start=(cc == 0), stop=(cc == 7))
            return last
        em.step("tensor", rkmm, waits=[(vcs, em.led[vcs]), (scs, em.led[scs]),
                                       (wd, base_w), (ld, base_l)], inc=tcs)
        em.step("scalar", lambda e, sub=sub: e.activation(
            A["rk_fm"][:, sub, :], PS[0][0:128, 0:64], AF.Identity,
            bias=A["rkvb_col"][:, sub:sub + 1]),
            waits=[(tcs, em.led[tcs])], inc=scs)
    if DEV_ATTN < 0:
        em.barrier()
        pa.close()
        return
    def rvmm(e):
        last = e.matmul(PS[1][0:64, 0:256], A["ones_bf"][0:1, 0:64],
                        A["rkvb_row"][0:1, 256:512], start=True, stop=False)
        for cc in range(8):
            last = e.matmul(PS[1][0:64, 0:256], A["pooledT"][:, cc, :],
                            A["rkvw"][:, cc, 256:512], start=False, stop=(cc == 7))
        return last
    em.step("tensor", rvmm, waits=[(scs, em.led[scs]), (vcs, em.led[vcs])], inc=tcs)
    em.step("vector", lambda e: e.tensor_copy(A["rv_tm"][:], PS[1][0:64, 0:256]),
            waits=[(tcs, em.led[tcs])], inc=vcs)

    if DEV_ATTN < 1:
        em.barrier()
        pa.close()
        return
    sbuf("xnT_a", (128, 8, 512), BF16)
    sbuf("q_fm", (128, 8, 512), BF16)
    sbuf("k_fm", (128, 8, 640), BF16)
    sbuf("v_tm", (128, 5, D), BF16)
    sbuf("rq_fm", (128, 2, 512), BF16)
    sbuf("msk_t", (128, 2, 256), F32)
    sbuf("mlat_t", (128, 2, 64), F32)
    sbuf("zl_t", (128, 2, 2), F32)
    sbuf("pad_t", (128, 1), F32)
    sbuf("sc4", (128, 4, 256), F32)
    A["scl4"] = A["sc4"][:, 0, :].rearrange("p (l k) -> p l k", k=64)
    sbuf("negmax4", (128, 4), F32)
    sbuf("den4", (128, 4), F32)
    sbuf("rec4", (128, 4), F32)
    sbuf("diag4", (128, 4, 128), BF16)
    sbuf("p4", (128, 4, 256), BF16)
    sbuf("pT4", (128, 8, 128), BF16)
    sbuf("pl4", (128, 4, 64), BF16)
    sbuf("pTl4", (64, 4, 128), BF16)
    sbuf("local_fm", (64, 16, 128), BF16)
    sbuf("lat_fm", (64, 4, 128), BF16)
    sbuf("remote_fm", (64, 16, 128), BF16)
    sbuf("gates_row", (1, 2048), BF16)
    sbuf("gb_fm", (64, 16, 128), BF16)
    sbuf("t1", (64, 16, 128), BF16)
    sbuf("mixed", (64, 16, 128), BF16)
    sbuf("g0_t", (128, D), BF16)
    A["m1_t"] = A["sq_scratch"]
    sbuf("x1_t", (128, D), F32)

    for gi, (g0t, gn) in enumerate(GROUPS):
        if gi >= DEV_GROUPS:
            break
        ntok = gn * 128
        if gi > 0:
            em.step("vector", lambda e: e.tensor_copy(
                A["k_fm"][:, :, 0:128], A["k_fm"][:, :, 512:640]),
                waits=[(vcs, em.led[vcs]), (tcs, em.led[tcs])], inc=vcs)
            em.step("vector", lambda e: e.tensor_copy(
                A["v_tm"][:, 0, :], A["v_tm"][:, 4, :]),
                waits=[(vcs, em.led[vcs])], inc=vcs)
        else:
            em.step("gpsimd", lambda e: (e.memset(A["k_fm"][:, :, 0:128], 0.0),
                                         e.memset(A["v_tm"][:, 0, :], 0.0))[-1],
                    waits=[("wd", base_w)], inc=wd)
        for ti in range(gn):
            T = g0t + ti
            em.dma("sync", A["xnT_a"][:, :, ti * 128:(ti + 1) * 128],
                   g["xn0T_hbm"][T, :, :].rearrange("p (c t) -> p c t", t=128),
                   waits=[(tcs, em.led[tcs]), (vcs, em.led[vcs])], inc=ld)
        ld_x = em.led[ld]
        for half, dst in ((0, "q_fm"), (1, "k_fm")):
            for sub in range(8):
                def qkmm(e, half=half, sub=sub, ntok=ntok):
                    last = None
                    for cc in range(8):
                        last = e.matmul(PS[0][0:128, 0:ntok],
                                        A["qkvw"][:, cc,
                                                  half * 1024 + sub * 128:
                                                  half * 1024 + (sub + 1) * 128],
                                        A["xnT_a"][:, cc, 0:ntok],
                                        start=(cc == 0), stop=(cc == 7))
                    return last
                em.step("tensor", qkmm, waits=[(ld, ld_x), (vcs, em.led[vcs]),
                                               (scs, em.led[scs])], inc=tcs)
                off = 128 if dst == "k_fm" else 0
                em.step("scalar", lambda e, dst=dst, sub=sub, half=half, off=off,
                        ntok=ntok: e.activation(
                    A[dst][:, sub, off:off + ntok], PS[0][0:128, 0:ntok], AF.Identity,
                    bias=A["qkb_col"][:, half * 8 + sub:half * 8 + sub + 1]),
                    waits=[(tcs, em.led[tcs])], inc=scs)
        for sub in range(2):
            def rqmm(e, sub=sub, ntok=ntok):
                last = None
                for cc in range(8):
                    last = e.matmul(PS[0][0:128, 0:ntok],
                                    A["rqw"][:, cc, sub * 128:(sub + 1) * 128],
                                    A["xnT_a"][:, cc, 0:ntok],
                                    start=(cc == 0), stop=(cc == 7))
                return last
            em.step("tensor", rqmm, waits=[(vcs, em.led[vcs]), (scs, em.led[scs])], inc=tcs)
            em.step("scalar", lambda e, sub=sub, ntok=ntok: e.activation(
                A["rq_fm"][:, sub, 0:ntok], PS[0][0:128, 0:ntok], AF.Identity,
                bias=A["rqb_col"][:, sub:sub + 1]),
                waits=[(tcs, em.led[tcs])], inc=scs)
        for ti in range(gn):
            def vmm(e, ti=ti):
                last = e.matmul(PS[1][0:128, 0:512], A["ones_bf"][0:1, :],
                                A["bv_row"][0:1, 0:512], start=True, stop=False)
                last = e.matmul(PS[2][0:128, 0:512], A["ones_bf"][0:1, :],
                                A["bv_row"][0:1, 512:1024], start=True, stop=False)
                for cc in range(8):
                    last = e.matmul(PS[1][0:128, 0:512],
                                    A["xnT_a"][:, cc, ti * 128:(ti + 1) * 128],
                                    A["qkvw"][:, cc, 2048:2560],
                                    start=False, stop=(cc == 7))
                    last = e.matmul(PS[2][0:128, 0:512],
                                    A["xnT_a"][:, cc, ti * 128:(ti + 1) * 128],
                                    A["qkvw"][:, cc, 2560:3072],
                                    start=False, stop=(cc == 7))
                return last
            em.step("tensor", vmm, waits=[(vcs, em.led[vcs]), (scs, em.led[scs])], inc=tcs)
            def vevac(e, ti=ti):
                e.tensor_copy(A["v_tm"][:, 1 + ti, 0:512], PS[1][0:128, 0:512])
                return e.tensor_copy(A["v_tm"][:, 1 + ti, 512:1024], PS[2][0:128, 0:512])
            em.step("vector", vevac, waits=[(tcs, em.led[tcs])], inc=vcs)

        if DEV_ATTN < 2:
            continue
        for ti in range(gn):
            T = g0t + ti
            em.dma("sync", A["msk_t"][:, ti % 2, :], g["mask_local"][T],
                   waits=[(vcs, em.led[vcs])], inc=ld)
            em.dma("sync", A["mlat_t"][:, ti % 2, :], g["mask_lat"][T],
                   waits=[(vcs, em.led[vcs])], inc=ld)
            em.dma("sync", A["zl_t"][:, ti % 2, 0:1], g["zl_local"][T],
                   waits=[(vcs, em.led[vcs])], inc=ld)
            em.dma("sync", A["zl_t"][:, ti % 2, 1:2], g["zl_lat"][T],
                   waits=[(vcs, em.led[vcs])], inc=ld)
            ld_m = em.led[ld]
            if DEV_ATTN == 18:
                continue
            for wv in range(4):
                def smm(e, wv=wv, ti=ti):
                    last = None
                    for hh in DEV_SMM_HH:
                        h = wv * 4 + hh
                        sub = h // 2
                        po = (h % 2) * 64
                        last = e.matmul(
                            PS[hh][0:128, 0:256],
                            A["q_fm"][po:po + 64, sub, ti * 128:(ti + 1) * 128],
                            A["k_fm"][po:po + 64, sub, ti * 128:(ti + 2) * 128],
                            start=True, stop=True)
                    return last
                em.step("tensor", smm, waits=[(vcs, em.led[vcs]),
                                              (scs, em.led[scs])], inc=tcs)
                if DEV_ATTN == 185:
                    continue
                def smask(e, ti=ti):
                    last = None
                    for hh in range(4):
                        last = e.scalar_tensor_tensor(
                            A["sc4"][:, hh, :], PS[hh][0:128, 0:256],
                            0.125, A["msk_t"][:, ti % 2, :], op0=ALU.mult, op1=ALU.add)
                    return last
                em.step("vector", smask, waits=[(tcs, em.led[tcs]), (ld, ld_m)], inc=vcs)
                if DEV_ATTN == 20:
                    continue
                em.step("vector", lambda e: e.reduce_max(
                    A["negmax4"][:], A["sc4"][:], axis=AX.X, negate=True),
                    waits=[(vcs, em.led[vcs])], inc=vcs)
                def sexp(e):
                    last = None
                    for hh in range(4):
                        last = e.activation(A["p4"][:, hh, :], A["sc4"][:, hh, :],
                                            AF.Exp, bias=A["negmax4"][:, hh:hh + 1],
                                            accum_out=A["den4"][:, hh:hh + 1])
                    return last
                em.step("scalar", sexp, waits=[(vcs, em.led[vcs])], inc=scs)
                if DEV_ATTN == 21:
                    continue
                em.step("vector", lambda e: e.reciprocal(A["rec4"][:], A["den4"][:]),
                        waits=[(scs, em.led[scs])], inc=vcs)
                em.step("vector", lambda e, ti=ti: e.tensor_scalar(
                    A["rec4"][:], A["rec4"][:], A["zl_t"][:, ti % 2, 0:1], None,
                    op0=ALU.mult),
                    waits=[(vcs, em.led[vcs])], inc=vcs)
                def sdiag(e):
                    last = None
                    for hh in range(4):
                        last = e.tensor_scalar(A["diag4"][:, hh, :], A["iota_row"][:],
                                               A["pidx"][:, 0:1], A["rec4"][:, hh:hh + 1],
                                               op0=ALU.is_equal, op1=ALU.mult)
                    return last
                em.step("vector", sdiag, waits=[(vcs, em.led[vcs])], inc=vcs)
                if DEV_ATTN == 22:
                    continue
                for kc in range(2):
                    def ptmm(e, kc=kc):
                        last = None
                        for hh in range(4):
                            last = e.matmul(
                                PS[hh][0:128, 0:128],
                                A["p4"][:, hh, kc * 128:(kc + 1) * 128],
                                A["diag4"][:, hh, :], start=True, stop=True)
                        return last
                    em.step("tensor", ptmm, waits=[(vcs, em.led[vcs]),
                                                   (scs, em.led[scs])], inc=tcs)
                    def ptev(e, kc=kc):
                        last = None
                        for hh in range(4):
                            last = e.tensor_copy(A["pT4"][:, 2 * hh + kc, :],
                                                 PS[hh][0:128, 0:128])
                        return last
                    em.step("vector", ptev, waits=[(tcs, em.led[tcs])], inc=vcs)
                if DEV_ATTN == 23:
                    continue
                def lmm(e, wv=wv, ti=ti):
                    last = None
                    for hh in range(4):
                        h = wv * 4 + hh
                        last = e.matmul(PS[hh][0:64, 0:128],
                                        A["v_tm"][:, ti, h * 64:(h + 1) * 64],
                                        A["pT4"][:, 2 * hh, :], start=True, stop=False)
                        last = e.matmul(PS[hh][0:64, 0:128],
                                        A["v_tm"][:, 1 + ti, h * 64:(h + 1) * 64],
                                        A["pT4"][:, 2 * hh + 1, :], start=False, stop=True)
                    return last
                em.step("tensor", lmm, waits=[(vcs, em.led[vcs])], inc=tcs)
                def locev(e, wv=wv):
                    last = None
                    for hh in range(4):
                        last = e.tensor_copy(A["local_fm"][:, wv * 4 + hh, :],
                                             PS[hh][0:64, 0:128])
                    return last
                em.step("vector", locev, waits=[(tcs, em.led[tcs])], inc=vcs)
            if DEV_ATTN < 3 or 20 <= DEV_ATTN < 30:
                continue
            def lsmm(e, ti=ti):
                last = None
                for lh in range(4):
                    sub = lh // 2
                    po = (lh % 2) * 64
                    last = e.matmul(PS[lh][0:128, 0:64],
                                    A["rq_fm"][po:po + 64, sub, ti * 128:(ti + 1) * 128],
                                    A["rk_fm"][po:po + 64, sub, :],
                                    start=True, stop=True)
                return last
            em.step("tensor", lsmm, waits=[(vcs, em.led[vcs]),
                                           (scs, em.led[scs])], inc=tcs)
            def lmask(e, ti=ti):
                last = None
                for lh in range(4):
                    last = e.scalar_tensor_tensor(
                        A["scl4"][:, lh, :], PS[lh][0:128, 0:64],
                        0.125, A["mlat_t"][:, ti % 2, :], op0=ALU.mult, op1=ALU.add)
                return last
            em.step("vector", lmask, waits=[(tcs, em.led[tcs]), (ld, ld_m)], inc=vcs)
            em.step("vector", lambda e: e.reduce_max(
                A["negmax4"][:], A["scl4"][:], axis=AX.X, negate=True),
                waits=[(vcs, em.led[vcs])], inc=vcs)
            def lexp(e):
                last = None
                for lh in range(4):
                    last = e.activation(A["pl4"][:, lh, :], A["scl4"][:, lh, :],
                                        AF.Exp, bias=A["negmax4"][:, lh:lh + 1],
                                        accum_out=A["den4"][:, lh:lh + 1])
                return last
            em.step("scalar", lexp, waits=[(vcs, em.led[vcs])], inc=scs)
            em.step("vector", lambda e: e.reciprocal(A["rec4"][:], A["den4"][:]),
                    waits=[(scs, em.led[scs])], inc=vcs)
            em.step("vector", lambda e, ti=ti: e.tensor_scalar(
                A["rec4"][:], A["rec4"][:], A["zl_t"][:, ti % 2, 1:2], None,
                op0=ALU.mult),
                waits=[(vcs, em.led[vcs])], inc=vcs)
            def ldiag(e):
                last = None
                for lh in range(4):
                    last = e.tensor_scalar(A["diag4"][:, lh, :], A["iota_row"][:],
                                           A["pidx"][:, 0:1], A["rec4"][:, lh:lh + 1],
                                           op0=ALU.is_equal, op1=ALU.mult)
                return last
            em.step("vector", ldiag, waits=[(vcs, em.led[vcs])], inc=vcs)
            def lptmm(e):
                last = None
                for lh in range(4):
                    last = e.matmul(PS[lh][0:64, 0:128],
                                    A["pl4"][:, lh, :], A["diag4"][:, lh, :],
                                    start=True, stop=True)
                return last
            em.step("tensor", lptmm, waits=[(vcs, em.led[vcs]),
                                            (scs, em.led[scs])], inc=tcs)
            def lptev(e):
                last = None
                for lh in range(4):
                    last = e.tensor_copy(A["pTl4"][:, lh, :], PS[lh][0:64, 0:128])
                return last
            em.step("vector", lptev, waits=[(tcs, em.led[tcs])], inc=vcs)
            def lomm(e):
                last = None
                for lh in range(4):
                    last = e.matmul(PS[lh][0:64, 0:128],
                                    A["rv_tm"][0:64, lh * 64:(lh + 1) * 64],
                                    A["pTl4"][:, lh, :], start=True, stop=True)
                return last
            em.step("tensor", lomm, waits=[(vcs, em.led[vcs])], inc=tcs)
            def lomev(e):
                last = None
                for lh in range(4):
                    last = e.tensor_copy(A["lat_fm"][:, lh, :], PS[lh][0:64, 0:128])
                return last
            em.step("vector", lomev, waits=[(tcs, em.led[tcs])], inc=vcs)
            if DEV_ATTN < 4 or 20 <= DEV_ATTN < 30:
                continue
            for wv in range(4):
                def rmm(e, wv=wv):
                    last = None
                    for hh in range(4):
                        h = wv * 4 + hh
                        for lh in range(4):
                            last = e.matmul(PS[hh][0:64, 0:128],
                                            A["routw"][:, lh, h * 64:(h + 1) * 64],
                                            A["lat_fm"][:, lh, :],
                                            start=(lh == 0), stop=(lh == 3))
                    return last
                em.step("tensor", rmm, waits=[(vcs, em.led[vcs]),
                                              (scs, em.led[scs])], inc=tcs)
                def revac(e, wv=wv):
                    last = None
                    for hh in range(4):
                        h = wv * 4 + hh
                        last = e.activation(A["remote_fm"][:, h, :],
                                            PS[hh][0:64, 0:128],
                                            AF.Identity, bias=A["routb_col"][:, h:h + 1])
                    return last
                em.step("scalar", revac, waits=[(tcs, em.led[tcs])], inc=scs)
            if 19 <= DEV_ATTN < 30:
                continue
            for wv in range(4):
                def gmm(e, wv=wv):
                    last = None
                    for hh in range(4):
                        h = wv * 4 + hh
                        last = e.matmul(PS[hh][0:1, 0:128],
                                        A["glgr"][0:64, 0:1], A["local_fm"][0:64, h, :],
                                        start=True, stop=False)
                        last = e.matmul(PS[hh][0:1, 0:128],
                                        A["glgr"][0:64, 1:2], A["remote_fm"][0:64, h, :],
                                        start=False, stop=True)
                    return last
                em.step("tensor", gmm, waits=[(vcs, em.led[vcs]),
                                              (scs, em.led[scs])], inc=tcs)
                def gsig(e, wv=wv):
                    last = None
                    for hh in range(4):
                        h = wv * 4 + hh
                        last = e.activation(A["gates_row"][0:1, h * 128:(h + 1) * 128],
                                            PS[hh][0:1, 0:128], AF.Sigmoid)
                    return last
                em.step("scalar", gsig, waits=[(tcs, em.led[tcs])], inc=scs)
            for wv in range(4):
                def gbmm(e, wv=wv):
                    last = None
                    for hh in range(4):
                        h = wv * 4 + hh
                        last = e.matmul(PS[hh][0:64, 0:128],
                                        A["ones_bf"][0:1, 0:64],
                                        A["gates_row"][0:1, h * 128:(h + 1) * 128],
                                        start=True, stop=True)
                    return last
                em.step("tensor", gbmm, waits=[(scs, em.led[scs]),
                                               (vcs, em.led[vcs])], inc=tcs)
                def gbev(e, wv=wv):
                    last = None
                    for hh in range(4):
                        h = wv * 4 + hh
                        last = e.tensor_copy(A["gb_fm"][:, h, :], PS[hh][0:64, 0:128])
                    return last
                em.step("vector", gbev, waits=[(tcs, em.led[tcs])], inc=vcs)
            em.step("vector", lambda e: e.tensor_tensor(
                A["t1"][:].rearrange("p a b -> p (a b)"),
                A["local_fm"][:].rearrange("p a b -> p (a b)"),
                A["remote_fm"][:].rearrange("p a b -> p (a b)"), op=ALU.subtract),
                waits=[(vcs, em.led[vcs]), (scs, em.led[scs])], inc=vcs)
            em.step("vector", lambda e: e.tensor_tensor(
                A["t1"][:].rearrange("p a b -> p (a b)"),
                A["t1"][:].rearrange("p a b -> p (a b)"),
                A["gb_fm"][:].rearrange("p a b -> p (a b)"), op=ALU.mult),
                waits=[(vcs, em.led[vcs])], inc=vcs)
            em.step("vector", lambda e: e.tensor_tensor(
                A["mixed"][:].rearrange("p a b -> p (a b)"),
                A["t1"][:].rearrange("p a b -> p (a b)"),
                A["remote_fm"][:].rearrange("p a b -> p (a b)"), op=ALU.add),
                waits=[(vcs, em.led[vcs])], inc=vcs)
            def s0mm(e):
                last = e.matmul(PS[1][0:128, 0:512], A["ones_bf"][0:1, :],
                                A["outb_row"][0:1, 0:512], start=True, stop=False)
                last = e.matmul(PS[2][0:128, 0:512], A["ones_bf"][0:1, :],
                                A["outb_row"][0:1, 512:1024], start=True, stop=False)
                for h in range(16):
                    last = e.matmul(PS[1][0:128, 0:512], A["mixed"][:, h, :],
                                    A["outw"][:, h, 0:512], start=False, stop=(h == 15))
                    last = e.matmul(PS[2][0:128, 0:512], A["mixed"][:, h, :],
                                    A["outw"][:, h, 512:1024], start=False, stop=(h == 15))
                return last
            em.step("tensor", s0mm, waits=[(vcs, em.led[vcs]), (scs, em.led[scs])], inc=tcs)
            em.dma("sync", A["g0_t"][:], g["g0_hbm"][T], waits=[(vcs, em.led[vcs])], inc=ld)
            xt = A["xf_t"][:, ti % 2, :]
            em.dma("sync", xt, g["x_ext"][(T + 1) * 128:(T + 2) * 128, :],
                   waits=[(vcs, em.led[vcs]), (scs, em.led[scs])], inc=ld)
            em.dma("sync", A["pad_t"][:], g["padmask"][T + 1],
                   waits=[(vcs, em.led[vcs])], inc=ld)
            ld_e = em.led[ld]
            def s0tap(e):
                e.tensor_copy(A["x1_t"][:, 0:512], PS[1][0:128, 0:512])
                return e.tensor_copy(A["x1_t"][:, 512:1024], PS[2][0:128, 0:512])
            em.step("vector", s0tap, waits=[(tcs, em.led[tcs]), (sts, em.led[sts])],
                    inc=vcs)
            em.dma("sync", g["dbg_s0"], A["x1_t"][:], waits=[(vcs, em.led[vcs])], inc=sts)
            def mul1(e, ti=ti):
                e.tensor_tensor(A["m1_t"][:, 0:512], PS[1][0:128, 0:512],
                                A["g0_t"][:, 0:512], op=ALU.mult)
                return e.tensor_tensor(A["m1_t"][:, 512:1024], PS[2][0:128, 0:512],
                                       A["g0_t"][:, 512:1024], op=ALU.mult)
            em.step("vector", mul1, waits=[(tcs, em.led[tcs]), (ld, ld_e)], inc=vcs)
            em.step("vector", lambda e, xt=xt: e.tensor_tensor(
                A["x1_t"][:], xt, A["m1_t"][:], op=ALU.add),
                waits=[(vcs, em.led[vcs]), (sts, em.led[sts])], inc=vcs)
            em.step("vector", lambda e: e.tensor_scalar(
                A["x1_t"][:], A["x1_t"][:], A["pad_t"][:, 0:1], None, op0=ALU.mult),
                waits=[(vcs, em.led[vcs])], inc=vcs)
            em.dma("sync", g["x1_hbm"][T], A["x1_t"][:],
                   waits=[(vcs, em.led[vcs])], inc=sts)
    for nm, buf in (("dbg_local", "local_fm"), ("dbg_remote", "remote_fm"),
                    ("dbg_mixed", "mixed"), ("dbg_gates", "gates_row"),
                    ("dbg_q", "q_fm"), ("dbg_k", "k_fm"), ("dbg_v", "v_tm"),
                    ("dbg_m1", "m1_t"), ("dbg_outw", "outw"),
                    ("dbg_outb", "outb_row"), ("dbg_g0t", "g0_t")):
        em.dma("sync", g[nm], A[buf][:],
               waits=[(vcs, em.led[vcs]), (scs, em.led[scs]), (tcs, em.led[tcs])],
               inc=sts)
    em.barrier()
    pa.close()


def _ph_xn1(nc, em, g, A, PS):
    A = dict(A)
    if DEV_PHASES < 3:
        return
    pa = PhaseAlloc(nc, A)
    sbuf = pa.sbuf
    ld, wd, tcs, vcs, scs, sts = "ld", "wd", "tc", "vc", "sc", "st"
    sbuf("xt2", (128, 2, D), F32)
    sbuf("sq_scratch", (128, D), F32)
    sbuf("xn1_t", (128, D), F32)
    sbuf("zrow", (1, D), F32)
    alloc_stats(pa, "s1")
    em.step("gpsimd", lambda e: e.memset(A["zrow"][:], 0.0), inc="wd")
    zw = em.led["wd"]
    for tgt, row in (("x2fA_hbm", L1F_QA_LO * 128 - 1),
                     ("yA_hbm", L1F_QA_LO * 128 - 1),
                     ("x2fB_hbm", L1F_QB_LO * 128 - 1)):
        em.dma("sync", g[tgt].rearrange("t p d -> (t p) d")[row:row + 1, :],
               A["zrow"][:], waits=[("wd", zw)], inc="st")
    for T in range(NT_OUT):
        xt = A["xt2"][:, T % 2, :]
        em.dma("sync", xt, g["x1_hbm"][T],
               waits=[(vcs, em.led[vcs]), (scs, em.led[scs])], inc=ld)
        emit_stats(em, A, xt, "s1", "eps_1em5", "sq_scratch",
                   extra_waits=[(ld, em.led[ld])])
        em.step("scalar", lambda e, xt=xt: e.activation(
            A["xn1_t"][:], xt, AF.Identity, bias=A["s1_negmr"][:, 0:1],
            scale=A["s1_r"][:, 0:1]),
            waits=[(vcs, em.led[vcs]), (sts, em.led[sts])], inc=scs)
        em.dma("sync", g["xn1_hbm"][T], A["xn1_t"][:],
               waits=[(scs, em.led[scs])], inc=sts)
    em.barrier()
    pa.close()


def _ph_l1f_layer(nc, em, g, A, PS, l, src_hbm, kvlo, qlo, dst_hbm, store_g1, mask_name):
    A = dict(A)
    pa = PhaseAlloc(nc, A)
    sbuf = pa.sbuf
    ld, wd, tcs, vcs, scs, sts = "ld", "wd", "tc", "vc", "sc", "st"
    nkv = NT_OUT - kvlo
    nq = NT_OUT - qlo

    sbuf("wqkv", (128, 8, 1536), BF16)
    sbuf("h1_bf", (128, D), BF16)
    sbuf("h1T", (128, 8, 128), BF16)
    sbuf("qk_32", (128, 2, 512), F32)
    sbuf("ss8", (128, 16), F32)
    sbuf("nr8", (128, 16), F32)
    sbuf("r8", (128, 16), F32)
    sbuf("qf", (128, 2, 512), BF16)
    sbuf("qsb", (128, 2, 64), BF16)
    sbuf("ksb", (128, 2, 64), BF16)
    sbuf("qsrow", (1, 256), BF16)
    sbuf("ropet", (128, 4, 64), BF16)
    sbuf("sq_l1", (128, 2, 512), F32)
    sbuf("ra", (128, 512), F32)
    sbuf("rb", (128, 512), F32)
    sbuf("v_tm", (128, nkv, 512), BF16)
    sbuf("kT", (128, 4, nkv * 128), BF16)
    sbuf("qT", (128, 4, nq * 128), BF16)
    em.dma("gpsimd", A["wqkv"][:],
           g["L1_ffn_wqkv"][l].rearrange("(c p) f -> p c f", p=128), inc=wd)
    em.dma("gpsimd", A["qsrow"][0:1, 0:64],
           g["L1_ffn_q_scale"][l].rearrange("(o d) -> o d", o=1), inc=wd)
    em.dma("gpsimd", A["qsrow"][0:1, 64:128],
           g["L1_ffn_k_scale"][l].rearrange("(o d) -> o d", o=1), inc=wd)
    em.dma("gpsimd", A["qsrow"][0:1, 128:160],
           g["L1_ffn_q_scale"][l][32:64].rearrange("(o d) -> o d", o=1), inc=wd)
    em.dma("gpsimd", A["qsrow"][0:1, 160:192],
           g["L1_ffn_q_scale"][l][0:32].rearrange("(o d) -> o d", o=1), inc=wd)
    em.dma("gpsimd", A["qsrow"][0:1, 192:224],
           g["L1_ffn_k_scale"][l][32:64].rearrange("(o d) -> o d", o=1), inc=wd)
    em.dma("gpsimd", A["qsrow"][0:1, 224:256],
           g["L1_ffn_k_scale"][l][0:32].rearrange("(o d) -> o d", o=1), inc=wd)
    wd0 = em.led[wd]
    def qsb_mm(e):
        e.matmul(PS[0][0:128, 0:64], A["ones_bf"][0:1, :], A["qsrow"][0:1, 0:64],
                 start=True, stop=True)
        e.matmul(PS[1][0:128, 0:64], A["ones_bf"][0:1, :], A["qsrow"][0:1, 128:192],
                 start=True, stop=True)
        e.matmul(PS[2][0:128, 0:64], A["ones_bf"][0:1, :], A["qsrow"][0:1, 64:128],
                 start=True, stop=True)
        return e.matmul(PS[3][0:128, 0:64], A["ones_bf"][0:1, :],
                        A["qsrow"][0:1, 192:256], start=True, stop=True)
    em.step("tensor", qsb_mm, waits=[(wd, wd0), (vcs, em.led[vcs]),
                                     (scs, em.led[scs])], inc=tcs)
    def qsb_ev(e):
        e.tensor_copy(A["qsb"][:, 0, :], PS[0][0:128, 0:64])
        e.tensor_copy(A["qsb"][:, 1, :], PS[1][0:128, 0:64])
        e.tensor_copy(A["ksb"][:, 0, :], PS[2][0:128, 0:64])
        return e.tensor_copy(A["ksb"][:, 1, :], PS[3][0:128, 0:64])
    em.step("vector", qsb_ev, waits=[(tcs, em.led[tcs])], inc=vcs)

    src_rows = src_hbm.rearrange("t p d -> (t p) d")
    for T in range(kvlo, NT_OUT):
        em.dma("gpsimd", A["h1_bf"][:, 0:512],
               src_rows[T * 128:(T + 1) * 128, 0:512],
               waits=[(tcs, em.led[tcs]), (vcs, em.led[vcs])], inc=wd)
        em.dma("gpsimd", A["h1_bf"][:, 512:1024],
               src_rows[max(T * 128 - 1, 0):(T + 1) * 128 - 1 + (1 if T == 0 else 0),
                        512:1024],
               waits=[], inc=wd)
        em.dma("gpsimd", A["ropet"][:, 0, :], g["rope_cos"][T],
               waits=[(vcs, em.led[vcs])], inc=wd)
        em.dma("gpsimd", A["ropet"][:, 1, :], g["rope_sin"][T], waits=[], inc=wd)
        em.dma("gpsimd", A["ropet"][:, 2, :], g["rope_cosi"][T], waits=[], inc=wd)
        em.dma("gpsimd", A["ropet"][:, 3, :], g["rope_sini"][T], waits=[], inc=wd)
        ld_t = em.led[wd]; wd_t = em.led[wd]
        for cc in range(8):
            bank = PS[4 + (cc % 2)]
            em.step("tensor", lambda e, cc=cc, bank=bank: e.matmul(
                bank[0:128, 0:128], A["h1_bf"][:, cc * 128:(cc + 1) * 128],
                A["identbf"][:], start=True, stop=True),
                waits=[(wd, wd_t), (vcs, em.led[vcs]), (scs, em.led[scs])], inc=tcs)
            em.step("vector", lambda e, cc=cc, bank=bank: e.tensor_copy(
                A["h1T"][:, cc, :], bank[0:128, 0:128]),
                waits=[(tcs, em.led[tcs])], inc=vcs)
        def qkvmm(e):
            last = None
            for cc in range(8):
                last = e.matmul(PS[0][0:128, 0:512], A["h1T"][:, cc, :],
                                A["wqkv"][:, cc, 0:512], start=(cc == 0),
                                stop=(cc == 7))
                last = e.matmul(PS[1][0:128, 0:512], A["h1T"][:, cc, :],
                                A["wqkv"][:, cc, 512:1024], start=(cc == 0),
                                stop=(cc == 7))
                last = e.matmul(PS[2][0:128, 0:512], A["h1T"][:, cc, :],
                                A["wqkv"][:, cc, 1024:1536], start=(cc == 0),
                                stop=(cc == 7))
            return last
        em.step("tensor", qkvmm, waits=[(vcs, em.led[vcs]), (scs, em.led[scs])],
                inc=tcs)
        em.step("vector", lambda e, T=T: e.tensor_copy(
            A["v_tm"][:, T - kvlo, :], PS[2][0:128, 0:512]),
            waits=[(tcs, em.led[tcs])], inc=vcs)
        em.step("vector", lambda e: e.tensor_copy(A["qk_32"][:, 0, :],
                                                  PS[0][0:128, 0:512]),
                waits=[], inc=vcs)
        em.step("vector", lambda e: e.tensor_copy(A["qk_32"][:, 1, :],
                                                  PS[1][0:128, 0:512]),
                waits=[], inc=vcs)
        def ssq(e):
            e.tensor_tensor(A["sq_l1"][:, 0, :], A["qk_32"][:, 0, :],
                            A["qk_32"][:, 0, :], op=ALU.mult)
            return e.tensor_tensor(A["sq_l1"][:, 1, :], A["qk_32"][:, 1, :],
                                   A["qk_32"][:, 1, :], op=ALU.mult)
        em.step("vector", ssq, waits=[(vcs, em.led[vcs])], inc=vcs)
        em.step("vector", lambda e: e.reduce_sum(
            A["ss8"][:], A["sq_l1"][:].rearrange("p a (h c) -> p (a h) c", c=64),
            axis=AX.X),
            waits=[(vcs, em.led[vcs])], inc=vcs)
        em.step("scalar", lambda e: e.activation(A["nr8"][:], A["ss8"][:], AF.Sqrt),
                waits=[(vcs, em.led[vcs])], inc=scs)
        em.step("vector", lambda e: e.tensor_scalar(
            A["nr8"][:], A["nr8"][:], 1e-12, None, op0=ALU.max),
            waits=[(scs, em.led[scs])], inc=vcs)
        em.step("scalar", lambda e: None, waits=[(vcs, em.led[vcs])], inc=scs)
        em.step("vector", lambda e: e.reciprocal(A["r8"][:], A["nr8"][:]),
                waits=[(scs, em.led[scs])], inc=vcs)
        em.step("scalar", lambda e: None, waits=[(vcs, em.led[vcs])], inc=scs)
        def qhat(e):
            e.tensor_tensor(A["qk_32"][:, 0, :].rearrange("p (h c) -> p h c", c=64),
                            A["qk_32"][:, 0, :].rearrange("p (h c) -> p h c", c=64),
                            A["r8"][:, 0:8].rearrange("p (h c) -> p h c", c=1)
                            .broadcast_to([128, 8, 64]), op=ALU.mult)
            return e.tensor_tensor(
                A["qk_32"][:, 1, :].rearrange("p (h c) -> p h c", c=64),
                A["qk_32"][:, 1, :].rearrange("p (h c) -> p h c", c=64),
                A["r8"][:, 8:16].rearrange("p (h c) -> p h c", c=1)
                .broadcast_to([128, 8, 64]), op=ALU.mult)
        em.step("vector", qhat, waits=[(scs, em.led[scs]), (vcs, em.led[vcs])], inc=vcs)
        def qsmul(e):
            e.tensor_tensor(A["qk_32"][:, 0, :].rearrange("p (h c) -> p h c", c=64),
                            A["qk_32"][:, 0, :].rearrange("p (h c) -> p h c", c=64),
                            A["qsb"][:, 0:1, :].broadcast_to([128, 8, 64]), op=ALU.mult)
            return e.tensor_tensor(
                A["qk_32"][:, 1, :].rearrange("p (h c) -> p h c", c=64),
                A["qk_32"][:, 1, :].rearrange("p (h c) -> p h c", c=64),
                A["ksb"][:, 0:1, :].broadcast_to([128, 8, 64]), op=ALU.mult)
        em.step("vector", qsmul, waits=[(vcs, em.led[vcs])], inc=vcs)
        for srcq, ct, st_, dst in ((0, 0, 1, 0), (1, 2, 3, 1)):
            def rope1(e, srcq=srcq, ct=ct, st_=st_, dst=dst, ld_t=ld_t):
                qv = A["qk_32"][:, srcq, :].rearrange("p (h c) -> p h c", c=64)
                e.tensor_tensor(A["ra"][:].rearrange("p (h c) -> p h c", c=64),
                                qv, A["ropet"][:, ct:ct + 1, :]
                                .broadcast_to([128, 8, 64]), op=ALU.mult)
                e.tensor_tensor(
                    A["rb"][:].rearrange("p (h c) -> p h c", c=64)[:, :, 0:32],
                    qv[:, :, 32:64],
                    A["ropet"][:, st_:st_ + 1, 0:32].broadcast_to([128, 8, 32]),
                    op=ALU.mult)
                return e.tensor_tensor(
                    A["rb"][:].rearrange("p (h c) -> p h c", c=64)[:, :, 32:64],
                    qv[:, :, 0:32],
                    A["ropet"][:, st_:st_ + 1, 32:64].broadcast_to([128, 8, 32]),
                    op=ALU.mult)
            em.step("vector", rope1, waits=[(wd, ld_t), (vcs, em.led[vcs])], inc=vcs)
            em.step("vector", lambda e, dst=dst: e.tensor_tensor(
                A["qf"][:, dst, :], A["ra"][:], A["rb"][:], op=ALU.add),
                waits=[(vcs, em.led[vcs])], inc=vcs)
        for cc in range(4):
            bank = PS[4 + (cc % 2)]
            em.step("tensor", lambda e, cc=cc, bank=bank: e.matmul(
                bank[0:128, 0:128], A["qf"][:, 1, cc * 128:(cc + 1) * 128],
                A["identbf"][:], start=True, stop=True),
                waits=[(vcs, em.led[vcs]), (scs, em.led[scs])], inc=tcs)
            em.step("vector", lambda e, cc=cc, bank=bank, T=T: e.tensor_copy(
                A["kT"][:, cc, (T - kvlo) * 128:(T - kvlo + 1) * 128],
                bank[0:128, 0:128]),
                waits=[(tcs, em.led[tcs])], inc=vcs)
        if T >= qlo:
            for cc in range(4):
                bank = PS[4 + (cc % 2)]
                em.step("tensor", lambda e, cc=cc, bank=bank: e.matmul(
                    bank[0:128, 0:128], A["qf"][:, 0, cc * 128:(cc + 1) * 128],
                    A["identbf"][:], start=True, stop=True),
                    waits=[(vcs, em.led[vcs]), (scs, em.led[scs])], inc=tcs)
                em.step("vector", lambda e, cc=cc, bank=bank, T=T: e.tensor_copy(
                    A["qT"][:, cc, (T - qlo) * 128:(T - qlo + 1) * 128],
                    bank[0:128, 0:128]),
                    waits=[(tcs, em.led[tcs])], inc=vcs)

    sbuf("mskf", (128, 1024), F32)
    sbuf("sc_l1", (128, 2, 1024), F32)
    sbuf("negmax1", (128, 2), F32)
    sbuf("den1", (128, 2), F32)
    sbuf("rec1", (128, 2), F32)
    sbuf("diag1", (128, 2, 128), BF16)
    sbuf("p_l1", (128, 2, 1024), BF16)
    sbuf("pT_l1", (128, 2, 8, 128), BF16)
    sbuf("o_fm", (64, 8, 128), BF16)
    sbuf("wo_sb", (64, 8, D), BF16)
    sbuf("xr_t", (128, D), F32)
    sbuf("x2_t", (128, D), F32)
    em.dma("gpsimd", A["wo_sb"][:],
           g["L1_ffn_wo"][l].rearrange("(h p) f -> p h f", p=64), inc=wd)
    a0_mod = (2048 - HALO) % 512
    for T in range(qlo, NT_OUT):
        em.dma("sync", A["mskf"][:], g[mask_name][T - qlo],
               waits=[(vcs, em.led[vcs])], inc=ld)
        ld_t = em.led[ld]
        absmod = (a0_mod + 128 * T) % 512
        kst = max(T - (absmod // 128) - 4, kvlo)
        kb = kst - kvlo
        assert 0 <= kb <= (NT_OUT - kvlo) - 8, (T, kst, kvlo)
        for hp in range(4):
            def smm(e, hp=hp, T=T, kb=kb):
                last = None
                for i2 in range(2):
                    h = hp * 2 + i2
                    po = (h % 2) * 64
                    sl = h // 2
                    qs_ = A["qT"][po:po + 64, sl,
                                  (T - qlo) * 128:(T - qlo + 1) * 128]
                    last = e.matmul(PS[2 * i2][0:128, 0:512], qs_,
                                    A["kT"][po:po + 64, sl, kb * 128:(kb + 4) * 128],
                                    start=True, stop=True)
                    last = e.matmul(PS[2 * i2 + 1][0:128, 0:512], qs_,
                                    A["kT"][po:po + 64, sl,
                                            (kb + 4) * 128:(kb + 8) * 128],
                                    start=True, stop=True)
                return last
            em.step("tensor", smm, waits=[(vcs, em.led[vcs]), (scs, em.led[scs])],
                    inc=tcs)
            def smask(e):
                last = None
                for i2 in range(2):
                    last = e.scalar_tensor_tensor(
                        A["sc_l1"][:, i2, 0:512], PS[2 * i2][0:128, 0:512],
                        8.0, A["mskf"][:, 0:512], op0=ALU.mult, op1=ALU.add)
                    last = e.scalar_tensor_tensor(
                        A["sc_l1"][:, i2, 512:1024], PS[2 * i2 + 1][0:128, 0:512],
                        8.0, A["mskf"][:, 512:1024], op0=ALU.mult, op1=ALU.add)
                return last
            em.step("vector", smask, waits=[(tcs, em.led[tcs]), (ld, ld_t)], inc=vcs)
            em.step("vector", lambda e: e.reduce_max(
                A["negmax1"][:], A["sc_l1"][:], axis=AX.X, negate=True),
                waits=[(vcs, em.led[vcs])], inc=vcs)
            def sexp(e):
                last = None
                for i2 in range(2):
                    last = e.activation(A["p_l1"][:, i2, :], A["sc_l1"][:, i2, :],
                                        AF.Exp, bias=A["negmax1"][:, i2:i2 + 1],
                                        accum_out=A["den1"][:, i2:i2 + 1])
                return last
            em.step("scalar", sexp, waits=[(vcs, em.led[vcs])], inc=scs)
            em.step("vector", lambda e: e.reciprocal(A["rec1"][:], A["den1"][:]),
                    waits=[(scs, em.led[scs])], inc=vcs)
            def sdiag(e):
                last = None
                for i2 in range(2):
                    last = e.tensor_scalar(A["diag1"][:, i2, :], A["iota_row"][:],
                                           A["pidx"][:, 0:1], A["rec1"][:, i2:i2 + 1],
                                           op0=ALU.is_equal, op1=ALU.mult)
                return last
            em.step("vector", sdiag, waits=[(vcs, em.led[vcs])], inc=vcs)
            for half in range(2):
                def ptmm(e, half=half):
                    last = None
                    for i2 in range(2):
                        for kc in range(2):
                            last = e.matmul(
                                PS[i2 * 2 + kc][0:128, 0:128],
                                A["p_l1"][:, i2, half * 512 + kc * 128 + 256 * 0:
                                          half * 512 + (kc + 1) * 128],
                                A["diag1"][:, i2, :], start=True, stop=True)
                    return last
                em.step("tensor", ptmm, waits=[(vcs, em.led[vcs]),
                                               (scs, em.led[scs])], inc=tcs)
                def ptev(e, half=half):
                    last = None
                    for i2 in range(2):
                        for kc in range(2):
                            last = e.tensor_copy(
                                A["pT_l1"][:, i2, half * 4 + kc, :],
                                PS[i2 * 2 + kc][0:128, 0:128])
                    return last
                em.step("vector", ptev, waits=[(tcs, em.led[tcs])], inc=vcs)
                def ptmm2(e, half=half):
                    last = None
                    for i2 in range(2):
                        for kc in range(2):
                            last = e.matmul(
                                PS[i2 * 2 + kc][0:128, 0:128],
                                A["p_l1"][:, i2, half * 512 + (kc + 2) * 128:
                                          half * 512 + (kc + 3) * 128],
                                A["diag1"][:, i2, :], start=True, stop=True)
                    return last
                em.step("tensor", ptmm2, waits=[(vcs, em.led[vcs])], inc=tcs)
                def ptev2(e, half=half):
                    last = None
                    for i2 in range(2):
                        for kc in range(2):
                            last = e.tensor_copy(
                                A["pT_l1"][:, i2, half * 4 + kc + 2, :],
                                PS[i2 * 2 + kc][0:128, 0:128])
                    return last
                em.step("vector", ptev2, waits=[(tcs, em.led[tcs])], inc=vcs)
            def avmm(e, hp=hp, kb=kb):
                last = None
                for i2 in range(2):
                    h = hp * 2 + i2
                    for kc in range(8):
                        last = e.matmul(PS[4 + i2][0:64, 0:128],
                                        A["v_tm"][:, kb + kc, h * 64:(h + 1) * 64],
                                        A["pT_l1"][:, i2, kc, :],
                                        start=(kc == 0), stop=(kc == 7))
                return last
            em.step("tensor", avmm, waits=[(vcs, em.led[vcs])], inc=tcs)
            def oev(e, hp=hp):
                last = None
                for i2 in range(2):
                    last = e.tensor_copy(A["o_fm"][:, hp * 2 + i2, :],
                                         PS[4 + i2][0:64, 0:128])
                return last
            em.step("vector", oev, waits=[(tcs, em.led[tcs])], inc=vcs)
        em.dma("sync", A["xr_t"][:], src_hbm[T],
               waits=[(vcs, em.led[vcs]), (scs, em.led[scs])], inc=ld)
        ld_x = em.led[ld]
        def owmm(e):
            last = None
            for h in range(8):
                last = e.matmul(PS[1][0:128, 0:512], A["o_fm"][:, h, :],
                                A["wo_sb"][:, h, 0:512], start=(h == 0),
                                stop=(h == 7))
                last = e.matmul(PS[2][0:128, 0:512], A["o_fm"][:, h, :],
                                A["wo_sb"][:, h, 512:1024], start=(h == 0),
                                stop=(h == 7))
            return last
        em.step("tensor", owmm, waits=[(vcs, em.led[vcs]), (scs, em.led[scs])],
                inc=tcs)
        def x2ev(e):
            e.tensor_tensor(A["x2_t"][:, 0:512], PS[1][0:128, 0:512],
                            A["xr_t"][:, 0:512], op=ALU.add)
            return e.tensor_tensor(A["x2_t"][:, 512:1024], PS[2][0:128, 0:512],
                                   A["xr_t"][:, 512:1024], op=ALU.add)
        em.step("vector", x2ev, waits=[(tcs, em.led[tcs]), (ld, ld_x),
                                       (sts, em.led[sts])], inc=vcs)
        em.dma("sync", g["x2fA_hbm" if l == 0 else "x2fB_hbm"][T], A["x2_t"][:],
               waits=[(vcs, em.led[vcs])], inc=sts)
    em.barrier()
    pa.close()

    pa = PhaseAlloc(nc, A)
    sbuf = pa.sbuf
    x2h = g["x2fA_hbm" if l == 0 else "x2fB_hbm"]
    x2rows = x2h.rearrange("t p d -> (t p) d")
    sbuf("w1", (128, 8, 5460), BF16)
    sbuf("w2", (128, 22, D), BF16)
    sbuf("b1row", (1, 5460), BF16)
    sbuf("b2row", (1, D), BF16)
    sbuf("gbc", (128, D), F32)
    sbuf("bbc", (128, D), F32)
    sbuf("h2_32", (128, D), F32)
    sbuf("h2n", (128, D), BF16)
    sbuf("h2nT", (128, 8, 128), BF16)
    sbuf("sq_scratch", (128, D), F32)
    alloc_stats(pa, "sf")
    sbuf("ag", (128, 5504), BF16)
    sbuf("i2T", (128, 22, 128), BF16)
    sbuf("xr2", (128, D), F32)
    sbuf("y2", (128, D), F32)
    sbuf("g1t", (128, D), BF16)
    em.dma("gpsimd", A["w1"][:],
           g["L1_ffn_w1"][l].rearrange("(c p) f -> p c f", p=128), inc=wd)
    em.dma("gpsimd", A["w2"][:],
           g["L1_ffn_w2_pad"][l].rearrange("(c p) f -> p c f", p=128), inc=wd)
    em.dma("gpsimd", A["b1row"][:], g["L1_ffn_b1"][l].rearrange("(o d) -> o d", o=1),
           inc=wd)
    em.dma("gpsimd", A["b2row"][:], g["L1_ffn_b2"][l].rearrange("(o d) -> o d", o=1),
           inc=wd)
    em.dma("sync", A["gbc"][:], g["l1f_gbc"][l], inc=ld)
    em.dma("sync", A["bbc"][:], g["l1f_bbc"][l], inc=ld)
    em.step("gpsimd", lambda e: e.memset(A["i2T"][:, 21, :], 0.0), inc=wd)
    wd1 = em.led[wd]; ld1 = em.led[ld]
    for T in range(qlo, NT_OUT):
        em.dma("sync", A["h2_32"][:, 0:512], x2rows[T * 128:(T + 1) * 128, 0:512],
               waits=[(vcs, em.led[vcs]), (scs, em.led[scs])], inc=ld)
        em.dma("sync", A["h2_32"][:, 512:1024],
               x2rows[T * 128 - 1:(T + 1) * 128 - 1, 512:1024],
               waits=[], inc=ld)
        emit_stats(em, A, A["h2_32"][:], "sf", "eps_1em5", "sq_scratch",
                   extra_waits=[(ld, em.led[ld]), (vcs, em.led[vcs])])
        em.step("scalar", lambda e: e.activation(
            A["sq_scratch"][:], A["h2_32"][:], AF.Identity,
            bias=A["sf_negmr"][:, 0:1], scale=A["sf_r"][:, 0:1]),
            waits=[(vcs, em.led[vcs])], inc=scs)
        em.step("vector", lambda e: e.tensor_tensor(
            A["sq_scratch"][:], A["sq_scratch"][:], A["gbc"][:], op=ALU.mult),
            waits=[(scs, em.led[scs]), (ld, ld1), (vcs, em.led[vcs])], inc=vcs)
        em.step("vector", lambda e: e.tensor_tensor(
            A["h2n"][:], A["sq_scratch"][:], A["bbc"][:], op=ALU.add),
            waits=[(vcs, em.led[vcs])], inc=vcs)
        for cc in range(8):
            bank = PS[4 + (cc % 2)]
            em.step("tensor", lambda e, cc=cc, bank=bank: e.matmul(
                bank[0:128, 0:128], A["h2n"][:, cc * 128:(cc + 1) * 128],
                A["identbf"][:], start=True, stop=True),
                waits=[(vcs, em.led[vcs]), (scs, em.led[scs]), (wd, wd1)], inc=tcs)
            em.step("vector", lambda e, cc=cc, bank=bank: e.tensor_copy(
                A["h2nT"][:, cc, :], bank[0:128, 0:128]),
                waits=[(tcs, em.led[tcs])], inc=vcs)
        for jg in range(11):
            j0 = jg * 512
            jn = min(512, 5460 - j0)
            def ffmm(e, j0=j0, jn=jn):
                last = e.matmul(PS[0][0:128, 0:jn], A["ones_bf"][0:1, :],
                                A["b1row"][0:1, j0:j0 + jn], start=True, stop=False)
                for cc in range(8):
                    last = e.matmul(PS[0][0:128, 0:jn], A["h2nT"][:, cc, :],
                                    A["w1"][:, cc, j0:j0 + jn],
                                    start=False, stop=(cc == 7))
                return last
            em.step("tensor", ffmm, waits=[(vcs, em.led[vcs]), (scs, em.led[scs])],
                    inc=tcs)
            def ffev(e, j0=j0, jn=jn):
                last = None
                if j0 < 2730:
                    an = min(jn, 2730 - j0)
                    last = e.copy(A["ag"][:, j0:j0 + an], PS[0][0:128, 0:an])
                    if an < jn:
                        last = e.activation(A["ag"][:, 2752:2752 + (jn - an)],
                                            PS[0][0:128, an:jn], AF.Gelu)
                else:
                    go = j0 - 2730
                    last = e.activation(A["ag"][:, 2752 + go:2752 + go + jn],
                                        PS[0][0:128, 0:jn], AF.Gelu)
                return last
            em.step("scalar", ffev, waits=[(tcs, em.led[tcs])], inc=scs)
        em.step("vector", lambda e: e.tensor_tensor(
            A["ag"][:, 0:2730], A["ag"][:, 0:2730], A["ag"][:, 2752:5482],
            op=ALU.mult),
            waits=[(scs, em.led[scs]), (vcs, em.led[vcs])], inc=vcs)
        for cc in range(22):
            w = 128 if cc < 21 else 42
            bank = PS[4 + (cc % 2)]
            em.step("tensor", lambda e, cc=cc, w=w, bank=bank: e.matmul(
                bank[0:w, 0:128], A["ag"][:, cc * 128:cc * 128 + w],
                A["identbf"][:], start=True, stop=True),
                waits=[(vcs, em.led[vcs]), (scs, em.led[scs])], inc=tcs)
            em.step("vector", lambda e, cc=cc, w=w, bank=bank: e.tensor_copy(
                A["i2T"][0:w, cc, :], bank[0:w, 0:128]),
                waits=[(tcs, em.led[tcs])], inc=vcs)
        em.dma("sync", A["xr2"][:], x2h[T],
               waits=[(vcs, em.led[vcs]), (scs, em.led[scs])], inc=ld)
        ld_x = em.led[ld]
        def w2mm(e):
            last = e.matmul(PS[1][0:128, 0:512], A["ones_bf"][0:1, :],
                            A["b2row"][0:1, 0:512], start=True, stop=False)
            last = e.matmul(PS[2][0:128, 0:512], A["ones_bf"][0:1, :],
                            A["b2row"][0:1, 512:1024], start=True, stop=False)
            for cc in range(22):
                last = e.matmul(PS[1][0:128, 0:512], A["i2T"][:, cc, :],
                                A["w2"][:, cc, 0:512], start=False, stop=(cc == 21))
                last = e.matmul(PS[2][0:128, 0:512], A["i2T"][:, cc, :],
                                A["w2"][:, cc, 512:1024], start=False, stop=(cc == 21))
            return last
        em.step("tensor", w2mm, waits=[(vcs, em.led[vcs]), (scs, em.led[scs])],
                inc=tcs)
        def yev(e):
            e.tensor_tensor(A["y2"][:, 0:512], PS[1][0:128, 0:512],
                            A["xr2"][:, 0:512], op=ALU.add)
            return e.tensor_tensor(A["y2"][:, 512:1024], PS[2][0:128, 0:512],
                                   A["xr2"][:, 512:1024], op=ALU.add)
        em.step("vector", yev, waits=[(tcs, em.led[tcs]), (ld, ld_x),
                                      (sts, em.led[sts])], inc=vcs)
        if store_g1:
            em.step("scalar", lambda e: e.activation(A["g1t"][:], A["y2"][:],
                                                     AF.Silu),
                    waits=[(vcs, em.led[vcs]), (sts, em.led[sts])], inc=scs)
            em.dma("sync", g["g1_hbm"][T], A["g1t"][:],
                   waits=[(scs, em.led[scs])], inc=sts)
        else:
            em.dma("sync", dst_hbm[T], A["y2"][:],
                   waits=[(vcs, em.led[vcs])], inc=sts)
    em.barrier()
    pa.close()


def _ph_l1a(nc, em, g, A, PS):
    A = dict(A)
    ld, wd, tcs, vcs, scs, sts = "ld", "wd", "tc", "vc", "sc", "st"
    NKV = NT_OUT - L1A_KV_LO
    NQ = NT_OUT - L1A_Q_LO

    pa = PhaseAlloc(nc, A)
    sbuf = pa.sbuf
    sbuf("xt", (128, D), F32)
    sbuf("q_fm", (128, 8, 16 * 128), BF16)
    sbuf("k_fm", (128, 8, 17 * 128), BF16)
    sbuf("v_tm", (128, 17, D), BF16)
    pa_w = PhaseAlloc(nc, A)
    sbufw = pa_w.sbuf
    sbufw("wqkv", (128, 8, 3072), BF16)
    sbufw("g1bc", (128, D), F32); sbufw("b1bc", (128, D), F32)
    sbufw("qb_col", (128, 8), F32)
    sbufw("kv_rows", (1, 2048), BF16)
    sbufw("sq_scratch", (128, D), F32)
    alloc_stats(pa_w, "sa")
    sbufw("h1f", (128, D), BF16)
    sbufw("h1T", (128, 8, 128), BF16)
    sbufw("ktv_bf", (128, D), BF16)
    sbufw("pad_t", (128, 1), F32)
    em.dma("gpsimd", A["wqkv"][:],
           g["L1_attn_w_qkv"].rearrange("(c p) f -> p c f", p=128), inc=wd)
    em.dma("sync", A["g1bc"][:], g["l1a_g1bc"], inc=ld)
    em.dma("sync", A["b1bc"][:], g["l1a_b1bc"], inc=ld)
    em.dma("sync", A["qb_col"][:], g["L1_attn_b_qkv"][0:1024]
           .rearrange("(s p) -> p s", p=128), inc=ld, slow=True)
    em.dma("gpsimd", A["kv_rows"][0:1, 0:1024],
           g["L1_attn_b_qkv"][1024:2048].rearrange("(o d) -> o d", o=1), inc=wd)
    em.dma("gpsimd", A["kv_rows"][0:1, 1024:2048],
           g["L1_attn_b_qkv"][2048:3072].rearrange("(o d) -> o d", o=1), inc=wd)
    wd0 = em.led[wd]; ld0 = em.led[ld]
    for T in range(L1A_KV_LO, NT_OUT):
        em.dma("sync", A["xt"][:], g["xn1_hbm"][T],
               waits=[(vcs, em.led[vcs]), (scs, em.led[scs])], inc=ld)
        em.dma("sync", A["pad_t"][:], g["padmask"][T + 1],
               waits=[], inc=ld)
        emit_stats(em, A, A["xt"][:], "sa", "eps_1em5", "sq_scratch",
                   extra_waits=[(ld, em.led[ld])])
        em.step("scalar", lambda e: e.activation(
            A["sq_scratch"][:], A["xt"][:], AF.Identity,
            bias=A["sa_negmr"][:, 0:1], scale=A["sa_r"][:, 0:1]),
            waits=[(vcs, em.led[vcs])], inc=scs)
        em.step("vector", lambda e: e.tensor_tensor(
            A["sq_scratch"][:], A["sq_scratch"][:], A["g1bc"][:], op=ALU.mult),
            waits=[(scs, em.led[scs]), (ld, ld0), (vcs, em.led[vcs])], inc=vcs)
        em.step("vector", lambda e: e.tensor_tensor(
            A["h1f"][:], A["sq_scratch"][:], A["b1bc"][:], op=ALU.add),
            waits=[(vcs, em.led[vcs])], inc=vcs)
        for cc in range(8):
            bank = PS[4 + (cc % 2)]
            em.step("tensor", lambda e, cc=cc, bank=bank: e.matmul(
                bank[0:128, 0:128], A["h1f"][:, cc * 128:(cc + 1) * 128],
                A["identbf"][:], start=True, stop=True),
                waits=[(vcs, em.led[vcs]), (scs, em.led[scs]), (wd, wd0)], inc=tcs)
            em.step("vector", lambda e, cc=cc, bank=bank: e.tensor_copy(
                A["h1T"][:, cc, :], bank[0:128, 0:128]),
                waits=[(tcs, em.led[tcs])], inc=vcs)
        for sub in range(8):
            def qmm(e, sub=sub):
                last = None
                for cc in range(8):
                    last = e.matmul(PS[0][0:128, 0:128],
                                    A["wqkv"][:, cc, sub * 128:(sub + 1) * 128],
                                    A["h1T"][:, cc, :], start=(cc == 0), stop=(cc == 7))
                return last
            em.step("tensor", qmm, waits=[(vcs, em.led[vcs]), (scs, em.led[scs])],
                    inc=tcs)
            em.step("scalar", lambda e, sub=sub, T=T: e.activation(
                A["q_fm"][:, sub, (T - L1A_Q_LO) * 128:(T - L1A_Q_LO + 1) * 128]
                if T >= L1A_Q_LO else A["ktv_bf"][:, 0:128],
                PS[0][0:128, 0:128], AF.Identity, bias=A["qb_col"][:, sub:sub + 1]),
                waits=[(tcs, em.led[tcs])], inc=scs)
        def kmm(e):
            last = e.matmul(PS[1][0:128, 0:512], A["ones_bf"][0:1, :],
                            A["kv_rows"][0:1, 0:512], start=True, stop=False)
            last = e.matmul(PS[2][0:128, 0:512], A["ones_bf"][0:1, :],
                            A["kv_rows"][0:1, 512:1024], start=True, stop=False)
            for cc in range(8):
                last = e.matmul(PS[1][0:128, 0:512], A["h1T"][:, cc, :],
                                A["wqkv"][:, cc, 1024:1536], start=False,
                                stop=(cc == 7))
                last = e.matmul(PS[2][0:128, 0:512], A["h1T"][:, cc, :],
                                A["wqkv"][:, cc, 1536:2048], start=False,
                                stop=(cc == 7))
            return last
        em.step("tensor", kmm, waits=[(vcs, em.led[vcs]), (scs, em.led[scs])], inc=tcs)
        def kev(e):
            e.tensor_scalar(A["ktv_bf"][:, 0:512], PS[1][0:128, 0:512],
                            A["pad_t"][:, 0:1], None, op0=ALU.mult)
            return e.tensor_scalar(A["ktv_bf"][:, 512:1024], PS[2][0:128, 0:512],
                                   A["pad_t"][:, 0:1], None, op0=ALU.mult)
        em.step("vector", kev, waits=[(tcs, em.led[tcs])], inc=vcs)
        for cc in range(8):
            bank = PS[4 + (cc % 2)]
            em.step("tensor", lambda e, cc=cc, bank=bank: e.matmul(
                bank[0:128, 0:128], A["ktv_bf"][:, cc * 128:(cc + 1) * 128],
                A["identbf"][:], start=True, stop=True),
                waits=[(vcs, em.led[vcs]), (scs, em.led[scs])], inc=tcs)
            em.step("vector", lambda e, cc=cc, bank=bank, T=T: e.tensor_copy(
                A["k_fm"][:, cc, (T - L1A_KV_LO) * 128:(T - L1A_KV_LO + 1) * 128],
                bank[0:128, 0:128]),
                waits=[(tcs, em.led[tcs])], inc=vcs)
        def vmm(e):
            last = e.matmul(PS[1][0:128, 0:512], A["ones_bf"][0:1, :],
                            A["kv_rows"][0:1, 1024:1536], start=True, stop=False)
            last = e.matmul(PS[2][0:128, 0:512], A["ones_bf"][0:1, :],
                            A["kv_rows"][0:1, 1536:2048], start=True, stop=False)
            for cc in range(8):
                last = e.matmul(PS[1][0:128, 0:512], A["h1T"][:, cc, :],
                                A["wqkv"][:, cc, 2048:2560], start=False,
                                stop=(cc == 7))
                last = e.matmul(PS[2][0:128, 0:512], A["h1T"][:, cc, :],
                                A["wqkv"][:, cc, 2560:3072], start=False,
                                stop=(cc == 7))
            return last
        em.step("tensor", vmm, waits=[(vcs, em.led[vcs]), (scs, em.led[scs])], inc=tcs)
        def vev(e, T=T):
            e.tensor_scalar(A["v_tm"][:, T - L1A_KV_LO, 0:512], PS[1][0:128, 0:512],
                            A["pad_t"][:, 0:1], None, op0=ALU.mult)
            return e.tensor_scalar(A["v_tm"][:, T - L1A_KV_LO, 512:1024],
                                   PS[2][0:128, 0:512], A["pad_t"][:, 0:1], None,
                                   op0=ALU.mult)
        em.step("vector", vev, waits=[(tcs, em.led[tcs])], inc=vcs)

    em.barrier()
    pa_w.close()
    sbuf("mska", (64, 128), F32)
    sbuf("wout", (64, 16, D), BF16)
    sbuf("bout_row", (1, D), BF16)
    sbuf("v_sh", (128, D), BF16)
    sbuf("sca", (64, 4, 128), F32)
    sbuf("negmaxa", (64, 4), F32)
    sbuf("dena", (64, 4), F32)
    sbuf("reca", (64, 4), F32)
    sbuf("diaga", (64, 4, 64), BF16)
    sbuf("pa4", (64, 4, 128), BF16)
    sbuf("pTa", (128, 4, 64), BF16)
    sbuf("ob_fm", (64, 16, 128), BF16)
    sbuf("xm_t", (128, D), F32)
    em.dma("sync", A["mska"][:], g["mask_l1a"], inc=ld)
    em.dma("gpsimd", A["wout"][:],
           g["L1_attn_w_out"].rearrange("(h p) f -> p h f", p=64), inc=wd)
    em.dma("gpsimd", A["bout_row"][:],
           g["L1_attn_b_out"].rearrange("(o d) -> o d", o=1), inc=wd)
    ld_b = em.led[ld]; wd_b = em.led[wd]
    for T in range(L1A_Q_LO, NT_OUT):
        kvi = T - L1A_KV_LO
        em.dma("sync", A["v_sh"][0:64, :], A["v_tm"][64:128, kvi - 1, :],
               waits=[(vcs, em.led[vcs]), (tcs, em.led[tcs])], inc=ld)
        em.dma("sync", A["v_sh"][64:128, :], A["v_tm"][0:64, kvi, :],
               waits=[], inc=ld)
        ld_v = em.led[ld]
        for wv in range(8):
            hpair = (wv * 2, wv * 2 + 1)
            def bsmm(e, hpair=hpair, T=T, kvi=kvi):
                last = None
                for i2, h in enumerate(hpair):
                    po = (h % 2) * 64
                    sl = h // 2
                    for b in range(2):
                        qcol = (T - L1A_Q_LO) * 128 + b * 64
                        kcol = kvi * 128 + b * 64 - 64
                        last = e.matmul(
                            PS[i2 * 2 + b][0:64, 0:128],
                            A["q_fm"][po:po + 64, sl, qcol:qcol + 64],
                            A["k_fm"][po:po + 64, sl, kcol:kcol + 128],
                            start=True, stop=True)
                return last
            em.step("tensor", bsmm, waits=[(vcs, em.led[vcs]), (scs, em.led[scs])],
                    inc=tcs)
            def bmask(e):
                last = None
                for u in range(4):
                    last = e.scalar_tensor_tensor(
                        A["sca"][:, u, :], PS[u][0:64, 0:128], 0.125,
                        A["mska"][:], op0=ALU.mult, op1=ALU.add)
                return last
            em.step("vector", bmask, waits=[(tcs, em.led[tcs]), (ld, ld_b)], inc=vcs)
            em.step("vector", lambda e: e.reduce_max(
                A["negmaxa"][:], A["sca"][:], axis=AX.X, negate=True),
                waits=[(vcs, em.led[vcs])], inc=vcs)
            def bexp(e):
                last = None
                for u in range(4):
                    last = e.activation(A["pa4"][:, u, :], A["sca"][:, u, :],
                                        AF.Exp, bias=A["negmaxa"][:, u:u + 1],
                                        accum_out=A["dena"][:, u:u + 1])
                return last
            em.step("scalar", bexp, waits=[(vcs, em.led[vcs])], inc=scs)
            em.step("vector", lambda e: e.reciprocal(A["reca"][:], A["dena"][:]),
                    waits=[(scs, em.led[scs])], inc=vcs)
            def bdiag(e):
                last = None
                for u in range(4):
                    last = e.tensor_scalar(A["diaga"][:, u, :],
                                           A["iota_row"][0:64, 0:64],
                                           A["pidx"][0:64, 0:1],
                                           A["reca"][:, u:u + 1],
                                           op0=ALU.is_equal, op1=ALU.mult)
                return last
            em.step("vector", bdiag, waits=[(vcs, em.led[vcs])], inc=vcs)
            def bptmm(e):
                last = None
                for u in range(4):
                    last = e.matmul(PS[u][0:128, 0:64], A["pa4"][:, u, :],
                                    A["diaga"][:, u, :], start=True, stop=True)
                return last
            em.step("tensor", bptmm, waits=[(vcs, em.led[vcs]),
                                            (scs, em.led[scs])], inc=tcs)
            def bptev(e):
                last = None
                for u in range(4):
                    last = e.tensor_copy(A["pTa"][:, u, :], PS[u][0:128, 0:64])
                return last
            em.step("vector", bptev, waits=[(tcs, em.led[tcs])], inc=vcs)
            def bavmm(e, hpair=hpair, kvi=kvi):
                last = None
                for i2, h in enumerate(hpair):
                    for b in range(2):
                        vsl = (A["v_sh"][:, h * 64:(h + 1) * 64] if b == 0
                               else A["v_tm"][:, kvi, h * 64:(h + 1) * 64])
                        last = e.matmul(PS[i2 * 2 + b][0:64, 0:64], vsl,
                                        A["pTa"][:, i2 * 2 + b, :],
                                        start=True, stop=True)
                return last
            em.step("tensor", bavmm, waits=[(vcs, em.led[vcs]), (ld, ld_v)], inc=tcs)
            def bobev(e, hpair=hpair):
                last = None
                for i2, h in enumerate(hpair):
                    for b in range(2):
                        last = e.tensor_copy(A["ob_fm"][:, h, b * 64:(b + 1) * 64],
                                             PS[i2 * 2 + b][0:64, 0:64])
                return last
            em.step("vector", bobev, waits=[(tcs, em.led[tcs])], inc=vcs)
        em.dma("sync", A["xt"][:], g["xn1_hbm"][T],
               waits=[(vcs, em.led[vcs]), (scs, em.led[scs])], inc=ld)
        ld_x = em.led[ld]
        def owmm(e):
            last = e.matmul(PS[1][0:128, 0:512], A["ones_bf"][0:1, :],
                            A["bout_row"][0:1, 0:512], start=True, stop=False)
            last = e.matmul(PS[2][0:128, 0:512], A["ones_bf"][0:1, :],
                            A["bout_row"][0:1, 512:1024], start=True, stop=False)
            for h in range(16):
                last = e.matmul(PS[1][0:128, 0:512], A["ob_fm"][:, h, :],
                                A["wout"][:, h, 0:512], start=False, stop=(h == 15))
                last = e.matmul(PS[2][0:128, 0:512], A["ob_fm"][:, h, :],
                                A["wout"][:, h, 512:1024], start=False, stop=(h == 15))
            return last
        em.step("tensor", owmm, waits=[(vcs, em.led[vcs]), (scs, em.led[scs]),
                                       (wd, wd_b)], inc=tcs)
        def xmev(e):
            e.tensor_tensor(A["xm_t"][:, 0:512], PS[1][0:128, 0:512],
                            A["xt"][:, 0:512], op=ALU.add)
            return e.tensor_tensor(A["xm_t"][:, 512:1024], PS[2][0:128, 0:512],
                                   A["xt"][:, 512:1024], op=ALU.add)
        em.step("vector", xmev, waits=[(tcs, em.led[tcs]), (ld, ld_x),
                                       (sts, em.led[sts])], inc=vcs)
        em.dma("sync", g["x2a_hbm"][T], A["xm_t"][:],
               waits=[(vcs, em.led[vcs])], inc=sts)
    em.barrier()
    pa.close()

    pa = PhaseAlloc(nc, A)
    sbuf = pa.sbuf
    sbuf("w1", (128, 8, 4096), BF16)
    sbuf("w2", (128, 32, D), BF16)
    sbuf("b1col", (128, 32), F32)
    sbuf("b2row", (1, D), BF16)
    sbuf("g2bc", (128, D), F32); sbuf("b2bc", (128, D), F32)
    sbuf("xm4", (128, 2, D), F32)
    sbuf("sq_scratch_c", (128, D), F32)
    alloc_stats(pa, "sm")
    sbuf("h2nT_g", (128, 8, 256), BF16)
    sbuf("m1_fm", (128, 32, 256), BF16)
    sbuf("s1_t", (128, D), F32)
    sbuf("x1r", (128, D), F32)
    sbuf("g1r", (128, D), BF16)
    sbuf("h2nb", (128, D), BF16)
    A["out_t"] = A["sq_scratch_c"]
    em.dma("gpsimd", A["w1"][:],
           g["L1_attn_w1"].rearrange("(c p) f -> p c f", p=128), inc=wd)
    em.dma("gpsimd", A["w2"][:],
           g["L1_attn_w2"].rearrange("(c p) f -> p c f", p=128), inc=wd)
    em.dma("sync", A["b1col"][:], g["L1_attn_b1"].rearrange("(s p) -> p s", p=128),
           inc=ld, slow=True)
    em.dma("gpsimd", A["b2row"][:], g["L1_attn_b2"].rearrange("(o d) -> o d", o=1),
           inc=wd)
    em.dma("sync", A["g2bc"][:], g["l1a_g2bc"], inc=ld)
    em.dma("sync", A["b2bc"][:], g["l1a_b2bc"], inc=ld)
    wd_c = em.led[wd]; ld_c = em.led[ld]
    for gq in range(8):
        T0 = L1A_Q_LO + gq * 2
        for ti in range(2):
            T = T0 + ti
            xm = A["xm4"][:, ti, :]
            em.dma("sync", xm, g["x2a_hbm"][T],
                   waits=[(vcs, em.led[vcs]), (scs, em.led[scs])], inc=ld)
            emit_stats(em, A, xm, "sm", "eps_1em5", "sq_scratch_c",
                       extra_waits=[(ld, em.led[ld]), (vcs, em.led[vcs]),
                                    (sts, em.led[sts])])
            em.step("scalar", lambda e, xm=xm: e.activation(
                A["sq_scratch_c"][:], xm, AF.Identity,
                bias=A["sm_negmr"][:, 0:1], scale=A["sm_r"][:, 0:1]),
                waits=[(vcs, em.led[vcs])], inc=scs)
            em.step("vector", lambda e: e.tensor_tensor(
                A["sq_scratch_c"][:], A["sq_scratch_c"][:], A["g2bc"][:], op=ALU.mult),
                waits=[(scs, em.led[scs]), (ld, ld_c), (vcs, em.led[vcs])], inc=vcs)
            em.step("vector", lambda e: e.tensor_tensor(
                A["h2nb"][:], A["sq_scratch_c"][:], A["b2bc"][:], op=ALU.add),
                waits=[(vcs, em.led[vcs])], inc=vcs)
            for cc in range(8):
                bank = PS[4 + (cc % 2)]
                em.step("tensor", lambda e, cc=cc, bank=bank: e.matmul(
                    bank[0:128, 0:128], A["h2nb"][:, cc * 128:(cc + 1) * 128],
                    A["identbf"][:], start=True, stop=True),
                    waits=[(vcs, em.led[vcs]), (scs, em.led[scs]), (wd, wd_c)],
                    inc=tcs)
                em.step("vector", lambda e, cc=cc, bank=bank, ti=ti: e.tensor_copy(
                    A["h2nT_g"][:, cc, ti * 128:(ti + 1) * 128], bank[0:128, 0:128]),
                    waits=[(tcs, em.led[tcs])], inc=vcs)
        for js in range(32):
            def m1mm(e, js=js):
                last = None
                for cc in range(8):
                    last = e.matmul(PS[0][0:128, 0:256],
                                    A["w1"][:, cc, js * 128:(js + 1) * 128],
                                    A["h2nT_g"][:, cc, :], start=(cc == 0),
                                    stop=(cc == 7))
                return last
            em.step("tensor", m1mm, waits=[(vcs, em.led[vcs]), (scs, em.led[scs])],
                    inc=tcs)
            em.step("scalar", lambda e, js=js: e.activation(
                A["m1_fm"][:, js, :], PS[0][0:128, 0:256], AF.Gelu,
                bias=A["b1col"][:, js:js + 1]),
                waits=[(tcs, em.led[tcs]), (ld, ld_c)], inc=scs)
        for ti in range(2):
            T = T0 + ti
            def w2mm(e, ti=ti):
                last = e.matmul(PS[1][0:128, 0:512], A["ones_bf"][0:1, :],
                                A["b2row"][0:1, 0:512], start=True, stop=False)
                last = e.matmul(PS[2][0:128, 0:512], A["ones_bf"][0:1, :],
                                A["b2row"][0:1, 512:1024], start=True, stop=False)
                for cc in range(32):
                    last = e.matmul(PS[1][0:128, 0:512],
                                    A["m1_fm"][:, cc, ti * 128:(ti + 1) * 128],
                                    A["w2"][:, cc, 0:512], start=False,
                                    stop=(cc == 31))
                    last = e.matmul(PS[2][0:128, 0:512],
                                    A["m1_fm"][:, cc, ti * 128:(ti + 1) * 128],
                                    A["w2"][:, cc, 512:1024], start=False,
                                    stop=(cc == 31))
                return last
            em.step("tensor", w2mm, waits=[(vcs, em.led[vcs]), (scs, em.led[scs])],
                    inc=tcs)
            def s1ev(e, ti=ti):
                e.tensor_tensor(A["s1_t"][:, 0:512], PS[1][0:128, 0:512],
                                A["xm4"][:, ti, 0:512], op=ALU.add)
                return e.tensor_tensor(A["s1_t"][:, 512:1024], PS[2][0:128, 0:512],
                                       A["xm4"][:, ti, 512:1024], op=ALU.add)
            em.step("vector", s1ev, waits=[(tcs, em.led[tcs])], inc=vcs)
            em.dma("sync", A["x1r"][:], g["x1_hbm"][T],
                   waits=[(vcs, em.led[vcs])], inc=ld)
            em.dma("sync", A["g1r"][:], g["g1_hbm"][T], waits=[], inc=ld)
            ld_f = em.led[ld]
            em.step("vector", lambda e: e.tensor_tensor(
                A["s1_t"][:], A["s1_t"][:], A["g1r"][:], op=ALU.mult),
                waits=[(vcs, em.led[vcs]), (ld, ld_f)], inc=vcs)
            em.step("vector", lambda e: e.tensor_tensor(
                A["out_t"][:], A["s1_t"][:], A["x1r"][:], op=ALU.add),
                waits=[(vcs, em.led[vcs]), (sts, em.led[sts])], inc=vcs)
            em.dma("sync", g["out_shard"][(T - L1A_Q_LO) * 128:(T - L1A_Q_LO + 1) * 128, :],
                   A["out_t"][:], waits=[(vcs, em.led[vcs])], inc=sts)
    em.barrier()
    pa.close()


_NC_CACHE = {}

def kernel(**inputs):
    if "nc" not in _NC_CACHE:
        _NC_CACHE["nc"] = build_program()
    nc = _NC_CACHE["nc"]
    in_maps = [shard_inputs(inputs, c) for c in range(8)]
    res = run_bass_kernel_spmd(nc, in_maps, core_ids=list(range(8)))
    out = np.zeros((2, 8192, D), np.float32)
    for c in range(8):
        b, s = divmod(c, 4)
        out[b, s * T_SH:(s + 1) * T_SH] = res.results[c]["out_shard"]
    return out
